# version 12
# baseline (speedup 1.0000x reference)
import math

import numpy as np
import ml_dtypes

import concourse.bass as bass
import concourse.mybir as mybir
from concourse.bass_utils import run_bass_kernel_spmd

F32 = mybir.dt.float32
BF16 = mybir.dt.bfloat16
I32 = mybir.dt.int32
ALU = mybir.AluOpType
AF = mybir.ActivationFunctionType
BFNP = ml_dtypes.bfloat16

D = 2048
T = 2048
BATCH = 4
NCORE = 8
HD = 64
NPAIR = 16
VD = 128
MLP = 16384
KC = D // 128
LAMBDA_INIT = 0.8 - 0.6 * math.exp(-0.3 * 12)
LN_EPS = 1e-5
GN_EPS = 1e-5

PI = np.concatenate([np.arange(0, D, 2), np.arange(1, D, 2)])
PI_INV = np.argsort(PI)

FVALS = [7.5, -1.0, 1.0, 0.5, -2.0, 4.0, -8.0, -10.0, 0.125,
         1.0 / 128.0, -1.0 / 128.0, LN_EPS, GN_EPS, 1.0 - LAMBDA_INIT,
         LAMBDA_INIT, 1.0 / D, 0.0, -16.0, 2.0, -0.5, 28.0, 24.0, -1e9,
         (1.0 - LAMBDA_INIT) ** 2]
IVALS = [15]
FIDX = {v: i for i, v in enumerate(FVALS)}
IIDX = {v: i for i, v in enumerate(IVALS)}


def own_qblocks(core):
    if core % 2 == 0:
        return [(0, 512), (1536, 2048)]
    return [(512, 1024), (1024, 1536)]


ENGS = ["sync", "vector", "scalar", "tensor", "gpsimd"]


class Prog:

    def __init__(self, nc, sems, eng_name, eng):
        self.nc = nc
        self.sems = sems
        self.eng_name = eng_name
        self.eng = eng
        self.ecnt = {e: 0 for e in ENGS}
        self.dcnt = {k: 0 for k in sems if not k.startswith("E_")}

    def on(self, name):
        return name == self.eng_name

    def op(self, eng_name, fn, inc=True, drain=None):
        if drain is None:
            drain = eng_name in ("vector", "scalar")
        if drain:
            self.wait_self(eng_name)
        if self.on(eng_name):
            instr = fn(self.eng)
            if inc and instr is not None:
                instr.then_inc(self.sems["E_" + eng_name], 1)
        if inc:
            self.ecnt[eng_name] += 1

    def wait_self(self, eng_name):
        if self.on(eng_name) and self.ecnt[eng_name] > 0:
            self.eng.wait_ge(self.sems["E_" + eng_name], self.ecnt[eng_name])

    def wait_eng(self, waiter, producer, at=None):
        v = self.ecnt[producer] if at is None else at
        if self.on(waiter) and v > 0:
            self.eng.wait_ge(self.sems["E_" + producer], v)

    def dma(self, eng_name, out, in_, stream):
        if self.on(eng_name):
            self.eng.dma_start(out=out, in_=in_).then_inc(self.sems[stream], 16)
        self.dcnt[stream] += 16

    def wait_dma(self, waiter, stream, at=None):
        v = self.dcnt[stream] if at is None else at
        if self.on(waiter) and v > 0:
            self.eng.wait_ge(self.sems[stream], v)

    def mark(self, *streams):
        return {k: self.dcnt[k] for k in streams}


class Ctx:
    pass


def make_consts(nc):
    cf_np = np.tile(np.array(FVALS, np.float32), (128, 1))
    ci_np = np.tile(np.array(IVALS, np.int32), (128, 1))
    cfd = nc.inline_tensor(cf_np, name="cfd").ap()
    cid = nc.inline_tensor(ci_np, name="cid").ap()
    cft = nc.alloc_sbuf_tensor("cft", [128, len(FVALS)], F32)
    cit = nc.alloc_sbuf_tensor("cit", [128, len(IVALS)], I32)
    for v, i in FIDX.items():
        nc.const_aps.register(F32, v, cft[:, i:i + 1])
    return cfd, cid, cft, cit


def CF(ctx, v):
    return ctx.cft[:, FIDX[v]:FIDX[v] + 1]


def CI(ctx, v):
    return ctx.cit[:, IIDX[v]:IIDX[v] + 1]


def CF1(ctx, v):
    return ctx.cft[0:1, FIDX[v]:FIDX[v] + 1]


def emit_dequant(p, ctx):
    F = 512
    mats = [
        (ctx.gbT, ctx.gsT, 8, ("lo", "hi"), 0),
        (ctx.ubT, ctx.usT, 8, ("lo", "hi"), T),
        (ctx.dbT, ctx.dsT, 16, ("dyn",), 2 * T),
    ]
    out_after_tile = []
    k = 0
    for mi, (bT, sT, ntt, nibbles, rbase) in enumerate(mats):
        for tt in range(ntt):
            for h in range(4):
                buf = k % 2
                csl = slice(F * h, F * h + F)
                if k >= 2:
                    p.wait_dma("sync", f"dq_out{buf}", at=out_after_tile[k - 2])
                p.dma("sync", ctx.dq_bytes[buf][:],
                      bT[128 * tt:128 * tt + 128, csl], f"dq_ld{buf}")
                p.dma("sync", ctx.dq_scale[buf][:],
                      sT[128 * tt:128 * tt + 128, csl], f"dq_ld{buf}")
                ld_target = p.dcnt[f"dq_ld{buf}"]
                first = True
                for nib in nibbles:
                    vbuf = ctx.dq_val[buf][0 if nib != "hi" else 1]
                    if first:
                        p.wait_dma("vector", f"dq_ld{buf}", at=ld_target)
                        first = False
                    bts = ctx.dq_bytes[buf][:]
                    if nib == "dyn":
                        p.op("vector", lambda e: e.tensor_single_scalar(
                            ctx.dq_ci[:], bts, CI(ctx, 15), ALU.bitwise_and))
                        p.op("vector", lambda e: e.tensor_copy(
                            ctx.dq_cf2[:], ctx.dq_ci[:]))
                        p.op("vector", lambda e: e.tensor_scalar(
                            ctx.dq_ci[:], bts, 4, CI(ctx, 15),
                            ALU.logical_shift_right, ALU.bitwise_and))
                        p.op("vector", lambda e: e.tensor_copy(
                            ctx.dq_cf[:], ctx.dq_ci[:]))
                        p.op("vector", lambda e: e.tensor_single_scalar(
                            ctx.dq_cf[:], ctx.dq_cf[:],
                            ctx.dqsel[:, 1:2], ALU.mult))
                        p.op("vector", lambda e: e.scalar_tensor_tensor(
                            ctx.dq_cf[:], ctx.dq_cf2[:],
                            ctx.dqsel[:, 0:1], ctx.dq_cf[:],
                            ALU.mult, ALU.add))
                    elif nib == "lo":
                        p.op("vector", lambda e: e.tensor_single_scalar(
                            ctx.dq_ci[:], bts, CI(ctx, 15), ALU.bitwise_and))
                        p.op("vector", lambda e: e.tensor_copy(
                            ctx.dq_cf[:], ctx.dq_ci[:]))
                    else:
                        p.op("vector", lambda e: e.tensor_scalar(
                            ctx.dq_ci[:], bts, 4, CI(ctx, 15),
                            ALU.logical_shift_right, ALU.bitwise_and))
                        p.op("vector", lambda e: e.tensor_copy(
                            ctx.dq_cf[:], ctx.dq_ci[:]))
                    p.wait_eng("scalar", "vector")
                    p.op("scalar", lambda e: e.activation(
                        ctx.dq_sgn[:], ctx.dq_cf[:], AF.Sign,
                        bias=7.5, scale=-1.0))
                    p.wait_eng("vector", "scalar")
                    p.op("vector", lambda e: e.scalar_tensor_tensor(
                        ctx.dq_t[:], ctx.dq_sgn[:], CF(ctx, 4.0),
                        ctx.dq_cf[:], ALU.mult, ALU.add))
                    p.wait_eng("scalar", "vector")
                    p.op("scalar", lambda e: e.activation(
                        ctx.dq_r1[:], ctx.dq_t[:], AF.Relu,
                        bias=-8.0, scale=1.0))
                    p.op("scalar", lambda e: e.activation(
                        ctx.dq_r2[:], ctx.dq_t[:], AF.Relu,
                        bias=-10.0, scale=1.0))
                    p.wait_eng("vector", "scalar")
                    p.op("vector", lambda e: e.scalar_tensor_tensor(
                        ctx.dq_a[:], ctx.dq_r1[:], CF(ctx, 0.5),
                        ctx.dq_r2[:], ALU.mult, ALU.add))
                    p.op("vector", lambda e: e.scalar_tensor_tensor(
                        ctx.dq_a[:], ctx.dq_t[:], CF(ctx, 0.5),
                        ctx.dq_a[:], ALU.mult, ALU.add))
                    p.op("vector", lambda e: e.scalar_tensor_tensor(
                        ctx.dq_a[:], ctx.dq_a[:], CF(ctx, -2.0),
                        ctx.dq_sgn[:], ALU.add, ALU.mult))
                    p.op("vector", lambda e, vbuf=vbuf: e.tensor_tensor(
                        vbuf[:], ctx.dq_a[:],
                        ctx.dq_scale[buf][:], ALU.mult))
                p.wait_eng("scalar", "vector")
                for nib in nibbles:
                    if mi < 2:
                        row0 = rbase + (0 if nib == "lo" else 1024) + 128 * tt
                    else:
                        row0 = rbase + 128 * tt
                    vbuf = ctx.dq_val[buf][0 if nib != "hi" else 1]
                    p.dma("scalar", ctx.cc_in[row0:row0 + 128, csl], vbuf[:],
                          f"dq_out{buf}")
                out_after_tile.append(p.dcnt[f"dq_out{buf}"])
                k += 1


def emit_dequant_wait_done(p, ctx):
    p.wait_dma("gpsimd", "dq_out0")
    p.wait_dma("gpsimd", "dq_out1")


A0 = 5120
A2 = A0 + 32 * 1024
B0 = A2 + 32 * 1024
C0 = B0 + 64 * 1024
D0 = C0 + 32 * 1024
E0 = D0 + 32 * 1024


def make_ctx(nc, mode):
    ctx = Ctx()
    ctx.mode = mode
    at = nc.alloc_sbuf_tensor_at

    def T_(name, off, shape, dt):
        return at(name, shape, dt, offset=off)

    dram = lambda n, sh, dt: nc.dram_tensor(n, sh, dt, kind="ExternalInput").ap()
    if mode in ("dequant", "fused"):
        ctx.gbT = dram("gbT", [1024, 2048], I32)
        ctx.ubT = dram("ubT", [1024, 2048], I32)
        ctx.dbT = dram("dbT", [2048, 2048], I32)
        ctx.gsT = dram("gsT", [1024, 2048], F32)
        ctx.usT = dram("usT", [1024, 2048], F32)
        ctx.dsT = dram("dsT", [2048, 2048], F32)
        ctx.dqsel_d = dram("dqsel", [128, 2], F32)
    if mode in ("attn", "fused", "ffn", "layer2"):
        ctx.xfm = dram("xfm", [2048, 2048], F32)
    if mode == "ffn":
        ctx.ccw_d = dram("ccw", [8, 3, 2048, 2048], BF16)
        ctx.s_in = dram("s_in", [2048, 1024], BF16)
    if mode == "layer2":
        ctx.ccw_d = dram("ccw", [8, 3, 2048, 2048], BF16)
    if mode in ("ffn", "layer2"):
        ctx.ccw = [[ctx.ccw_d[b5, mi] for mi in range(3)] for b5 in range(8)]
    if mode == "fused":
        ctx.cc_in = nc.dram_tensor("cc_in", [3 * T, 2048], BF16,
                                   kind="Internal").ap()
        ctx.cc_out = nc.dram_tensor("cc_out", [8 * 3 * T, 2048], BF16,
                                    kind="Internal", addr_space="Shared").ap()
        ctx.ccw = [[ctx.cc_out[(3 * b5 + mi) * T:(3 * b5 + mi + 1) * T, :]
                    for mi in range(3)] for b5 in range(8)]
    if mode in ("attn", "fused", "layer2"):
        ctx.wq = dram("wq", [2048, 2048], F32)
        ctx.wk = dram("wk", [2048, 2048], F32)
        ctx.wv = dram("wv", [2048, 2048], F32)
        ctx.wo = dram("wo", [2048, 2048], F32)
        ctx.lam_d = dram("lams", [1, 256], F32)
        ctx.osel_d = dram("osel", [128, 2], F32)
        ctx.masks_d = dram("masks", [64, 128, 512], BF16)

    cf_np = np.tile(np.array(FVALS, np.float32), (128, 1))
    ci_np = np.tile(np.array(IVALS, np.int32), (128, 1))
    ctx.cfd = nc.inline_tensor(cf_np, name="cfd").ap()
    ctx.cid = nc.inline_tensor(ci_np, name="cid").ap()
    ctx.ones_d = nc.inline_tensor(np.ones((128, 2), np.float32), name="ones_d").ap()
    ctx.onesb_d = nc.inline_tensor(np.ones((128, 2), ml_dtypes.bfloat16),
                                   name="onesb_d").ap()
    ctx.onesr_d = nc.inline_tensor(np.ones((1, 128), np.float32), name="onesr_d").ap()

    e = E0
    def Es(name, shape, dt):
        nonlocal e
        e = (e + 31) & ~31
        t = T_(name, e, shape, dt)
        e += shape[1] * mybir.dt.size(dt)
        return t
    ctx.cft = Es("cft", [128, len(FVALS)], F32)
    ctx.cit = Es("cit", [128, len(IVALS)], I32)
    for v, i in FIDX.items():
        nc.const_aps.register(F32, v, ctx.cft[:, i:i + 1])
    ctx.ones_f = Es("ones_f", [128, 2], F32)
    ctx.ones_b = Es("ones_b", [128, 2], BF16)
    ctx.ones_r = Es("ones_r", [1, 128], F32)
    if mode in ("dequant", "fused"):
        ctx.dqsel = Es("dqsel_t", [128, 2], F32)
    if mode in ("attn", "fused", "ffn", "layer2"):
        ctx.wqkf = Es("wqkf", [128, 2048], F32)
        ctx.wqb = Es("wqb", [128, 2048], BF16)
        ctx.wkb = Es("wkb", [128, 2048], BF16)
        ctx.P01 = [Es(f"P01_{i}", [128, 1024], BF16) for i in range(2)]
        ctx.maskb = [Es(f"maskb{i}", [128, 512], BF16) for i in range(2)]
        ctx.crow = Es("crow", [1, 16], F32)
        ctx.lrows = Es("lrows", [1, 256], F32)
        ctx.lscr = Es("lscr", [1, 128], F32)
        ctx.lam = Es("lam", [1, 1], F32)
        ctx.mlam = Es("mlam", [1, 1], F32)
        ctx.mccol = Es("mccol", [128, 1], F32)
        ctx.osel = Es("osel_t", [128, 2], F32)
        ctx.wob = [ctx.wqb, ctx.wkb]
    ctx.e_end = e
    assert e <= 229376, f"E region overflow: {e}"

    if mode in ("attn", "fused", "ffn", "layer2"):
        ctx.xn_own = T_("xn_own", A0, [128, 16 * 1024], BF16)
        ctx.xn_rest = T_("xn_rest", A2, [128, 16 * 1024], BF16)
        ctx.murow = T_("murow", B0, [1, 2048], F32)
        ctx.isrow = T_("isrow", B0 + 8192, [1, 2048], F32)
        ctx.rtmp = T_("rtmp", B0 + 16384, [1, 2048], F32)
        ctx.V = T_("V", B0, [128, 16 * 1024], BF16)
        ctx.s_bf = T_("s_bf", B0, [128, 16 * 1024], BF16)
        ctx.gn_eff = T_("gn_eff", B0 + 32 * 1024, [128, 16 * 1024], BF16)
        ctx.xbuf = [T_(f"xbuf{i}", C0 + i * 8192, [128, 2048], F32) for i in range(2)]
        ctx.x2 = T_("x2", C0 + 16384, [128, 2048], BF16)
        ctx.ttmp = T_("ttmp", C0 + 20480, [128, 2048], F32)
        ctx.wvf = [T_(f"wvf{i}", C0 + 28672 + i * 2048, [128, 512], F32)
                   for i in range(2)]
        ctx.wvb = T_("wvb", C0, [128, 16 * 512], BF16)
        ctx.gn_all = T_("gn_all", C0, [128, 16 * 1024], BF16)
        dd = D0
        dd0 = dd
        def Dm(name, shape, dt):
            nonlocal dd
            dd = (dd + 31) & ~31
            t = T_(name, dd, shape, dt)
            dd += shape[1] * mybir.dt.size(dt)
            return t
        ctx.kt_off = (dd + 31) & ~31
        ctx.KT = Dm("KT", [128, 2048], BF16)
        ctx.QT = Dm("QT", [128, 2048], BF16)
        ctx.Q2 = Dm("Q2", [128, 2048], BF16)
        ctx.K2 = Dm("K2", [128, 2048], BF16)
        ctx.t1f = Dm("t1f", [128, 512], F32)
        ctx.opf = Dm("opf", [128, 512], F32)
        ctx.sqb = Dm("sqb", [128, 512], BF16)
        ctx.ou0 = Dm("ou0", [128, 512], F32)
        ctx.ou1 = Dm("ou1", [128, 512], F32)
        ctx.erow = Dm("erow", [1, 1024], F32)
        assert dd <= E0, f"D attn block overflow: {dd}"
        ctx.wof2 = T_("wof2", ctx.kt_off, [128, 2048], F32)
        ctx.wof = [ctx.wqkf, ctx.wof2]
    if mode in ("dequant", "fused"):
        d = D0
        def Ds(name, shape, dt):
            nonlocal d
            d = (d + 31) & ~31
            t = T_(name, d, shape, dt)
            d += shape[1] * mybir.dt.size(dt)
            return t
        ctx.dq_bytes = [Ds(f"dqb{i}", [128, 512], I32) for i in range(2)]
        ctx.dq_scale = [Ds(f"dqs{i}", [128, 512], F32) for i in range(2)]
        ctx.dq_val = [[Ds(f"dqv{i}{j}", [128, 512], BF16) for j in range(2)]
                      for i in range(2)]
        ctx.dq_ci = Ds("dqci", [128, 512], I32)
        ctx.dq_cf = Ds("dqcf", [128, 512], BF16)
        ctx.dq_cf2 = Ds("dqcf2", [128, 512], BF16)
        ctx.dq_sgn = Ds("dqsg", [128, 512], BF16)
        ctx.dq_t = Ds("dqt", [128, 512], BF16)
        ctx.dq_r1 = Ds("dqr1", [128, 512], BF16)
        ctx.dq_r2 = Ds("dqr2", [128, 512], BF16)
        ctx.dq_a = Ds("dqa", [128, 512], BF16)
        assert d <= D0 + 26 * 1024, f"dequant arena overflow {d - D0}"

    if mode in ("ffn", "fused", "layer2"):
        ctx.dacc_a = T_("dacc_a", A2, [128, 8 * 1024], F32)
        ctx.dacc_b = T_("dacc_b", B0 + 32 * 1024, [128, 8 * 1024], F32)
        ctx.inter = T_("inter", C0, [128, 16 * 1024], BF16)
        ctx.wgt = [T_(f"wgt{i}", D0 + 4096 * i, [128, 2048], BF16) for i in range(2)]
        ctx.wut = [T_(f"wut{i}", D0 + 8192 + 4096 * i, [128, 2048], BF16)
                   for i in range(2)]
        ctx.wdt = [T_(f"wdt{i}", D0 + 16384 + 4096 * i, [128, 2048], BF16)
                   for i in range(2)]
        ctx.ffn_gg = T_("ffn_gg", D0 + 24576, [128, 1024], BF16)
        ctx.ffn_ub = T_("ffn_ub", D0 + 26624, [128, 1024], BF16)
        ctx.gbuf = T_("gbuf", D0 + 16384, [128, 1024], BF16)
        ctx.sgtmp = T_("sgtmp", D0 + 24576, [128, 1024], F32)
        ctx.obuf = [T_("obuf0", ctx.wqkf.manual_sbuf_range[0], [128, 1024], F32),
                    T_("obuf1", ctx.wqkf.manual_sbuf_range[0] + 4096,
                       [128, 1024], F32)]
        ctx.xfbuf = [T_("xfbuf0", ctx.wqb.manual_sbuf_range[0], [128, 1024], F32),
                     T_("xfbuf1", ctx.wkb.manual_sbuf_range[0], [128, 1024], F32)]

    ctx.ps = nc.alloc_psum_tensor("ps", [128, 4096], F32)

    reset_sched_state(ctx)

    def xn_sl(kc, t0, w=512):
        if t0 < 1024:
            return ctx.xn_own[:, 1024 * kc + t0:1024 * kc + t0 + w]
        return ctx.xn_rest[:, 1024 * kc + (t0 - 1024):1024 * kc + (t0 - 1024) + w]
    ctx.xn_sl = xn_sl
    return ctx


def reset_sched_state(ctx):
    ctx.xs_pe_done = {}
    ctx.xs_act_done = {}
    ctx.x2_pe_done = 0
    ctx.xa_dve_done = {}
    ctx.bcast_cons = {}
    ctx.wv_cast_done = {}
    ctx.vp_evac_done = {}
    ctx.wqk_cast_done = 0
    ctx.kq_evac_done = 0
    ctx.cb_red_done = {}
    ctx.mask_cons = {}
    ctx.exp_done = {}
    ctx.wo_cast_done = {}
    ctx.op_evac_done = {}
    ctx.op_mm_done = {}
    ctx.av_done = 0
    ctx.dq_vec_done = 0
    ctx.dq_act_done = 0
    ctx.kq_mm_done = 0


def sem_names_for(mode):
    names = ["consts"] + ["E_" + x for x in ENGS]
    if mode in ("dequant", "fused"):
        names += ["dq_ld0", "dq_ld1", "dq_out0", "dq_out1"]
    if mode in ("attn", "fused", "ffn", "layer2"):
        names += ["xs0", "xs1", "xa0", "xa1", "wv0", "wv1", "wqk",
                  "mk0", "mk1", "wo0", "wo1", "sout"]
    if mode in ("ffn", "fused", "layer2"):
        names += ["wg00", "wg01", "wg10", "wg11", "wd0", "wd1",
                  "xf0", "xf1", "xo0", "xo1", "sld"]
    return names


def emit_preamble(p, ctx):
    p.dma("sync", ctx.cft[:], ctx.cfd, "consts")
    p.dma("sync", ctx.cit[:], ctx.cid, "consts")
    p.dma("sync", ctx.ones_f[:], ctx.ones_d, "consts")
    p.dma("sync", ctx.ones_b[:], ctx.onesb_d, "consts")
    p.dma("sync", ctx.ones_r[:], ctx.onesr_d, "consts")
    if hasattr(ctx, "dqsel_d"):
        p.dma("sync", ctx.dqsel[:], ctx.dqsel_d, "consts")
    if hasattr(ctx, "lam_d"):
        p.dma("sync", ctx.lrows[:], ctx.lam_d, "consts")
        p.dma("sync", ctx.osel[:], ctx.osel_d, "consts")
    for e in ENGS[1:]:
        p.wait_dma(e, "consts")
    p.wait_dma("sync", "consts")


def emit_lambda(p, ctx):
    for i in (0, 1):
        q = ctx.lrows[:, 64 * (2 * i):64 * (2 * i) + 64]
        kk = ctx.lrows[:, 64 * (2 * i + 1):64 * (2 * i + 1) + 64]
        p.op("vector", lambda e, q=q, kk=kk: e.tensor_mul(ctx.lscr[:, 0:64], q, kk))
        p.op("vector", lambda e, i=i: e.reduce_sum(
            ctx.lscr[:, 64 + i:65 + i], ctx.lscr[:, 0:64],
            axis=mybir.AxisListType.X))
    p.wait_eng("scalar", "vector")
    p.op("scalar", lambda e: e.activation(
        ctx.lscr[:, 66:68], ctx.lscr[:, 64:66], AF.Exp))
    p.wait_eng("vector", "scalar")
    p.op("vector", lambda e: e.tensor_sub(
        ctx.lam[:, 0:1], ctx.lscr[:, 66:67], ctx.lscr[:, 67:68]))
    p.op("vector", lambda e: e.tensor_scalar(
        ctx.lam[:, 0:1], ctx.lam[:, 0:1], CF1(ctx, LAMBDA_INIT), None, ALU.add))
    p.op("vector", lambda e: e.tensor_scalar(
        ctx.mlam[:, 0:1], ctx.lam[:, 0:1], CF1(ctx, -1.0), None, ALU.mult))


def emit_ln(p, ctx):
    PS = ctx.ps
    for kc in range(KC):
        b = kc % 2
        if kc >= 2:
            p.wait_eng("sync", "tensor", at=ctx.xs_pe_done[kc - 2])
            p.wait_eng("sync", "scalar", at=ctx.xs_act_done[kc - 2])
        p.dma("sync", ctx.xbuf[b][:], ctx.xfm[128 * kc:128 * kc + 128, :], f"xs{b}")
        p.wait_dma("scalar", f"xs{b}")
        p.wait_eng("scalar", "tensor", at=ctx.x2_pe_done)
        p.op("scalar", lambda e, b=b: e.activation(ctx.x2[:, :], ctx.xbuf[b][:],
                                                   AF.Square))
        ctx.xs_act_done[kc] = p.ecnt["scalar"]
        p.wait_dma("tensor", f"xs{b}")
        for tc in range(4):
            sl = slice(512 * tc, 512 * tc + 512)
            p.op("tensor", lambda e, b=b, sl=sl, tc=tc, kc=kc: e.matmul(
                PS[0:1, 512 * tc:512 * tc + 512], ctx.ones_f[:, 0:1],
                ctx.xbuf[b][:, sl], start=(kc == 0), stop=(kc == KC - 1)),
                inc=(tc == 3))
        p.wait_eng("tensor", "scalar")
        for tc in range(4):
            sl = slice(512 * tc, 512 * tc + 512)
            p.op("tensor", lambda e, sl=sl, tc=tc, kc=kc: e.matmul(
                PS[0:1, 2048 + 512 * tc:2048 + 512 * tc + 512], ctx.ones_b[:, 0:1],
                ctx.x2[:, sl], start=(kc == 0), stop=(kc == KC - 1)),
                inc=(tc == 3))
        ctx.xs_pe_done[kc] = p.ecnt["tensor"]
        ctx.x2_pe_done = p.ecnt["tensor"]
    p.wait_eng("vector", "tensor")
    p.op("vector", lambda e: e.tensor_scalar(
        ctx.murow[:, :], PS[0:1, 0:2048], CF1(ctx, 1.0 / D), None, ALU.mult))
    p.op("vector", lambda e: e.tensor_scalar(
        ctx.isrow[:, :], PS[0:1, 2048:4096], CF1(ctx, 1.0 / D), None, ALU.mult))
    p.op("vector", lambda e: e.tensor_mul(ctx.rtmp[:, :], ctx.murow[:, :],
                                          ctx.murow[:, :]))
    p.op("vector", lambda e: e.tensor_sub(ctx.isrow[:, :], ctx.isrow[:, :],
                                          ctx.rtmp[:, :]))
    p.op("vector", lambda e: e.tensor_scalar(
        ctx.isrow[:, :], ctx.isrow[:, :], CF1(ctx, LN_EPS), None, ALU.add))
    p.op("vector", lambda e: e.reciprocal(ctx.isrow[:, :], ctx.isrow[:, :]))
    p.wait_eng("scalar", "vector")
    p.op("scalar", lambda e: e.activation(ctx.isrow[:, :], ctx.isrow[:, :], AF.Sqrt))
    stats_t = p.ecnt["tensor"]
    stats_s = p.ecnt["scalar"]
    for kc in range(KC):
        b = kc % 2
        if kc >= 2:
            p.wait_eng("sync", "vector", at=ctx.xa_dve_done[kc - 2])
        else:
            p.wait_eng("sync", "tensor", at=stats_t)
            p.wait_eng("sync", "scalar", at=stats_s)
        p.dma("sync", ctx.xbuf[b][:], ctx.xfm[128 * kc:128 * kc + 128, :], f"xa{b}")
        p.wait_dma("tensor", f"xa{b}")
        p.wait_eng("tensor", "scalar")
        for tc in range(4):
            bb = 2 * (tc % 2)
            p.wait_eng("tensor", "vector", at=ctx.bcast_cons.get(bb, 0))
            p.op("tensor", lambda e, tc=tc, bb=bb: e.matmul(
                PS[:, 512 * bb:512 * bb + 512], ctx.ones_r[0:1, :],
                ctx.murow[0:1, 512 * tc:512 * tc + 512], start=True, stop=True))
            p.op("tensor", lambda e, tc=tc, bb=bb: e.matmul(
                PS[:, 512 * (bb + 1):512 * (bb + 1) + 512], ctx.ones_r[0:1, :],
                ctx.isrow[0:1, 512 * tc:512 * tc + 512], start=True, stop=True))
            p.wait_eng("vector", "tensor")
            sl = slice(512 * tc, 512 * tc + 512)
            xnsl = ctx.xn_sl(kc, 512 * tc)
            p.op("vector", lambda e, b=b, sl=sl, bb=bb: e.tensor_sub(
                ctx.ttmp[:, 0:512], ctx.xbuf[b][:, sl], PS[:, 512 * bb:512 * bb + 512]))
            p.op("vector", lambda e, xnsl=xnsl, bb=bb: e.tensor_mul(
                xnsl, ctx.ttmp[:, 0:512], PS[:, 512 * (bb + 1):512 * (bb + 1) + 512]))
            ctx.bcast_cons[bb] = p.ecnt["vector"]
        ctx.xa_dve_done[kc] = p.ecnt["vector"]


def emit_vproj(p, ctx):
    PS = ctx.ps
    vq_pe_done = 0
    casts = []
    for quarter in range(2):
        vc0 = 512 * quarter
        for kc in range(KC):
            b = kc % 2
            gi = 16 * quarter + kc
            if gi >= 2:
                p.wait_eng("sync", "vector", at=casts[gi - 2])
            src = ctx.wv.rearrange("(kc pp) a -> pp kc a", pp=128)[:, kc, vc0:vc0 + 512]
            p.dma("sync", ctx.wvf[b][:, 0:512], src, f"wv{b}")
            p.wait_dma("vector", f"wv{b}")
            if quarter > 0:
                p.wait_eng("vector", "tensor", at=vq_pe_done)
            p.op("vector", lambda e, b=b, kc=kc: e.tensor_copy(
                ctx.wvb[:, 512 * kc:512 * kc + 512], ctx.wvf[b][:, 0:512]))
            casts.append(p.ecnt["vector"])
        p.wait_eng("tensor", "vector")
        for tt in range(16):
            bank = 4 + (tt % 2)
            p.wait_eng("tensor", "scalar", at=ctx.vp_evac_done.get(bank, 0))
            for kc in range(KC):
                stat = ctx.xn_sl(kc, 128 * tt, w=128)
                p.op("tensor", lambda e, stat=stat, kc=kc, bank=bank: e.matmul(
                    PS[:, 512 * bank:512 * bank + 512], stat,
                    ctx.wvb[:, 512 * kc:512 * kc + 512],
                    start=(kc == 0), stop=(kc == KC - 1)), inc=(kc == KC - 1))
            p.wait_eng("scalar", "tensor")
            p.op("scalar", lambda e, tt=tt, vc0=vc0, bank=bank: e.activation(
                ctx.V[:, 1024 * tt + vc0:1024 * tt + vc0 + 512],
                PS[:, 512 * bank:512 * bank + 512], AF.Copy))
            ctx.vp_evac_done[bank] = p.ecnt["scalar"]
        vq_pe_done = p.ecnt["tensor"]


def emit_kqproj(p, ctx, hp):
    PS = ctx.ps
    for wi, (wdram, wbt) in enumerate([(ctx.wq, ctx.wqb), (ctx.wk, ctx.wkb)]):
        p.wait_eng("sync", "vector", at=ctx.wqk_cast_done)
        src = wdram.rearrange("(kc pp) a -> pp kc a", pp=128)[:, :, 128 * hp:128 * hp + 128]
        p.dma("sync", ctx.wqkf[:].rearrange("pp (kc c) -> pp kc c", c=128), src, "wqk")
        p.wait_dma("vector", "wqk")
        p.wait_eng("vector", "tensor", at=ctx.kq_mm_done)
        p.op("vector", lambda e, wbt=wbt: e.tensor_copy(wbt[:], ctx.wqkf[:]))
        ctx.wqk_cast_done = p.ecnt["vector"]
    p.wait_eng("tensor", "vector", at=ctx.wqk_cast_done)
    for dst, wbt, nch, scale, bank in [(ctx.KT, ctx.wkb, 4, None, 6),
                                       (ctx.QT, ctx.wqb, 4, 0.125, 7)]:
        for tch in range(nch):
            p.wait_eng("tensor", "scalar", at=ctx.kq_evac_done)
            for kc in range(KC):
                stat = wbt[:, 128 * kc:128 * kc + 128]
                mov = ctx.xn_sl(kc, 512 * tch, w=512)
                p.op("tensor", lambda e, stat=stat, mov=mov, kc=kc, bank=bank: e.matmul(
                    PS[:, 512 * bank:512 * bank + 512], stat, mov,
                    start=(kc == 0), stop=(kc == KC - 1)), inc=(kc == KC - 1))
            p.wait_eng("scalar", "tensor")
            p.wait_eng("scalar", "vector", at=ctx.dq_vec_done)
            if scale is None:
                p.op("scalar", lambda e, dst=dst, tch=tch, bank=bank: e.activation(
                    dst[:, 512 * tch:512 * tch + 512],
                    PS[:, 512 * bank:512 * bank + 512], AF.Copy))
            else:
                p.op("scalar", lambda e, dst=dst, tch=tch, bank=bank, scale=scale:
                     e.activation(dst[:, 512 * tch:512 * tch + 512],
                                  PS[:, 512 * bank:512 * bank + 512], AF.Copy,
                                  scale=scale))
            ctx.kq_evac_done = p.ecnt["scalar"]
    ctx.kq_mm_done = p.ecnt["tensor"]


def emit_cbound(p, ctx, hp):
    PS = ctx.ps
    p.op("scalar", lambda e: e.activation(ctx.Q2[:], ctx.QT[:], AF.Square))
    p.op("scalar", lambda e: e.activation(ctx.K2[:], ctx.KT[:], AF.Square))
    p.wait_eng("tensor", "scalar")
    nm = 0
    for src, nch in [(ctx.Q2, 4), (ctx.K2, 4)]:
        for i in range(nch):
            bank = 4 if nm % 2 == 0 else 5
            p.wait_eng("tensor", "vector", at=ctx.cb_red_done.get(bank, 0))
            p.op("tensor", lambda e, src=src, i=i, bank=bank: e.matmul(
                PS[0:1, 512 * bank:512 * bank + 512], ctx.ones_b[:, 0:1],
                src[:, 512 * i:512 * i + 512], start=True, stop=True))
            p.wait_eng("vector", "tensor")
            p.op("vector", lambda e, nm=nm, bank=bank: e.reduce_max(
                ctx.crow[:, nm:nm + 1], PS[0:1, 512 * bank:512 * bank + 512],
                axis=mybir.AxisListType.X))
            ctx.cb_red_done[bank] = p.ecnt["vector"]
            nm += 1
    p.op("vector", lambda e: e.reduce_max(ctx.crow[:, 8:9], ctx.crow[:, 0:4],
                                          axis=mybir.AxisListType.X))
    p.op("vector", lambda e: e.reduce_max(ctx.crow[:, 9:10], ctx.crow[:, 4:8],
                                          axis=mybir.AxisListType.X))
    p.op("vector", lambda e: e.tensor_mul(ctx.crow[:, 8:9], ctx.crow[:, 8:9],
                                          ctx.crow[:, 9:10]))
    p.wait_eng("scalar", "vector")
    p.op("scalar", lambda e: e.activation(ctx.crow[:, 10:11], ctx.crow[:, 8:9],
                                          AF.Sqrt))
    p.wait_eng("vector", "scalar")
    p.op("vector", lambda e: e.tensor_scalar(
        ctx.crow[:, 10:11], ctx.crow[:, 10:11], CF1(ctx, -1.0), None, ALU.mult))
    p.wait_eng("tensor", "vector")
    p.op("tensor", lambda e: e.matmul(
        PS[:, 512 * 4:512 * 4 + 1], ctx.ones_r[0:1, :], ctx.crow[0:1, 10:11],
        start=True, stop=True))
    p.wait_eng("scalar", "tensor")
    p.op("scalar", lambda e: e.activation(ctx.mccol[:, 0:1],
                                          PS[:, 512 * 4:512 * 4 + 1], AF.Copy))


def emit_attn_block(p, ctx, hp, blk):
    PS = ctx.ps
    q0 = 512 * blk
    for tk in range(16):
        it = 16 * blk + tk
        mb = it % 2
        p.wait_eng("sync", "vector", at=ctx.mask_cons.get(mb, 0))
        p.dma("sync", ctx.maskb[mb][:], ctx.masks_d[it], f"mk{mb}")
        p.wait_eng("tensor", "scalar", at=ctx.exp_done.get(0, 0))
        p.op("tensor", lambda e, tk=tk, q0=q0: e.matmul(
            PS[:, 0:512], ctx.KT[0:64, 128 * tk:128 * tk + 128],
            ctx.QT[0:64, q0:q0 + 512], start=True, stop=True))
        p.op("tensor", lambda e, tk=tk, q0=q0: e.matmul(
            PS[:, 512:1024], ctx.KT[64:128, 128 * tk:128 * tk + 128],
            ctx.QT[64:128, q0:q0 + 512], start=True, stop=True,
            tile_position=(64, 0)))
        p.wait_eng("scalar", "tensor")
        pb = ctx.P01[it % 2]
        p.op("scalar", lambda e, pb=pb: e.activation(
            pb[:, 0:512], PS[:, 0:512], AF.Exp, bias=ctx.mccol[:, 0:1]))
        p.op("scalar", lambda e, pb=pb: e.activation(
            pb[:, 512:1024], PS[:, 512:1024], AF.Exp, bias=ctx.mccol[:, 0:1]))
        ctx.exp_done[0] = p.ecnt["scalar"]
        p.wait_dma("vector", f"mk{mb}")
        p.wait_eng("vector", "scalar")
        p.op("vector", lambda e, pb=pb, mb=mb: e.tensor_mul(
            pb[:, 0:512], pb[:, 0:512], ctx.maskb[mb][:]))
        p.op("vector", lambda e, pb=pb, mb=mb: e.tensor_mul(
            pb[:, 512:1024], pb[:, 512:1024], ctx.maskb[mb][:]))
        ctx.mask_cons[mb] = p.ecnt["vector"]
        p.wait_eng("tensor", "vector")
        p.op("tensor", lambda e, pb=pb, tk=tk: e.matmul(
            PS[0:1, 512 * 4:512 * 4 + 512], ctx.ones_b[:, 0:1], pb[:, 0:512],
            start=(tk == 0), stop=(tk == 15)))
        p.op("tensor", lambda e, pb=pb, tk=tk: e.matmul(
            PS[0:1, 512 * 5:512 * 5 + 512], ctx.ones_b[:, 0:1], pb[:, 512:1024],
            start=(tk == 0), stop=(tk == 15)))
        vstat = ctx.V[:, 1024 * tk + 128 * hp:1024 * tk + 128 * hp + 128]
        p.op("tensor", lambda e, pb=pb, vstat=vstat, tk=tk: e.matmul(
            PS[:, 512 * 2:512 * 2 + 512], vstat, pb[:, 0:512],
            start=(tk == 0), stop=(tk == 15)))
        p.op("tensor", lambda e, pb=pb, vstat=vstat, tk=tk: e.matmul(
            PS[:, 512 * 3:512 * 3 + 512], vstat, pb[:, 512:1024],
            start=(tk == 0), stop=(tk == 15)))
        ctx.av_done = p.ecnt["tensor"]
    p.wait_eng("vector", "tensor", at=ctx.av_done)
    p.op("vector", lambda e: e.reciprocal(ctx.erow[:, 0:512],
                                          PS[0:1, 512 * 4:512 * 4 + 512]))
    p.op("vector", lambda e: e.reciprocal(ctx.erow[:, 512:1024],
                                          PS[0:1, 512 * 5:512 * 5 + 512]))
    p.op("vector", lambda e: e.tensor_scalar(
        ctx.erow[:, 512:1024], ctx.erow[:, 512:1024], ctx.lam[0:1, 0:1], None,
        ALU.mult))
    p.wait_eng("tensor", "vector")
    p.op("tensor", lambda e: e.matmul(
        PS[:, 512 * 6:512 * 6 + 512], ctx.ones_r[0:1, :], ctx.erow[0:1, 0:512],
        start=True, stop=True))
    p.op("tensor", lambda e: e.matmul(
        PS[:, 512 * 7:512 * 7 + 512], ctx.ones_r[0:1, :], ctx.erow[0:1, 512:1024],
        start=True, stop=True))
    p.wait_eng("scalar", "tensor")
    p.op("scalar", lambda e: e.activation(ctx.ou0[:], PS[:, 512 * 2:512 * 2 + 512],
                                          AF.Copy))
    p.op("scalar", lambda e: e.activation(ctx.ou1[:], PS[:, 512 * 3:512 * 3 + 512],
                                          AF.Copy))
    p.wait_eng("vector", "scalar")
    p.op("vector", lambda e: e.tensor_mul(
        ctx.t1f[:], ctx.ou1[:], PS[:, 512 * 7:512 * 7 + 512]))
    p.op("vector", lambda e: e.tensor_mul(
        ctx.opf[:], ctx.ou0[:], PS[:, 512 * 6:512 * 6 + 512]))
    p.op("vector", lambda e: e.tensor_sub(
        ctx.opf[:], ctx.opf[:], ctx.t1f[:]))
    p.wait_eng("scalar", "vector")
    p.op("scalar", lambda e: e.activation(ctx.sqb[:], ctx.opf[:], AF.Square))
    p.wait_eng("tensor", "scalar")
    p.op("tensor", lambda e: e.matmul(
        PS[0:1, 512 * 4:512 * 4 + 512], ctx.ones_f[:, 0:1], ctx.opf[:],
        start=True, stop=True))
    p.op("tensor", lambda e: e.matmul(
        PS[0:1, 512 * 5:512 * 5 + 512], ctx.ones_b[:, 0:1], ctx.sqb[:],
        start=True, stop=True))
    p.wait_eng("vector", "tensor")
    p.op("vector", lambda e: e.tensor_scalar(
        ctx.erow[:, 0:512], PS[0:1, 512 * 4:512 * 4 + 512],
        CF1(ctx, 1.0 / 128.0), None, ALU.mult))
    p.op("vector", lambda e: e.tensor_mul(ctx.erow[:, 512:1024],
                                          ctx.erow[:, 0:512], ctx.erow[:, 0:512]))
    p.op("vector", lambda e: e.scalar_tensor_tensor(
        ctx.erow[:, 512:1024], PS[0:1, 512 * 5:512 * 5 + 512],
        CF1(ctx, 1.0 / 128.0), ctx.erow[:, 512:1024], ALU.mult, ALU.subtract))
    p.op("vector", lambda e: e.tensor_scalar(
        ctx.erow[:, 512:1024], ctx.erow[:, 512:1024], CF1(ctx, GN_EPS), None,
        ALU.add))
    p.op("vector", lambda e: e.reciprocal(ctx.erow[:, 512:1024],
                                          ctx.erow[:, 512:1024]))
    p.op("vector", lambda e: e.tensor_scalar(
        ctx.erow[:, 512:1024], ctx.erow[:, 512:1024],
        CF1(ctx, (1.0 - LAMBDA_INIT) ** 2), None, ALU.mult))
    p.wait_eng("scalar", "vector")
    p.op("scalar", lambda e: e.activation(ctx.erow[:, 512:1024],
                                          ctx.erow[:, 512:1024], AF.Sqrt))
    p.wait_eng("vector", "scalar")
    p.op("vector", lambda e: e.tensor_mul(ctx.erow[:, 0:512], ctx.erow[:, 0:512],
                                          ctx.erow[:, 512:1024]))
    p.wait_eng("tensor", "vector")
    p.op("tensor", lambda e: e.matmul(
        PS[:, 512 * 6:512 * 6 + 512], ctx.ones_r[0:1, :], ctx.erow[0:1, 512:1024],
        start=True, stop=True))
    p.op("tensor", lambda e: e.matmul(
        PS[:, 512 * 7:512 * 7 + 512], ctx.ones_r[0:1, :], ctx.erow[0:1, 0:512],
        start=True, stop=True))
    p.wait_eng("vector", "tensor")
    gsl = ctx.gn_all[:, 2048 * hp + 512 * blk:2048 * hp + 512 * blk + 512]
    p.op("vector", lambda e: e.tensor_mul(
        ctx.t1f[:], ctx.opf[:], PS[:, 512 * 6:512 * 6 + 512]))
    p.op("vector", lambda e, gsl=gsl: e.tensor_sub(
        gsl, ctx.t1f[:], PS[:, 512 * 7:512 * 7 + 512]))


def emit_oproj(p, ctx):
    PS = ctx.ps
    p.wait_eng("sync", "tensor")
    p.wait_eng("sync", "vector")
    p.wait_eng("sync", "scalar")
    p.wait_eng("vector", "tensor")
    for hi in range(8):
        base = 2048 * hi
        for half in range(2):
            a = ctx.gn_all[:, base + 1024 * half:base + 1024 * half + 1024]
            bb_ = ctx.gn_all[:, base + 1024 * (1 - half):base + 1024 * (1 - half) + 1024]
            dst = ctx.gn_eff[:, base + 1024 * half:base + 1024 * half + 1024]
            p.op("vector", lambda e, bb_=bb_: e.tensor_single_scalar(
                ctx.P01[0][:], bb_, ctx.osel[:, 1:2], ALU.mult))
            p.op("vector", lambda e, a=a, dst=dst: e.scalar_tensor_tensor(
                dst, a, ctx.osel[:, 0:1], ctx.P01[0][:], ALU.mult, ALU.add))
    for dt in range(16):
        b = dt % 2
        p.wait_eng("sync", "vector", at=ctx.wo_cast_done.get(b, 0))
        src = ctx.wo.rearrange("(ac pp) d -> pp ac d", pp=128)[:, :, 128 * dt:128 * dt + 128]
        p.dma("sync", ctx.wof[b][:].rearrange("pp (ac c) -> pp ac c", c=128), src,
              f"wo{b}")
        p.wait_dma("vector", f"wo{b}")
        p.wait_eng("vector", "tensor", at=ctx.op_mm_done.get(b, 0))
        p.op("vector", lambda e, b=b: e.tensor_copy(ctx.wob[b][:], ctx.wof[b][:]))
        ctx.wo_cast_done[b] = p.ecnt["vector"]
        p.wait_eng("tensor", "vector")
        p.wait_eng("tensor", "scalar", at=ctx.op_evac_done.get(0, 0))
        for kc in range(KC):
            stat = ctx.wob[b][:, 128 * kc:128 * kc + 128]
            for hi in range(8):
                mov = ctx.gn_eff[:, 2048 * hi:2048 * hi + 2048].rearrange(
                    "p (r j) -> p j r", j=16)[:, kc, :]
                p.op("tensor", lambda e, stat=stat, mov=mov, hi=hi, kc=kc: e.matmul(
                    PS[:, 512 * hi:512 * hi + 128], stat, mov,
                    start=(kc == 0), stop=(kc == KC - 1)),
                    inc=(kc == KC - 1 and hi == 7))
        p.wait_eng("scalar", "tensor")
        for hi in range(8):
            p.op("scalar", lambda e, dt=dt, hi=hi: e.activation(
                ctx.s_bf[:, 1024 * dt + 128 * hi:1024 * dt + 128 * hi + 128],
                PS[:, 512 * hi:512 * hi + 128], AF.Copy))
        ctx.op_evac_done[0] = p.ecnt["scalar"]
        ctx.op_mm_done[b] = p.ecnt["tensor"]


def emit_ffn(p, ctx):
    PS = ctx.ps
    p.wait_eng("vector", "tensor")
    p.wait_eng("sync", "tensor")
    p.wait_eng("sync", "scalar")
    p.wait_eng("sync", "vector")
    p.op("vector", lambda e: e.memset(ctx.dacc_a[:], 0.0))
    p.op("vector", lambda e: e.memset(ctx.dacc_b[:], 0.0))
    gu_mm_done = {}
    dn_mm_done = {}
    ld_done = {}
    for b5 in range(8):
        wsrc = ctx.ccw[b5]
        for l in range(16):
            wb = l % 2
            for mi, wt in [(0, ctx.wgt[wb]), (1, ctx.wut[wb])]:
                p.wait_eng("sync", "tensor", at=gu_mm_done.get((wb, mi), 0))
                src = wsrc[mi].rearrange("(kc pp) n -> pp kc n", pp=128)[:, :, 128 * l:128 * l + 128]
                p.dma("sync", wt[:].rearrange("pp (kc c) -> pp kc c", c=128), src,
                      f"wg{wb}{mi}")
            p.wait_dma("tensor", f"wg{wb}0")
            p.wait_dma("tensor", f"wg{wb}1")
            p.wait_eng("tensor", "scalar", at=ld_done.get("gelu", 0))
            for kc in range(KC):
                stat = ctx.wgt[wb][:, 128 * kc:128 * kc + 128]
                for tqc in range(2):
                    p.op("tensor", lambda e, stat=stat, kc=kc, tqc=tqc: e.matmul(
                        PS[:, 512 * tqc:512 * tqc + 512], stat,
                        ctx.xn_own[:, 1024 * kc + 512 * tqc:1024 * kc + 512 * tqc + 512],
                        start=(kc == 0), stop=(kc == KC - 1)),
                        inc=(kc == KC - 1 and tqc == 1))
            p.wait_eng("tensor", "scalar", at=ld_done.get("up", 0))
            for kc in range(KC):
                stat = ctx.wut[wb][:, 128 * kc:128 * kc + 128]
                for tqc in range(2):
                    p.op("tensor", lambda e, stat=stat, kc=kc, tqc=tqc: e.matmul(
                        PS[:, 1024 + 512 * tqc:1024 + 512 * tqc + 512], stat,
                        ctx.xn_own[:, 1024 * kc + 512 * tqc:1024 * kc + 512 * tqc + 512],
                        start=(kc == 0), stop=(kc == KC - 1)),
                        inc=(kc == KC - 1 and tqc == 1))
            gu_mm_done[(wb, 0)] = gu_mm_done[(wb, 1)] = p.ecnt["tensor"]
            p.wait_eng("scalar", "tensor")
            p.wait_eng("scalar", "vector", at=ld_done.get("inter", 0))
            p.op("scalar", lambda e: e.activation(ctx.ffn_gg[:], PS[:, 0:1024],
                                                  AF.Gelu_apprx_tanh))
            ld_done["gelu"] = p.ecnt["scalar"]
            p.op("scalar", lambda e: e.activation(ctx.ffn_ub[:], PS[:, 1024:2048],
                                                  AF.Copy))
            ld_done["up"] = p.ecnt["scalar"]
            p.wait_eng("vector", "scalar")
            p.op("vector", lambda e, l=l: e.tensor_mul(
                ctx.inter[:, 1024 * l:1024 * l + 1024], ctx.ffn_gg[:], ctx.ffn_ub[:]))
            ld_done["inter"] = p.ecnt["vector"]
        inter_done = p.ecnt["vector"]
        for ht in range(16):
            wb = ht % 2
            p.wait_eng("sync", "tensor", at=dn_mm_done.get(wb, 0))
            src = wsrc[2].rearrange("(nc pp) hh -> pp nc hh", pp=128)[:, :, 128 * ht:128 * ht + 128]
            p.dma("sync", ctx.wdt[wb][:].rearrange("pp (nc c) -> pp nc c", c=128),
                  src, f"wd{wb}")
            p.wait_dma("tensor", f"wd{wb}")
            if ht == 0:
                p.wait_eng("tensor", "vector", at=inter_done)
            p.wait_eng("tensor", "vector", at=ld_done.get("dacc", 0))
            for nc_ in range(KC):
                stat = ctx.wdt[wb][:, 128 * nc_:128 * nc_ + 128]
                for tqc in range(2):
                    p.op("tensor", lambda e, stat=stat, nc_=nc_, tqc=tqc: e.matmul(
                        PS[:, 2048 + 512 * tqc:2048 + 512 * tqc + 512], stat,
                        ctx.inter[:, 1024 * nc_ + 512 * tqc:1024 * nc_ + 512 * tqc + 512],
                        start=(nc_ == 0), stop=(nc_ == KC - 1)),
                        inc=(nc_ == KC - 1 and tqc == 1))
            dn_mm_done[wb] = p.ecnt["tensor"]
            p.wait_eng("vector", "tensor")
            dsl = ctx.dacc_a[:, 1024 * ht:1024 * ht + 1024] if ht < 8 else \
                ctx.dacc_b[:, 1024 * (ht - 8):1024 * (ht - 8) + 1024]
            p.op("vector", lambda e, dsl=dsl: e.tensor_add(dsl, dsl,
                                                           PS[:, 2048:3072]))
            ld_done["dacc"] = p.ecnt["vector"]
    oadds = []
    for ht in range(16):
        b = ht % 2
        dsl = ctx.dacc_a[:, 1024 * ht:1024 * ht + 1024] if ht < 8 else \
            ctx.dacc_b[:, 1024 * (ht - 8):1024 * (ht - 8) + 1024]
        p.wait_eng("scalar", "vector")
        p.op("scalar", lambda e, dsl=dsl: e.activation(ctx.gbuf[:], dsl, AF.Silu))
        if ht >= 2:
            p.wait_eng("sync", "vector", at=oadds[ht - 2])
        else:
            p.wait_eng("sync", "tensor")
        p.dma("sync", ctx.xfbuf[b][:], ctx.xfm[128 * ht:128 * ht + 128, 0:1024],
              f"xf{b}")
        p.wait_eng("vector", "scalar")
        p.op("vector", lambda e, ht=ht: e.tensor_mul(
            ctx.sgtmp[:], ctx.s_bf[:, 1024 * ht:1024 * ht + 1024], ctx.gbuf[:]))
        p.wait_dma("vector", f"xf{b}")
        if ht >= 2:
            p.wait_dma("vector", f"xo{b}")
        p.op("vector", lambda e, b=b: e.tensor_add(
            ctx.obuf[b][:], ctx.xfbuf[b][:], ctx.sgtmp[:]))
        oadds.append(p.ecnt["vector"])
        p.wait_eng("scalar", "vector")
        p.dma("scalar", ctx.out_fm[128 * ht:128 * ht + 128, :], ctx.obuf[b][:],
              f"xo{b}")
    p.wait_dma("gpsimd", "xo0")
    p.wait_dma("gpsimd", "xo1")


def emit_attention(p, ctx):
    emit_lambda(p, ctx)
    emit_ln(p, ctx)
    emit_vproj(p, ctx)
    for hp in range(8):
        emit_kqproj(p, ctx, hp)
        emit_cbound(p, ctx, hp)
        for blk in range(4):
            emit_attn_block(p, ctx, hp, blk)
    emit_oproj(p, ctx)


def build_program(mode):
    nc = bass.Bass("TRN2", target_bir_lowering=False,
                   dynamic_dma_scratch_size=4096)
    ctx = make_ctx(nc, mode)
    if mode == "dequant":
        ctx.cc_in = nc.dram_tensor("cc", [3 * T, 2048], BF16, kind="ExternalOutput").ap()
    if mode in ("ffn", "fused", "layer2"):
        ctx.out_fm = nc.dram_tensor("out_fm", [2048, 1024], F32,
                                    kind="ExternalOutput").ap()
    if mode == "fused":
        ctx.dbg_s = nc.dram_tensor("dbg_s", [128, 16 * 1024], BF16,
                                   kind="ExternalOutput").ap()
        ctx.dbg_xn = nc.dram_tensor("dbg_xn", [128, 16 * 1024], BF16,
                                    kind="ExternalOutput").ap()
        ctx.dbg_gn = nc.dram_tensor("dbg_gn", [128, 16 * 1024], BF16,
                                    kind="ExternalOutput").ap()
        ctx.dbg_cc = nc.dram_tensor("dbg_cc", [24, 128, 2048], BF16,
                                    kind="ExternalOutput").ap()
        ctx.dbg_da = nc.dram_tensor("dbg_da", [128, 8 * 1024], F32,
                                    kind="ExternalOutput").ap()
    if mode == "attn":
        ctx.s_out = nc.dram_tensor("s_out", [2048, 1024], BF16, kind="ExternalOutput").ap()
        ctx.xn_out = nc.dram_tensor("xn_out", [2048, 2048], BF16, kind="ExternalOutput").ap()
        ctx.dbg_gn = nc.dram_tensor("dbg_gn", [2048, 1024], BF16, kind="ExternalOutput").ap()
        ctx.dbg_kt = nc.dram_tensor("dbg_kt", [128, 2048], BF16, kind="ExternalOutput").ap()
        ctx.dbg_v = nc.dram_tensor("dbg_v", [128, 2048], BF16, kind="ExternalOutput").ap()
        ctx.dbg_rows = nc.dram_tensor("dbg_rows", [128, 16], F32, kind="ExternalOutput").ap()
    sems = {n: nc.alloc_semaphore(n) for n in sem_names_for(mode)}

    with nc.Block() as block:
        def emit_all(p):
            reset_sched_state(ctx)
            emit_preamble(p, ctx)
            if mode == "dequant":
                emit_dequant(p, ctx)
                emit_dequant_wait_done(p, ctx)
            elif mode == "ffn":
                emit_ln(p, ctx)
                p.wait_eng("sync", "tensor")
                p.wait_eng("sync", "vector")
                for dt in range(16):
                    p.dma("sync", ctx.s_bf[:, 1024 * dt:1024 * dt + 1024],
                          ctx.s_in[128 * dt:128 * dt + 128, :], "sld")
                p.wait_dma("vector", "sld")
                emit_ffn(p, ctx)
            elif mode == "layer2":
                emit_attention(p, ctx)
                emit_ffn(p, ctx)
            elif mode == "fused":
                emit_lambda(p, ctx)
                emit_ln(p, ctx)
                emit_vproj(p, ctx)
                emit_dequant(p, ctx)
                emit_dequant_wait_done(p, ctx)
                if not globals().get("SKIP_CC", False):
                    p.op("gpsimd", lambda e: e.collective_compute(
                        "AllGather", mybir.AluOpType.bypass,
                        replica_groups=[[0, 1, 2, 3, 4, 5, 6, 7]],
                        ins=[ctx.cc_in[:]], outs=[ctx.cc_out[:]]))
                else:
                    p.op("gpsimd", lambda e: e.engine_nop())
                ctx.dq_vec_done = p.ecnt["vector"]
                p.wait_dma("scalar", "dq_out0")
                p.wait_dma("scalar", "dq_out1")
                p.wait_dma("vector", "dq_out0")
                p.wait_dma("vector", "dq_out1")
                for hp in range(8):
                    emit_kqproj(p, ctx, hp)
                    emit_cbound(p, ctx, hp)
                    for blk in range(4):
                        emit_attn_block(p, ctx, hp, blk)
                emit_oproj(p, ctx)
                p.wait_eng("sync", "gpsimd")
                p.wait_eng("sync", "scalar")
                p.wait_eng("sync", "vector")
                p.wait_eng("sync", "tensor")
                p.dma("sync", ctx.dbg_s, ctx.s_bf[:], "sout")
                p.dma("sync", ctx.dbg_xn, ctx.xn_own[:], "sout")
                p.dma("sync", ctx.dbg_gn, ctx.gn_all[:], "sout")
                for j in range(24):
                    p.dma("sync", ctx.dbg_cc[j],
                          ctx.cc_out[2048 * j:2048 * j + 128, :], "sout")
                emit_ffn(p, ctx)
                p.wait_eng("sync", "vector")
                p.dma("sync", ctx.dbg_da, ctx.dacc_a[:], "sout")
                p.wait_dma("gpsimd", "sout")
            elif mode == "attn":
                emit_attention(p, ctx)
                p.wait_eng("sync", "scalar")
                for dt in range(16):
                    p.dma("sync", ctx.dbg_gn[128 * dt:128 * dt + 128, :],
                          ctx.gn_all[:, 1024 * dt:1024 * dt + 1024], "sout")
                for dt in range(16):
                    p.dma("sync", ctx.s_out[128 * dt:128 * dt + 128, :],
                          ctx.s_bf[:, 1024 * dt:1024 * dt + 1024], "sout")
                    p.dma("sync", ctx.xn_out[128 * dt:128 * dt + 128, 0:1024],
                          ctx.xn_own[:, 1024 * dt:1024 * dt + 1024], "sout")
                    p.dma("sync", ctx.xn_out[128 * dt:128 * dt + 128, 1024:2048],
                          ctx.xn_rest[:, 1024 * dt:1024 * dt + 1024], "sout")
                p.dma("sync", ctx.dbg_rows[0:1, 0:11], ctx.crow[0:1, 0:11], "sout")
                p.dma("sync", ctx.dbg_rows[1:2, 0:1], ctx.lam[:], "sout")
                p.dma("sync", ctx.dbg_rows[2:3, 0:1], ctx.mccol[0:1, :], "sout")
                p.wait_dma("sync", "sout")

        for name, reg in [("sync", block.sync), ("vector", block.vector),
                          ("scalar", block.scalar), ("tensor", block.tensor),
                          ("gpsimd", block.gpsimd)]:
            def body(eng, name=name):
                emit_all(Prog(nc, sems, name, eng))
            reg(body)
    return nc


def build_dequant_test():
    return build_program("dequant")


def pi2_rows(core):
    return np.arange(2048) * 2 + 4096 * (core % 4) + (1 if core >= 4 else 0)


def host_dequant_slices(core, gW, gS, uW, uS, dW, dS):
    rows = pi2_rows(core)
    gbT = np.ascontiguousarray(gW[rows, :].T)
    ubT = np.ascontiguousarray(uW[rows, :].T)
    gsT = np.ascontiguousarray(np.repeat(gS[rows, :].T, 16, axis=0))
    usT = np.ascontiguousarray(np.repeat(uS[rows, :].T, 16, axis=0))
    nb0 = 2048 * (core % 4)
    dbT = np.ascontiguousarray(dW[:, nb0:nb0 + 2048].T)
    dbT = np.ascontiguousarray(dbT[:, PI])
    j0 = 128 * (core % 4)
    dsT = np.ascontiguousarray(np.repeat(dS[:, j0:j0 + 128].T, 16, axis=0))
    dsT = np.ascontiguousarray(dsT[:, PI])
    w = 1.0 if core < 4 else 0.0
    dqsel = np.tile(np.array([[w, 1.0 - w]], np.float32), (128, 1))
    return gbT, ubT, gsT, usT, dbT, dsT, dqsel


def tau_perm(core):
    if core % 2 == 0:
        return np.arange(2048)
    return np.concatenate([np.arange(1024, 2048), np.arange(0, 1024)])


def head_order(core):
    own = np.arange(8) + 8 * (core % 2)
    rest = np.arange(8) + 8 * (1 - core % 2)
    return np.concatenate([own, rest])


def host_attn_inputs(core, x, Wq, Wk, Wv, Wo, lq1, lk1, lq2, lk2):
    b = core // 2
    tau = tau_perm(core)
    ho = head_order(core)
    qk_cols = np.concatenate([np.arange(128 * g, 128 * g + 128) for g in ho])
    v_cols = np.concatenate([np.arange(128 * g, 128 * g + 128) for g in ho])
    xfm = np.ascontiguousarray(x[b].T[PI][:, tau], np.float32)
    wq = np.ascontiguousarray(Wq[PI, :][:, qk_cols], np.float32)
    wk = np.ascontiguousarray(Wk[PI, :][:, qk_cols], np.float32)
    wv = np.ascontiguousarray(Wv[PI, :][:, v_cols], np.float32)
    wo = np.ascontiguousarray(Wo[:, PI], np.float32)
    lams = np.concatenate([lq1, lk1, lq2, lk2]).reshape(1, 256).astype(np.float32)
    masks = np.zeros((64, 128, 512), ml_dtypes.bfloat16)
    for i in range(64):
        blk, tk = i // 16, i % 16
        kpos = tau[128 * tk:128 * tk + 128][:, None]
        qpos = tau[512 * blk:512 * blk + 512][None, :]
        masks[i] = (kpos <= qpos).astype(ml_dtypes.bfloat16)
    w = 1.0 if core % 2 == 0 else 0.0
    osel = np.tile(np.array([[w, 1.0 - w]], np.float32), (128, 1))
    return {"xfm": xfm, "wq": wq, "wk": wk, "wv": wv, "wo": wo,
            "lams": lams, "masks": masks, "osel": osel}


_PROG_CACHE = {}


def _prog(mode):
    if mode not in _PROG_CACHE:
        _PROG_CACHE[mode] = build_program(mode)
    return _PROG_CACHE[mode]


def kernel(L0_attn_x, L0_attn_Wq, L0_attn_Wk, L0_attn_Wv, L0_attn_Wo,
           L0_attn_lambda_q1, L0_attn_lambda_k1, L0_attn_lambda_q2,
           L0_attn_lambda_k2, L0_ffn_gate_W_packed, L0_ffn_gate_scales,
           L0_ffn_up_W_packed, L0_ffn_up_scales, L0_ffn_down_W_packed,
           L0_ffn_down_scales):
    x = np.asarray(L0_attn_x, np.float32)
    gW = np.asarray(L0_ffn_gate_W_packed, np.int32)
    uW = np.asarray(L0_ffn_up_W_packed, np.int32)
    dW = np.asarray(L0_ffn_down_W_packed, np.int32)
    gS = np.asarray(L0_ffn_gate_scales, np.float32)
    uS = np.asarray(L0_ffn_up_scales, np.float32)
    dS = np.asarray(L0_ffn_down_scales, np.float32)

    in_maps = []
    for c in range(NCORE):
        gbT, ubT, gsT, usT, dbT, dsT, dqsel = host_dequant_slices(
            c, gW, gS, uW, uS, dW, dS)
        d = host_attn_inputs(c, x, np.asarray(L0_attn_Wq, np.float32),
                             np.asarray(L0_attn_Wk, np.float32),
                             np.asarray(L0_attn_Wv, np.float32),
                             np.asarray(L0_attn_Wo, np.float32),
                             np.asarray(L0_attn_lambda_q1, np.float32),
                             np.asarray(L0_attn_lambda_k1, np.float32),
                             np.asarray(L0_attn_lambda_q2, np.float32),
                             np.asarray(L0_attn_lambda_k2, np.float32))
        d.update({"gbT": gbT, "ubT": ubT, "gsT": gsT, "usT": usT,
                  "dbT": dbT, "dsT": dsT, "dqsel": dqsel})
        in_maps.append(d)
    res = run_bass_kernel_spmd(_prog("fused"), in_maps, core_ids=list(range(NCORE)))

    out = np.zeros((BATCH, T, D), np.float32)
    for c in range(NCORE):
        o = np.asarray(res.results[c]["out_fm"], np.float32)
        tau = tau_perm(c)
        out[c // 2, tau[:1024], :] = o[PI_INV, :].T
    return out


# version 13
# speedup vs baseline: 1.1450x; 1.1450x over previous
import math

import numpy as np
import ml_dtypes

import concourse.bass as bass
import concourse.mybir as mybir
from concourse.bass_utils import run_bass_kernel_spmd

F32 = mybir.dt.float32
BF16 = mybir.dt.bfloat16
I32 = mybir.dt.int32
ALU = mybir.AluOpType
AF = mybir.ActivationFunctionType
BFNP = ml_dtypes.bfloat16

D = 2048
T = 2048
BATCH = 4
NCORE = 8
HD = 64
NPAIR = 16
VD = 128
MLP = 16384
KC = D // 128
LAMBDA_INIT = 0.8 - 0.6 * math.exp(-0.3 * 12)
LN_EPS = 1e-5
GN_EPS = 1e-5

PI = np.concatenate([np.arange(0, D, 2), np.arange(1, D, 2)])
PI_INV = np.argsort(PI)

FVALS = [7.5, -1.0, 1.0, 0.5, -2.0, 4.0, -8.0, -10.0, 0.125,
         1.0 / 128.0, -1.0 / 128.0, LN_EPS, GN_EPS, 1.0 - LAMBDA_INIT,
         LAMBDA_INIT, 1.0 / D, 0.0, -16.0, 2.0, -0.5, 28.0, 24.0, -1e9,
         (1.0 - LAMBDA_INIT) ** 2]
IVALS = [15]
FIDX = {v: i for i, v in enumerate(FVALS)}
IIDX = {v: i for i, v in enumerate(IVALS)}


def own_qblocks(core):
    if core % 2 == 0:
        return [(0, 512), (1536, 2048)]
    return [(512, 1024), (1024, 1536)]


ENGS = ["sync", "vector", "scalar", "tensor", "gpsimd"]


class Prog:

    def __init__(self, nc, sems, eng_name, eng):
        self.nc = nc
        self.sems = sems
        self.eng_name = eng_name
        self.eng = eng
        self.ecnt = {e: 0 for e in ENGS}
        self.dcnt = {k: 0 for k in sems if not k.startswith("E_")}

    def on(self, name):
        return name == self.eng_name

    def op(self, eng_name, fn, inc=True, drain=None):
        if drain is None:
            drain = eng_name in ("vector", "scalar")
        if drain:
            self.wait_self(eng_name)
        if self.on(eng_name):
            instr = fn(self.eng)
            if inc and instr is not None:
                instr.then_inc(self.sems["E_" + eng_name], 1)
        if inc:
            self.ecnt[eng_name] += 1

    def wait_self(self, eng_name):
        if self.on(eng_name) and self.ecnt[eng_name] > 0:
            self.eng.wait_ge(self.sems["E_" + eng_name], self.ecnt[eng_name])

    def wait_eng(self, waiter, producer, at=None):
        v = self.ecnt[producer] if at is None else at
        if self.on(waiter) and v > 0:
            self.eng.wait_ge(self.sems["E_" + producer], v)

    def dma(self, eng_name, out, in_, stream):
        if self.on(eng_name):
            self.eng.dma_start(out=out, in_=in_).then_inc(self.sems[stream], 16)
        self.dcnt[stream] += 16

    def wait_dma(self, waiter, stream, at=None):
        v = self.dcnt[stream] if at is None else at
        if self.on(waiter) and v > 0:
            self.eng.wait_ge(self.sems[stream], v)

    def mark(self, *streams):
        return {k: self.dcnt[k] for k in streams}


class Ctx:
    pass


def make_consts(nc):
    cf_np = np.tile(np.array(FVALS, np.float32), (128, 1))
    ci_np = np.tile(np.array(IVALS, np.int32), (128, 1))
    cfd = nc.inline_tensor(cf_np, name="cfd").ap()
    cid = nc.inline_tensor(ci_np, name="cid").ap()
    cft = nc.alloc_sbuf_tensor("cft", [128, len(FVALS)], F32)
    cit = nc.alloc_sbuf_tensor("cit", [128, len(IVALS)], I32)
    for v, i in FIDX.items():
        nc.const_aps.register(F32, v, cft[:, i:i + 1])
    return cfd, cid, cft, cit


def CF(ctx, v):
    return ctx.cft[:, FIDX[v]:FIDX[v] + 1]


def CI(ctx, v):
    return ctx.cit[:, IIDX[v]:IIDX[v] + 1]


def CF1(ctx, v):
    return ctx.cft[0:1, FIDX[v]:FIDX[v] + 1]


DQ_TILES = []
for _mi, _ntt, _nibs, _rb in [(0, 8, ("lo", "hi"), 0), (1, 8, ("lo", "hi"), T),
                              (2, 16, ("dyn",), 2 * T)]:
    for _tt in range(_ntt):
        for _h in range(4):
            DQ_TILES.append((_mi, _tt, _h, _nibs, _rb))


def emit_dq_tiles(p, ctx, n):
    F = 512
    st = ctx.dq_state
    nbuf = len(ctx.dq_bytes)
    for _ in range(n):
        if st["k"] >= len(DQ_TILES):
            return
        k = st["k"]
        mi, tt, h, nibbles, rbase = DQ_TILES[k]
        bT = [ctx.gbT, ctx.ubT, ctx.dbT][mi]
        sT = [ctx.gsT, ctx.usT, ctx.dsT][mi]
        buf = k % nbuf
        csl = slice(F * h, F * h + F)
        if k >= nbuf:
            p.wait_dma("sync", f"dq_out{buf}", at=st["oat"][k - nbuf])
        p.dma("sync", ctx.dq_bytes[buf][:],
              bT[128 * tt:128 * tt + 128, csl], f"dq_ld{buf}")
        p.dma("sync", ctx.dq_scale[buf][:],
              sT[128 * tt:128 * tt + 128, csl], f"dq_ld{buf}")
        ld_target = p.dcnt[f"dq_ld{buf}"]
        first = True
        for nib in nibbles:
            vbuf = ctx.dq_val[buf][0 if nib != "hi" else 1]
            if first:
                p.wait_dma("vector", f"dq_ld{buf}", at=ld_target)
                first = False
            bts = ctx.dq_bytes[buf][:]
            if nib == "dyn":
                p.op("vector", lambda e: e.tensor_single_scalar(
                    ctx.dq_ci[:], bts, CI(ctx, 15), ALU.bitwise_and))
                p.op("vector", lambda e: e.tensor_copy(
                    ctx.dq_cf2[:], ctx.dq_ci[:]))
                p.op("vector", lambda e: e.tensor_scalar(
                    ctx.dq_ci[:], bts, 4, CI(ctx, 15),
                    ALU.logical_shift_right, ALU.bitwise_and))
                p.op("vector", lambda e: e.tensor_copy(
                    ctx.dq_cf[:], ctx.dq_ci[:]))
                p.op("vector", lambda e: e.tensor_single_scalar(
                    ctx.dq_cf[:], ctx.dq_cf[:], ctx.dqsel[:, 1:2], ALU.mult))
                p.op("vector", lambda e: e.scalar_tensor_tensor(
                    ctx.dq_cf[:], ctx.dq_cf2[:], ctx.dqsel[:, 0:1],
                    ctx.dq_cf[:], ALU.mult, ALU.add))
            elif nib == "lo":
                p.op("vector", lambda e: e.tensor_single_scalar(
                    ctx.dq_ci[:], bts, CI(ctx, 15), ALU.bitwise_and))
                p.op("vector", lambda e: e.tensor_copy(
                    ctx.dq_cf[:], ctx.dq_ci[:]))
            else:
                p.op("vector", lambda e: e.tensor_scalar(
                    ctx.dq_ci[:], bts, 4, CI(ctx, 15),
                    ALU.logical_shift_right, ALU.bitwise_and))
                p.op("vector", lambda e: e.tensor_copy(
                    ctx.dq_cf[:], ctx.dq_ci[:]))
            p.wait_eng("scalar", "vector")
            p.op("scalar", lambda e: e.activation(
                ctx.dq_sgn[:], ctx.dq_cf[:], AF.Sign, bias=7.5, scale=-1.0))
            p.wait_eng("vector", "scalar")
            p.op("vector", lambda e: e.scalar_tensor_tensor(
                ctx.dq_t[:], ctx.dq_sgn[:], CF(ctx, 4.0),
                ctx.dq_cf[:], ALU.mult, ALU.add))
            p.wait_eng("scalar", "vector")
            p.op("scalar", lambda e: e.activation(
                ctx.dq_r1[:], ctx.dq_t[:], AF.Relu, bias=-8.0, scale=1.0))
            p.op("scalar", lambda e: e.activation(
                ctx.dq_r2[:], ctx.dq_t[:], AF.Relu, bias=-10.0, scale=1.0))
            p.wait_eng("vector", "scalar")
            p.op("vector", lambda e: e.scalar_tensor_tensor(
                ctx.dq_a[:], ctx.dq_r1[:], CF(ctx, 0.5),
                ctx.dq_r2[:], ALU.mult, ALU.add))
            p.op("vector", lambda e: e.scalar_tensor_tensor(
                ctx.dq_a[:], ctx.dq_t[:], CF(ctx, 0.5),
                ctx.dq_a[:], ALU.mult, ALU.add))
            p.op("vector", lambda e: e.scalar_tensor_tensor(
                ctx.dq_a[:], ctx.dq_a[:], CF(ctx, -2.0),
                ctx.dq_sgn[:], ALU.add, ALU.mult))
            p.op("vector", lambda e, vbuf=vbuf: e.tensor_tensor(
                vbuf[:], ctx.dq_a[:], ctx.dq_scale[buf][:], ALU.mult))
        p.wait_eng("scalar", "vector")
        for nib in nibbles:
            if mi < 2:
                row0 = rbase + (0 if nib == "lo" else 1024) + 128 * tt
            else:
                row0 = rbase + 128 * tt
            vbuf = ctx.dq_val[buf][0 if nib != "hi" else 1]
            p.dma("scalar", ctx.cc_in[row0:row0 + 128, csl], vbuf[:],
                  f"dq_out{buf}")
        st["oat"].append(p.dcnt[f"dq_out{buf}"])
        st["k"] = k + 1


def emit_dequant(p, ctx):
    emit_dq_tiles(p, ctx, len(DQ_TILES))


def emit_dequant_wait_done(p, ctx):
    for i in range(3):
        p.wait_dma("gpsimd", f"dq_out{i}")


A0 = 5120
A2 = A0 + 32 * 1024
B0 = A2 + 32 * 1024
C0 = B0 + 64 * 1024
D0 = C0 + 32 * 1024
E0 = D0 + 32 * 1024


def make_ctx(nc, mode):
    ctx = Ctx()
    ctx.mode = mode
    at = nc.alloc_sbuf_tensor_at

    def T_(name, off, shape, dt):
        return at(name, shape, dt, offset=off)

    dram = lambda n, sh, dt: nc.dram_tensor(n, sh, dt, kind="ExternalInput").ap()
    if mode in ("dequant", "fused"):
        ctx.gbT = dram("gbT", [1024, 2048], I32)
        ctx.ubT = dram("ubT", [1024, 2048], I32)
        ctx.dbT = dram("dbT", [2048, 2048], I32)
        ctx.gsT = dram("gsT", [1024, 2048], F32)
        ctx.usT = dram("usT", [1024, 2048], F32)
        ctx.dsT = dram("dsT", [2048, 2048], F32)
        ctx.dqsel_d = dram("dqsel", [128, 2], F32)
    if mode in ("attn", "fused", "ffn", "layer2"):
        ctx.xfm = dram("xfm", [2048, 2048], F32)
    if mode == "ffn":
        ctx.ccw_d = dram("ccw", [8, 3, 2048, 2048], BF16)
        ctx.s_in = dram("s_in", [2048, 1024], BF16)
    if mode == "layer2":
        ctx.ccw_d = dram("ccw", [8, 3, 2048, 2048], BF16)
    if mode in ("ffn", "layer2"):
        ctx.ccw = [[ctx.ccw_d[b5, mi] for mi in range(3)] for b5 in range(8)]
    if mode == "fused":
        ctx.cc_in = nc.dram_tensor("cc_in", [3 * T, 2048], BF16,
                                   kind="Internal").ap()
        ctx.cc_out = nc.dram_tensor("cc_out", [8 * 3 * T, 2048], BF16,
                                    kind="Internal", addr_space="Shared").ap()
        ctx.ccw = [[ctx.cc_out[(3 * b5 + mi) * T:(3 * b5 + mi + 1) * T, :]
                    for mi in range(3)] for b5 in range(8)]
    if mode in ("attn", "fused", "layer2"):
        ctx.wq = dram("wq", [2048, 2048], F32)
        ctx.wk = dram("wk", [2048, 2048], F32)
        ctx.wv = dram("wv", [2048, 2048], F32)
        ctx.wo = dram("wo", [2048, 2048], F32)
        ctx.lam_d = dram("lams", [1, 256], F32)
        ctx.osel_d = dram("osel", [128, 2], F32)
        ctx.masks_d = dram("masks", [64, 128, 512], BF16)

    cf_np = np.tile(np.array(FVALS, np.float32), (128, 1))
    ci_np = np.tile(np.array(IVALS, np.int32), (128, 1))
    ctx.cfd = nc.inline_tensor(cf_np, name="cfd").ap()
    ctx.cid = nc.inline_tensor(ci_np, name="cid").ap()
    ctx.ones_d = nc.inline_tensor(np.ones((128, 2), np.float32), name="ones_d").ap()
    ctx.onesb_d = nc.inline_tensor(np.ones((128, 2), ml_dtypes.bfloat16),
                                   name="onesb_d").ap()
    ctx.onesr_d = nc.inline_tensor(np.ones((1, 128), np.float32), name="onesr_d").ap()

    e = E0
    def Es(name, shape, dt):
        nonlocal e
        e = (e + 31) & ~31
        t = T_(name, e, shape, dt)
        e += shape[1] * mybir.dt.size(dt)
        return t
    ctx.cft = Es("cft", [128, len(FVALS)], F32)
    ctx.cit = Es("cit", [128, len(IVALS)], I32)
    for v, i in FIDX.items():
        nc.const_aps.register(F32, v, ctx.cft[:, i:i + 1])
    ctx.ones_f = Es("ones_f", [128, 2], F32)
    ctx.ones_b = Es("ones_b", [128, 2], BF16)
    ctx.ones_r = Es("ones_r", [1, 128], F32)
    if mode in ("dequant", "fused"):
        ctx.dqsel = Es("dqsel_t", [128, 2], F32)
    if mode in ("attn", "fused", "ffn", "layer2"):
        ctx.wqkf = Es("wqkf", [128, 2048], F32)
        ctx.wqb = Es("wqb", [128, 2048], BF16)
        ctx.wkb = Es("wkb", [128, 2048], BF16)
        ctx.P01 = [Es(f"P01_{i}", [128, 1024], BF16) for i in range(2)]
        ctx.maskb = [Es(f"maskb{i}", [128, 512], BF16) for i in range(2)]
        ctx.crow = Es("crow", [1, 16], F32)
        ctx.lrows = Es("lrows", [1, 256], F32)
        ctx.lscr = Es("lscr", [1, 128], F32)
        ctx.lam = Es("lam", [1, 1], F32)
        ctx.mlam = Es("mlam", [1, 1], F32)
        ctx.mccol = Es("mccol", [128, 1], F32)
        ctx.osel = Es("osel_t", [128, 2], F32)
        ctx.wob = [ctx.wqb, ctx.wkb]
    ctx.e_end = e
    assert e <= 229376, f"E region overflow: {e}"

    if mode in ("attn", "fused", "ffn", "layer2"):
        ctx.xn_own = T_("xn_own", A0, [128, 16 * 1024], BF16)
        ctx.xn_rest = T_("xn_rest", A2, [128, 16 * 1024], BF16)
        ctx.murow = T_("murow", B0, [1, 2048], F32)
        ctx.isrow = T_("isrow", B0 + 8192, [1, 2048], F32)
        ctx.rtmp = T_("rtmp", B0 + 16384, [1, 2048], F32)
        ctx.V = T_("V", B0, [128, 16 * 1024], BF16)
        ctx.s_bf = T_("s_bf", B0, [128, 16 * 1024], BF16)
        ctx.gn_eff = T_("gn_eff", D0, [128, 16 * 1024], BF16)
        ctx.xbuf = [T_(f"xbuf{i}", C0 + i * 8192, [128, 2048], F32) for i in range(2)]
        ctx.x2 = T_("x2", C0 + 16384, [128, 2048], BF16)
        ctx.ttmp = T_("ttmp", C0 + 20480, [128, 2048], F32)
        ctx.wvf = [T_(f"wvf{i}", C0 + 28672 + i * 2048, [128, 512], F32)
                   for i in range(2)]
        ctx.wvb = T_("wvb", C0, [128, 16 * 512], BF16)
        ctx.gn_all = T_("gn_all", C0, [128, 16 * 1024], BF16)
        dd = B0 + 32 * 1024
        dd0 = dd
        def Dm(name, shape, dt):
            nonlocal dd
            dd = (dd + 31) & ~31
            t = T_(name, dd, shape, dt)
            dd += shape[1] * mybir.dt.size(dt)
            return t
        ctx.kt_off = (dd + 31) & ~31
        ctx.KT = Dm("KT", [128, 2048], BF16)
        ctx.QT = Dm("QT", [128, 2048], BF16)
        ctx.Q2 = Dm("Q2", [128, 2048], BF16)
        ctx.K2 = Dm("K2", [128, 2048], BF16)
        ctx.t1f = Dm("t1f", [128, 512], F32)
        ctx.opf = Dm("opf", [128, 512], F32)
        ctx.sqb = Dm("sqb", [128, 512], BF16)
        ctx.ou0 = Dm("ou0", [128, 512], F32)
        ctx.ou1 = Dm("ou1", [128, 512], F32)
        ctx.erow = Dm("erow", [1, 1024], F32)
        assert dd <= C0, f"B attn block overflow: {dd}"
        ctx.wof2 = T_("wof2", ctx.kt_off, [128, 2048], F32)
        ctx.wof = [ctx.wqkf, ctx.wof2]
    if mode in ("dequant", "fused"):
        d = D0
        def Ds(name, shape, dt):
            nonlocal d
            d = (d + 31) & ~31
            t = T_(name, d, shape, dt)
            d += shape[1] * mybir.dt.size(dt)
            return t
        ctx.dq_bytes = [Ds(f"dqb{i}", [128, 512], I32) for i in range(3)]
        ctx.dq_scale = [Ds(f"dqs{i}", [128, 512], F32) for i in range(3)]
        ctx.dq_val = [[Ds(f"dqv{i}{j}", [128, 512], BF16) for j in range(2)]
                      for i in range(3)]
        ctx.dq_ci = Ds("dqci", [128, 512], I32)
        ctx.dq_cf = Ds("dqcf", [128, 512], BF16)
        ctx.dq_cf2 = Ds("dqcf2", [128, 512], BF16)
        ctx.dq_sgn = Ds("dqsg", [128, 512], BF16)
        ctx.dq_t = Ds("dqt", [128, 512], BF16)
        ctx.dq_r1 = Ds("dqr1", [128, 512], BF16)
        ctx.dq_r2 = Ds("dqr2", [128, 512], BF16)
        ctx.dq_a = Ds("dqa", [128, 512], BF16)
        assert d <= D0 + 32 * 1024, f"dequant arena overflow {d - D0}"

    if mode in ("ffn", "fused", "layer2"):
        ctx.dacc_a = T_("dacc_a", A2, [128, 8 * 1024], F32)
        ctx.dacc_b = T_("dacc_b", B0 + 32 * 1024, [128, 8 * 1024], F32)
        ctx.inter = T_("inter", C0, [128, 16 * 1024], BF16)
        ctx.wgt = [T_(f"wgt{i}", D0 + 4096 * i, [128, 2048], BF16) for i in range(2)]
        ctx.wut = [T_(f"wut{i}", D0 + 8192 + 4096 * i, [128, 2048], BF16)
                   for i in range(2)]
        ctx.wdt = [T_(f"wdt{i}", D0 + 16384 + 4096 * i, [128, 2048], BF16)
                   for i in range(2)]
        ctx.ffn_gg = T_("ffn_gg", D0 + 24576, [128, 1024], BF16)
        ctx.ffn_ub = T_("ffn_ub", D0 + 26624, [128, 1024], BF16)
        ctx.gbuf = T_("gbuf", D0 + 16384, [128, 1024], BF16)
        ctx.sgtmp = T_("sgtmp", D0 + 24576, [128, 1024], F32)
        ctx.obuf = [T_("obuf0", ctx.wqkf.manual_sbuf_range[0], [128, 1024], F32),
                    T_("obuf1", ctx.wqkf.manual_sbuf_range[0] + 4096,
                       [128, 1024], F32)]
        ctx.xfbuf = [T_("xfbuf0", ctx.wqb.manual_sbuf_range[0], [128, 1024], F32),
                     T_("xfbuf1", ctx.wkb.manual_sbuf_range[0], [128, 1024], F32)]

    ctx.ps = nc.alloc_psum_tensor("ps", [128, 4096], F32)

    reset_sched_state(ctx)

    def xn_sl(kc, t0, w=512):
        if t0 < 1024:
            return ctx.xn_own[:, 1024 * kc + t0:1024 * kc + t0 + w]
        return ctx.xn_rest[:, 1024 * kc + (t0 - 1024):1024 * kc + (t0 - 1024) + w]
    ctx.xn_sl = xn_sl
    return ctx


def reset_sched_state(ctx):
    ctx.xs_pe_done = {}
    ctx.xs_act_done = {}
    ctx.x2_pe_done = 0
    ctx.xa_dve_done = {}
    ctx.bcast_cons = {}
    ctx.wv_cast_done = {}
    ctx.vp_evac_done = {}
    ctx.wqk_cast_done = 0
    ctx.kq_evac_done = 0
    ctx.cb_red_done = {}
    ctx.mask_cons = {}
    ctx.exp_done = {}
    ctx.wo_cast_done = {}
    ctx.op_evac_done = {}
    ctx.op_mm_done = {}
    ctx.av_done = 0
    ctx.dq_vec_done = 0
    ctx.dq_act_done = 0
    ctx.kq_mm_done = 0
    ctx.dq_state = {"k": 0, "oat": []}
    ctx.score_done = {}
    ctx.mask_done = {}


def sem_names_for(mode):
    names = ["consts"] + ["E_" + x for x in ENGS]
    if mode in ("dequant", "fused"):
        names += [f"dq_ld{i}" for i in range(3)] + [f"dq_out{i}" for i in range(3)]
    if mode in ("attn", "fused", "ffn", "layer2"):
        names += ["xs0", "xs1", "xa0", "xa1", "wv0", "wv1", "wqk",
                  "mk0", "mk1", "wo0", "wo1", "sout"]
    if mode in ("ffn", "fused", "layer2"):
        names += ["wg00", "wg01", "wg10", "wg11", "wd0", "wd1",
                  "xf0", "xf1", "xo0", "xo1", "sld"]
    return names


def emit_preamble(p, ctx):
    p.dma("sync", ctx.cft[:], ctx.cfd, "consts")
    p.dma("sync", ctx.cit[:], ctx.cid, "consts")
    p.dma("sync", ctx.ones_f[:], ctx.ones_d, "consts")
    p.dma("sync", ctx.ones_b[:], ctx.onesb_d, "consts")
    p.dma("sync", ctx.ones_r[:], ctx.onesr_d, "consts")
    if hasattr(ctx, "dqsel_d"):
        p.dma("sync", ctx.dqsel[:], ctx.dqsel_d, "consts")
    if hasattr(ctx, "lam_d"):
        p.dma("sync", ctx.lrows[:], ctx.lam_d, "consts")
        p.dma("sync", ctx.osel[:], ctx.osel_d, "consts")
    for e in ENGS[1:]:
        p.wait_dma(e, "consts")
    p.wait_dma("sync", "consts")


def emit_lambda(p, ctx):
    for i in (0, 1):
        q = ctx.lrows[:, 64 * (2 * i):64 * (2 * i) + 64]
        kk = ctx.lrows[:, 64 * (2 * i + 1):64 * (2 * i + 1) + 64]
        p.op("vector", lambda e, q=q, kk=kk: e.tensor_mul(ctx.lscr[:, 0:64], q, kk))
        p.op("vector", lambda e, i=i: e.reduce_sum(
            ctx.lscr[:, 64 + i:65 + i], ctx.lscr[:, 0:64],
            axis=mybir.AxisListType.X))
    p.wait_eng("scalar", "vector")
    p.op("scalar", lambda e: e.activation(
        ctx.lscr[:, 66:68], ctx.lscr[:, 64:66], AF.Exp))
    p.wait_eng("vector", "scalar")
    p.op("vector", lambda e: e.tensor_sub(
        ctx.lam[:, 0:1], ctx.lscr[:, 66:67], ctx.lscr[:, 67:68]))
    p.op("vector", lambda e: e.tensor_scalar(
        ctx.lam[:, 0:1], ctx.lam[:, 0:1], CF1(ctx, LAMBDA_INIT), None, ALU.add))
    p.op("vector", lambda e: e.tensor_scalar(
        ctx.mlam[:, 0:1], ctx.lam[:, 0:1], CF1(ctx, -1.0), None, ALU.mult))


def emit_ln(p, ctx):
    PS = ctx.ps
    for kc in range(KC):
        b = kc % 2
        if kc >= 2:
            p.wait_eng("sync", "tensor", at=ctx.xs_pe_done[kc - 2])
            p.wait_eng("sync", "scalar", at=ctx.xs_act_done[kc - 2])
        p.dma("sync", ctx.xbuf[b][:], ctx.xfm[128 * kc:128 * kc + 128, :], f"xs{b}")
        p.wait_dma("scalar", f"xs{b}")
        p.wait_eng("scalar", "tensor", at=ctx.x2_pe_done)
        p.op("scalar", lambda e, b=b: e.activation(ctx.x2[:, :], ctx.xbuf[b][:],
                                                   AF.Square))
        ctx.xs_act_done[kc] = p.ecnt["scalar"]
        p.wait_dma("tensor", f"xs{b}")
        for tc in range(4):
            sl = slice(512 * tc, 512 * tc + 512)
            p.op("tensor", lambda e, b=b, sl=sl, tc=tc, kc=kc: e.matmul(
                PS[0:1, 512 * tc:512 * tc + 512], ctx.ones_f[:, 0:1],
                ctx.xbuf[b][:, sl], start=(kc == 0), stop=(kc == KC - 1)),
                inc=(tc == 3))
        p.wait_eng("tensor", "scalar")
        for tc in range(4):
            sl = slice(512 * tc, 512 * tc + 512)
            p.op("tensor", lambda e, sl=sl, tc=tc, kc=kc: e.matmul(
                PS[0:1, 2048 + 512 * tc:2048 + 512 * tc + 512], ctx.ones_b[:, 0:1],
                ctx.x2[:, sl], start=(kc == 0), stop=(kc == KC - 1)),
                inc=(tc == 3))
        ctx.xs_pe_done[kc] = p.ecnt["tensor"]
        ctx.x2_pe_done = p.ecnt["tensor"]
    p.wait_eng("vector", "tensor")
    p.op("vector", lambda e: e.tensor_scalar(
        ctx.murow[:, :], PS[0:1, 0:2048], CF1(ctx, 1.0 / D), None, ALU.mult))
    p.op("vector", lambda e: e.tensor_scalar(
        ctx.isrow[:, :], PS[0:1, 2048:4096], CF1(ctx, 1.0 / D), None, ALU.mult))
    p.op("vector", lambda e: e.tensor_mul(ctx.rtmp[:, :], ctx.murow[:, :],
                                          ctx.murow[:, :]))
    p.op("vector", lambda e: e.tensor_sub(ctx.isrow[:, :], ctx.isrow[:, :],
                                          ctx.rtmp[:, :]))
    p.op("vector", lambda e: e.tensor_scalar(
        ctx.isrow[:, :], ctx.isrow[:, :], CF1(ctx, LN_EPS), None, ALU.add))
    p.op("vector", lambda e: e.reciprocal(ctx.isrow[:, :], ctx.isrow[:, :]))
    p.wait_eng("scalar", "vector")
    p.op("scalar", lambda e: e.activation(ctx.isrow[:, :], ctx.isrow[:, :], AF.Sqrt))
    stats_t = p.ecnt["tensor"]
    stats_s = p.ecnt["scalar"]
    for kc in range(KC):
        b = kc % 2
        if kc >= 2:
            p.wait_eng("sync", "vector", at=ctx.xa_dve_done[kc - 2])
        else:
            p.wait_eng("sync", "tensor", at=stats_t)
            p.wait_eng("sync", "scalar", at=stats_s)
        p.dma("sync", ctx.xbuf[b][:], ctx.xfm[128 * kc:128 * kc + 128, :], f"xa{b}")
        p.wait_dma("tensor", f"xa{b}")
        p.wait_eng("tensor", "scalar")
        for tc in range(4):
            bb = 2 * (tc % 2)
            p.wait_eng("tensor", "vector", at=ctx.bcast_cons.get(bb, 0))
            p.op("tensor", lambda e, tc=tc, bb=bb: e.matmul(
                PS[:, 512 * bb:512 * bb + 512], ctx.ones_r[0:1, :],
                ctx.murow[0:1, 512 * tc:512 * tc + 512], start=True, stop=True))
            p.op("tensor", lambda e, tc=tc, bb=bb: e.matmul(
                PS[:, 512 * (bb + 1):512 * (bb + 1) + 512], ctx.ones_r[0:1, :],
                ctx.isrow[0:1, 512 * tc:512 * tc + 512], start=True, stop=True))
            p.wait_eng("vector", "tensor")
            sl = slice(512 * tc, 512 * tc + 512)
            xnsl = ctx.xn_sl(kc, 512 * tc)
            p.op("vector", lambda e, b=b, sl=sl, bb=bb: e.tensor_sub(
                ctx.ttmp[:, 0:512], ctx.xbuf[b][:, sl], PS[:, 512 * bb:512 * bb + 512]))
            p.op("vector", lambda e, xnsl=xnsl, bb=bb: e.tensor_mul(
                xnsl, ctx.ttmp[:, 0:512], PS[:, 512 * (bb + 1):512 * (bb + 1) + 512]))
            ctx.bcast_cons[bb] = p.ecnt["vector"]
        ctx.xa_dve_done[kc] = p.ecnt["vector"]


def emit_vproj(p, ctx, hook=None):
    PS = ctx.ps
    vq_pe_done = 0
    casts = []
    for quarter in range(2):
        vc0 = 512 * quarter
        for kc in range(KC):
            b = kc % 2
            gi = 16 * quarter + kc
            if gi >= 2:
                p.wait_eng("sync", "vector", at=casts[gi - 2])
            src = ctx.wv.rearrange("(kc pp) a -> pp kc a", pp=128)[:, kc, vc0:vc0 + 512]
            p.dma("sync", ctx.wvf[b][:, 0:512], src, f"wv{b}")
            p.wait_dma("vector", f"wv{b}")
            if quarter > 0:
                p.wait_eng("vector", "tensor", at=vq_pe_done)
            p.op("vector", lambda e, b=b, kc=kc: e.tensor_copy(
                ctx.wvb[:, 512 * kc:512 * kc + 512], ctx.wvf[b][:, 0:512]))
            casts.append(p.ecnt["vector"])
        p.wait_eng("tensor", "vector")
        for tt in range(16):
            bank = 4 + (tt % 2)
            p.wait_eng("tensor", "scalar", at=ctx.vp_evac_done.get(bank, 0))
            for kc in range(KC):
                stat = ctx.xn_sl(kc, 128 * tt, w=128)
                p.op("tensor", lambda e, stat=stat, kc=kc, bank=bank: e.matmul(
                    PS[:, 512 * bank:512 * bank + 512], stat,
                    ctx.wvb[:, 512 * kc:512 * kc + 512],
                    start=(kc == 0), stop=(kc == KC - 1)), inc=(kc == KC - 1))
            p.wait_eng("scalar", "tensor")
            p.op("scalar", lambda e, tt=tt, vc0=vc0, bank=bank: e.activation(
                ctx.V[:, 1024 * tt + vc0:1024 * tt + vc0 + 512],
                PS[:, 512 * bank:512 * bank + 512], AF.Copy))
            ctx.vp_evac_done[bank] = p.ecnt["scalar"]
        vq_pe_done = p.ecnt["tensor"]
        if hook is not None:
            hook(quarter)


def emit_kqproj(p, ctx, hp):
    PS = ctx.ps
    for wi, (wdram, wbt) in enumerate([(ctx.wq, ctx.wqb), (ctx.wk, ctx.wkb)]):
        p.wait_eng("sync", "vector", at=ctx.wqk_cast_done)
        src = wdram.rearrange("(kc pp) a -> pp kc a", pp=128)[:, :, 128 * hp:128 * hp + 128]
        p.dma("sync", ctx.wqkf[:].rearrange("pp (kc c) -> pp kc c", c=128), src, "wqk")
        p.wait_dma("vector", "wqk")
        p.wait_eng("vector", "tensor", at=ctx.kq_mm_done)
        p.op("vector", lambda e, wbt=wbt: e.tensor_copy(wbt[:], ctx.wqkf[:]))
        ctx.wqk_cast_done = p.ecnt["vector"]
    p.wait_eng("tensor", "vector", at=ctx.wqk_cast_done)
    p.wait_eng("tensor", "scalar")
    for dst, wbt, nch, scale, bank in [(ctx.KT, ctx.wkb, 4, None, 0),
                                       (ctx.QT, ctx.wqb, 4, 0.125, 1)]:
        for tch in range(nch):
            p.wait_eng("tensor", "scalar", at=ctx.kq_evac_done)
            for kc in range(KC):
                stat = wbt[:, 128 * kc:128 * kc + 128]
                mov = ctx.xn_sl(kc, 512 * tch, w=512)
                p.op("tensor", lambda e, stat=stat, mov=mov, kc=kc, bank=bank: e.matmul(
                    PS[:, 512 * bank:512 * bank + 512], stat, mov,
                    start=(kc == 0), stop=(kc == KC - 1)), inc=(kc == KC - 1))
            p.wait_eng("scalar", "tensor")
            if scale is None:
                p.op("scalar", lambda e, dst=dst, tch=tch, bank=bank: e.activation(
                    dst[:, 512 * tch:512 * tch + 512],
                    PS[:, 512 * bank:512 * bank + 512], AF.Copy))
            else:
                p.op("scalar", lambda e, dst=dst, tch=tch, bank=bank, scale=scale:
                     e.activation(dst[:, 512 * tch:512 * tch + 512],
                                  PS[:, 512 * bank:512 * bank + 512], AF.Copy,
                                  scale=scale))
            ctx.kq_evac_done = p.ecnt["scalar"]
    ctx.kq_mm_done = p.ecnt["tensor"]


def emit_cbound(p, ctx, hp):
    PS = ctx.ps
    p.op("scalar", lambda e: e.activation(ctx.Q2[:], ctx.QT[:], AF.Square))
    p.op("scalar", lambda e: e.activation(ctx.K2[:], ctx.KT[:], AF.Square))
    p.wait_eng("tensor", "scalar")
    nm = 0
    for src, nch in [(ctx.Q2, 4), (ctx.K2, 4)]:
        for i in range(nch):
            bank = 4 if nm % 2 == 0 else 5
            p.wait_eng("tensor", "vector", at=ctx.cb_red_done.get(bank, 0))
            p.op("tensor", lambda e, src=src, i=i, bank=bank: e.matmul(
                PS[0:1, 512 * bank:512 * bank + 512], ctx.ones_b[:, 0:1],
                src[:, 512 * i:512 * i + 512], start=True, stop=True))
            p.wait_eng("vector", "tensor")
            p.op("vector", lambda e, nm=nm, bank=bank: e.reduce_max(
                ctx.crow[:, nm:nm + 1], PS[0:1, 512 * bank:512 * bank + 512],
                axis=mybir.AxisListType.X))
            ctx.cb_red_done[bank] = p.ecnt["vector"]
            nm += 1
    p.op("vector", lambda e: e.reduce_max(ctx.crow[:, 8:9], ctx.crow[:, 0:4],
                                          axis=mybir.AxisListType.X))
    p.op("vector", lambda e: e.reduce_max(ctx.crow[:, 9:10], ctx.crow[:, 4:8],
                                          axis=mybir.AxisListType.X))
    p.op("vector", lambda e: e.tensor_mul(ctx.crow[:, 8:9], ctx.crow[:, 8:9],
                                          ctx.crow[:, 9:10]))
    p.wait_eng("scalar", "vector")
    p.op("scalar", lambda e: e.activation(ctx.crow[:, 10:11], ctx.crow[:, 8:9],
                                          AF.Sqrt))
    p.wait_eng("vector", "scalar")
    p.op("vector", lambda e: e.tensor_scalar(
        ctx.crow[:, 10:11], ctx.crow[:, 10:11], CF1(ctx, -1.0), None, ALU.mult))
    p.wait_eng("tensor", "vector")
    p.op("tensor", lambda e: e.matmul(
        PS[:, 512 * 4:512 * 4 + 1], ctx.ones_r[0:1, :], ctx.crow[0:1, 10:11],
        start=True, stop=True))
    p.wait_eng("scalar", "tensor")
    p.op("scalar", lambda e: e.activation(ctx.mccol[:, 0:1],
                                          PS[:, 512 * 4:512 * 4 + 1], AF.Copy))


def emit_attn_block(p, ctx, hp, blk):
    PS = ctx.ps
    q0 = 512 * blk

    def emit_scores(tk):
        it = 16 * blk + tk
        mb = it % 2
        sb = 2 * (tk % 2)
        p.wait_eng("sync", "vector", at=ctx.mask_cons.get(mb, 0))
        p.dma("sync", ctx.maskb[mb][:], ctx.masks_d[it], f"mk{mb}")
        p.wait_eng("tensor", "scalar", at=ctx.exp_done.get(tk % 2, 0))
        p.op("tensor", lambda e, tk=tk, q0=q0, sb=sb: e.matmul(
            PS[:, 512 * sb:512 * sb + 512],
            ctx.KT[0:64, 128 * tk:128 * tk + 128],
            ctx.QT[0:64, q0:q0 + 512], start=True, stop=True))
        p.op("tensor", lambda e, tk=tk, q0=q0, sb=sb: e.matmul(
            PS[:, 512 * (sb + 1):512 * (sb + 1) + 512],
            ctx.KT[64:128, 128 * tk:128 * tk + 128],
            ctx.QT[64:128, q0:q0 + 512], start=True, stop=True,
            tile_position=(64, 0)))
        ctx.score_done[tk] = p.ecnt["tensor"]
        pb = ctx.P01[it % 2]
        p.wait_eng("scalar", "tensor", at=ctx.score_done[tk])
        p.op("scalar", lambda e, pb=pb, sb=sb: e.activation(
            pb[:, 0:512], PS[:, 512 * sb:512 * sb + 512], AF.Exp,
            bias=ctx.mccol[:, 0:1]))
        p.op("scalar", lambda e, pb=pb, sb=sb: e.activation(
            pb[:, 512:1024], PS[:, 512 * (sb + 1):512 * (sb + 1) + 512], AF.Exp,
            bias=ctx.mccol[:, 0:1]))
        ctx.exp_done[tk % 2] = p.ecnt["scalar"]
        p.wait_dma("vector", f"mk{mb}")
        p.wait_eng("vector", "scalar")
        p.op("vector", lambda e, pb=pb, mb=mb: e.tensor_mul(
            pb[:, 0:512], pb[:, 0:512], ctx.maskb[mb][:]))
        p.op("vector", lambda e, pb=pb, mb=mb: e.tensor_mul(
            pb[:, 512:1024], pb[:, 512:1024], ctx.maskb[mb][:]))
        ctx.mask_cons[mb] = p.ecnt["vector"]
        ctx.mask_done[tk] = p.ecnt["vector"]

    def emit_zav(tk):
        it = 16 * blk + tk
        pb = ctx.P01[it % 2]
        p.wait_eng("tensor", "vector", at=ctx.mask_done[tk])
        p.op("tensor", lambda e, pb=pb, tk=tk: e.matmul(
            PS[0:1, 512 * 6:512 * 6 + 512], ctx.ones_b[:, 0:1], pb[:, 0:512],
            start=(tk == 0), stop=(tk == 15)))
        p.op("tensor", lambda e, pb=pb, tk=tk: e.matmul(
            PS[0:1, 512 * 7:512 * 7 + 512], ctx.ones_b[:, 0:1], pb[:, 512:1024],
            start=(tk == 0), stop=(tk == 15)))
        vstat = ctx.V[:, 1024 * tk + 128 * hp:1024 * tk + 128 * hp + 128]
        p.op("tensor", lambda e, pb=pb, vstat=vstat, tk=tk: e.matmul(
            PS[:, 512 * 4:512 * 4 + 512], vstat, pb[:, 0:512],
            start=(tk == 0), stop=(tk == 15)))
        p.op("tensor", lambda e, pb=pb, vstat=vstat, tk=tk: e.matmul(
            PS[:, 512 * 5:512 * 5 + 512], vstat, pb[:, 512:1024],
            start=(tk == 0), stop=(tk == 15)))
        ctx.av_done = p.ecnt["tensor"]

    for tk in range(16):
        emit_scores(tk)
        if tk > 0:
            emit_zav(tk - 1)
    emit_zav(15)

    p.wait_eng("vector", "tensor", at=ctx.av_done)
    p.op("vector", lambda e: e.reciprocal(ctx.erow[:, 0:512],
                                          PS[0:1, 512 * 6:512 * 6 + 512]))
    p.op("vector", lambda e: e.reciprocal(ctx.erow[:, 512:1024],
                                          PS[0:1, 512 * 7:512 * 7 + 512]))
    p.op("vector", lambda e: e.tensor_scalar(
        ctx.erow[:, 512:1024], ctx.erow[:, 512:1024], ctx.lam[0:1, 0:1], None,
        ALU.mult))
    p.wait_eng("tensor", "vector")
    p.op("tensor", lambda e: e.matmul(
        PS[:, 512 * 6:512 * 6 + 512], ctx.ones_r[0:1, :], ctx.erow[0:1, 0:512],
        start=True, stop=True))
    p.op("tensor", lambda e: e.matmul(
        PS[:, 512 * 7:512 * 7 + 512], ctx.ones_r[0:1, :], ctx.erow[0:1, 512:1024],
        start=True, stop=True))
    p.wait_eng("scalar", "tensor")
    p.op("scalar", lambda e: e.activation(ctx.ou0[:], PS[:, 512 * 4:512 * 4 + 512],
                                          AF.Copy))
    p.op("scalar", lambda e: e.activation(ctx.ou1[:], PS[:, 512 * 5:512 * 5 + 512],
                                          AF.Copy))
    p.wait_eng("vector", "scalar")
    p.op("vector", lambda e: e.tensor_mul(
        ctx.t1f[:], ctx.ou1[:], PS[:, 512 * 7:512 * 7 + 512]))
    p.op("vector", lambda e: e.tensor_mul(
        ctx.opf[:], ctx.ou0[:], PS[:, 512 * 6:512 * 6 + 512]))
    p.op("vector", lambda e: e.tensor_sub(
        ctx.opf[:], ctx.opf[:], ctx.t1f[:]))
    p.wait_eng("scalar", "vector")
    p.op("scalar", lambda e: e.activation(ctx.sqb[:], ctx.opf[:], AF.Square))
    p.wait_eng("tensor", "scalar")
    p.op("tensor", lambda e: e.matmul(
        PS[0:1, 512 * 6:512 * 6 + 512], ctx.ones_f[:, 0:1], ctx.opf[:],
        start=True, stop=True))
    p.op("tensor", lambda e: e.matmul(
        PS[0:1, 512 * 7:512 * 7 + 512], ctx.ones_b[:, 0:1], ctx.sqb[:],
        start=True, stop=True))
    p.wait_eng("vector", "tensor")
    p.op("vector", lambda e: e.tensor_scalar(
        ctx.erow[:, 0:512], PS[0:1, 512 * 6:512 * 6 + 512],
        CF1(ctx, 1.0 / 128.0), None, ALU.mult))
    p.op("vector", lambda e: e.tensor_mul(ctx.erow[:, 512:1024],
                                          ctx.erow[:, 0:512], ctx.erow[:, 0:512]))
    p.op("vector", lambda e: e.scalar_tensor_tensor(
        ctx.erow[:, 512:1024], PS[0:1, 512 * 7:512 * 7 + 512],
        CF1(ctx, 1.0 / 128.0), ctx.erow[:, 512:1024], ALU.mult, ALU.subtract))
    p.op("vector", lambda e: e.tensor_scalar(
        ctx.erow[:, 512:1024], ctx.erow[:, 512:1024], CF1(ctx, GN_EPS), None,
        ALU.add))
    p.op("vector", lambda e: e.reciprocal(ctx.erow[:, 512:1024],
                                          ctx.erow[:, 512:1024]))
    p.op("vector", lambda e: e.tensor_scalar(
        ctx.erow[:, 512:1024], ctx.erow[:, 512:1024],
        CF1(ctx, (1.0 - LAMBDA_INIT) ** 2), None, ALU.mult))
    p.wait_eng("scalar", "vector")
    p.op("scalar", lambda e: e.activation(ctx.erow[:, 512:1024],
                                          ctx.erow[:, 512:1024], AF.Sqrt))
    p.wait_eng("vector", "scalar")
    p.op("vector", lambda e: e.tensor_mul(ctx.erow[:, 0:512], ctx.erow[:, 0:512],
                                          ctx.erow[:, 512:1024]))
    p.wait_eng("tensor", "vector")
    p.op("tensor", lambda e: e.matmul(
        PS[:, 512 * 6:512 * 6 + 512], ctx.ones_r[0:1, :], ctx.erow[0:1, 512:1024],
        start=True, stop=True))
    p.op("tensor", lambda e: e.matmul(
        PS[:, 512 * 7:512 * 7 + 512], ctx.ones_r[0:1, :], ctx.erow[0:1, 0:512],
        start=True, stop=True))
    p.wait_eng("vector", "tensor")
    gsl = ctx.gn_all[:, 2048 * hp + 512 * blk:2048 * hp + 512 * blk + 512]
    p.op("vector", lambda e: e.tensor_mul(
        ctx.t1f[:], ctx.opf[:], PS[:, 512 * 6:512 * 6 + 512]))
    p.op("vector", lambda e, gsl=gsl: e.tensor_sub(
        gsl, ctx.t1f[:], PS[:, 512 * 7:512 * 7 + 512]))


def emit_oproj(p, ctx):
    PS = ctx.ps
    p.wait_eng("sync", "tensor")
    p.wait_eng("sync", "vector")
    p.wait_eng("sync", "scalar")
    p.wait_eng("vector", "tensor")
    for i in range(3):
        if f"dq_out{i}" in p.dcnt:
            p.wait_dma("vector", f"dq_out{i}")
    for hi in range(8):
        base = 2048 * hi
        for half in range(2):
            a = ctx.gn_all[:, base + 1024 * half:base + 1024 * half + 1024]
            bb_ = ctx.gn_all[:, base + 1024 * (1 - half):base + 1024 * (1 - half) + 1024]
            dst = ctx.gn_eff[:, base + 1024 * half:base + 1024 * half + 1024]
            p.op("vector", lambda e, bb_=bb_: e.tensor_single_scalar(
                ctx.P01[0][:], bb_, ctx.osel[:, 1:2], ALU.mult))
            p.op("vector", lambda e, a=a, dst=dst: e.scalar_tensor_tensor(
                dst, a, ctx.osel[:, 0:1], ctx.P01[0][:], ALU.mult, ALU.add))
    for dt in range(16):
        b = dt % 2
        p.wait_eng("sync", "vector", at=ctx.wo_cast_done.get(b, 0))
        src = ctx.wo.rearrange("(ac pp) d -> pp ac d", pp=128)[:, :, 128 * dt:128 * dt + 128]
        p.dma("sync", ctx.wof[b][:].rearrange("pp (ac c) -> pp ac c", c=128), src,
              f"wo{b}")
        p.wait_dma("vector", f"wo{b}")
        p.wait_eng("vector", "tensor", at=ctx.op_mm_done.get(b, 0))
        p.op("vector", lambda e, b=b: e.tensor_copy(ctx.wob[b][:], ctx.wof[b][:]))
        ctx.wo_cast_done[b] = p.ecnt["vector"]
        p.wait_eng("tensor", "vector")
        p.wait_eng("tensor", "scalar", at=ctx.op_evac_done.get(0, 0))
        for kc in range(KC):
            stat = ctx.wob[b][:, 128 * kc:128 * kc + 128]
            for hi in range(8):
                mov = ctx.gn_eff[:, 2048 * hi:2048 * hi + 2048].rearrange(
                    "p (r j) -> p j r", j=16)[:, kc, :]
                p.op("tensor", lambda e, stat=stat, mov=mov, hi=hi, kc=kc: e.matmul(
                    PS[:, 512 * hi:512 * hi + 128], stat, mov,
                    start=(kc == 0), stop=(kc == KC - 1)),
                    inc=(kc == KC - 1 and hi == 7))
        p.wait_eng("scalar", "tensor")
        for hi in range(8):
            p.op("scalar", lambda e, dt=dt, hi=hi: e.activation(
                ctx.s_bf[:, 1024 * dt + 128 * hi:1024 * dt + 128 * hi + 128],
                PS[:, 512 * hi:512 * hi + 128], AF.Copy))
        ctx.op_evac_done[0] = p.ecnt["scalar"]
        ctx.op_mm_done[b] = p.ecnt["tensor"]


def emit_ffn(p, ctx):
    PS = ctx.ps
    p.wait_eng("vector", "tensor")
    p.wait_eng("sync", "tensor")
    p.wait_eng("sync", "scalar")
    p.wait_eng("sync", "vector")
    for i in range(3):
        if f"dq_out{i}" in p.dcnt:
            p.wait_dma("sync", f"dq_out{i}")
    p.op("vector", lambda e: e.memset(ctx.dacc_a[:], 0.0))
    p.op("vector", lambda e: e.memset(ctx.dacc_b[:], 0.0))
    gu_mm_done = {}
    dn_mm_done = {}
    ld_done = {}
    for b5 in range(8):
        wsrc = ctx.ccw[b5]
        for l in range(16):
            wb = l % 2
            for mi, wt in [(0, ctx.wgt[wb]), (1, ctx.wut[wb])]:
                p.wait_eng("sync", "tensor", at=gu_mm_done.get((wb, mi), 0))
                src = wsrc[mi].rearrange("(kc pp) n -> pp kc n", pp=128)[:, :, 128 * l:128 * l + 128]
                p.dma("sync", wt[:].rearrange("pp (kc c) -> pp kc c", c=128), src,
                      f"wg{wb}{mi}")
            p.wait_dma("tensor", f"wg{wb}0")
            p.wait_dma("tensor", f"wg{wb}1")
            p.wait_eng("tensor", "scalar", at=ld_done.get("gelu", 0))
            for kc in range(KC):
                stat = ctx.wgt[wb][:, 128 * kc:128 * kc + 128]
                for tqc in range(2):
                    p.op("tensor", lambda e, stat=stat, kc=kc, tqc=tqc: e.matmul(
                        PS[:, 512 * tqc:512 * tqc + 512], stat,
                        ctx.xn_own[:, 1024 * kc + 512 * tqc:1024 * kc + 512 * tqc + 512],
                        start=(kc == 0), stop=(kc == KC - 1)),
                        inc=(kc == KC - 1 and tqc == 1))
            p.wait_eng("tensor", "scalar", at=ld_done.get("up", 0))
            for kc in range(KC):
                stat = ctx.wut[wb][:, 128 * kc:128 * kc + 128]
                for tqc in range(2):
                    p.op("tensor", lambda e, stat=stat, kc=kc, tqc=tqc: e.matmul(
                        PS[:, 1024 + 512 * tqc:1024 + 512 * tqc + 512], stat,
                        ctx.xn_own[:, 1024 * kc + 512 * tqc:1024 * kc + 512 * tqc + 512],
                        start=(kc == 0), stop=(kc == KC - 1)),
                        inc=(kc == KC - 1 and tqc == 1))
            gu_mm_done[(wb, 0)] = gu_mm_done[(wb, 1)] = p.ecnt["tensor"]
            p.wait_eng("scalar", "tensor")
            p.wait_eng("scalar", "vector", at=ld_done.get("inter", 0))
            p.op("scalar", lambda e: e.activation(ctx.ffn_gg[:], PS[:, 0:1024],
                                                  AF.Gelu_apprx_tanh))
            ld_done["gelu"] = p.ecnt["scalar"]
            p.op("scalar", lambda e: e.activation(ctx.ffn_ub[:], PS[:, 1024:2048],
                                                  AF.Copy))
            ld_done["up"] = p.ecnt["scalar"]
            p.wait_eng("vector", "scalar")
            p.op("vector", lambda e, l=l: e.tensor_mul(
                ctx.inter[:, 1024 * l:1024 * l + 1024], ctx.ffn_gg[:], ctx.ffn_ub[:]))
            ld_done["inter"] = p.ecnt["vector"]
        inter_done = p.ecnt["vector"]
        for ht in range(16):
            wb = ht % 2
            p.wait_eng("sync", "tensor", at=dn_mm_done.get(wb, 0))
            src = wsrc[2].rearrange("(nc pp) hh -> pp nc hh", pp=128)[:, :, 128 * ht:128 * ht + 128]
            p.dma("sync", ctx.wdt[wb][:].rearrange("pp (nc c) -> pp nc c", c=128),
                  src, f"wd{wb}")
            p.wait_dma("tensor", f"wd{wb}")
            if ht == 0:
                p.wait_eng("tensor", "vector", at=inter_done)
            p.wait_eng("tensor", "vector", at=ld_done.get("dacc", 0))
            for nc_ in range(KC):
                stat = ctx.wdt[wb][:, 128 * nc_:128 * nc_ + 128]
                for tqc in range(2):
                    p.op("tensor", lambda e, stat=stat, nc_=nc_, tqc=tqc: e.matmul(
                        PS[:, 2048 + 512 * tqc:2048 + 512 * tqc + 512], stat,
                        ctx.inter[:, 1024 * nc_ + 512 * tqc:1024 * nc_ + 512 * tqc + 512],
                        start=(nc_ == 0), stop=(nc_ == KC - 1)),
                        inc=(nc_ == KC - 1 and tqc == 1))
            dn_mm_done[wb] = p.ecnt["tensor"]
            p.wait_eng("vector", "tensor")
            dsl = ctx.dacc_a[:, 1024 * ht:1024 * ht + 1024] if ht < 8 else \
                ctx.dacc_b[:, 1024 * (ht - 8):1024 * (ht - 8) + 1024]
            p.op("vector", lambda e, dsl=dsl: e.tensor_add(dsl, dsl,
                                                           PS[:, 2048:3072]))
            ld_done["dacc"] = p.ecnt["vector"]
    oadds = []
    for ht in range(16):
        b = ht % 2
        dsl = ctx.dacc_a[:, 1024 * ht:1024 * ht + 1024] if ht < 8 else \
            ctx.dacc_b[:, 1024 * (ht - 8):1024 * (ht - 8) + 1024]
        p.wait_eng("scalar", "vector")
        p.op("scalar", lambda e, dsl=dsl: e.activation(ctx.gbuf[:], dsl, AF.Silu))
        if ht >= 2:
            p.wait_eng("sync", "vector", at=oadds[ht - 2])
        else:
            p.wait_eng("sync", "tensor")
        p.dma("sync", ctx.xfbuf[b][:], ctx.xfm[128 * ht:128 * ht + 128, 0:1024],
              f"xf{b}")
        p.wait_eng("vector", "scalar")
        p.op("vector", lambda e, ht=ht: e.tensor_mul(
            ctx.sgtmp[:], ctx.s_bf[:, 1024 * ht:1024 * ht + 1024], ctx.gbuf[:]))
        p.wait_dma("vector", f"xf{b}")
        if ht >= 2:
            p.wait_dma("vector", f"xo{b}")
        p.op("vector", lambda e, b=b: e.tensor_add(
            ctx.obuf[b][:], ctx.xfbuf[b][:], ctx.sgtmp[:]))
        oadds.append(p.ecnt["vector"])
        p.wait_eng("scalar", "vector")
        p.dma("scalar", ctx.out_fm[128 * ht:128 * ht + 128, :], ctx.obuf[b][:],
              f"xo{b}")
    p.wait_dma("gpsimd", "xo0")
    p.wait_dma("gpsimd", "xo1")


def emit_attention(p, ctx):
    emit_lambda(p, ctx)
    emit_ln(p, ctx)
    emit_vproj(p, ctx)
    for hp in range(8):
        emit_kqproj(p, ctx, hp)
        emit_cbound(p, ctx, hp)
        for blk in range(4):
            emit_attn_block(p, ctx, hp, blk)
    emit_oproj(p, ctx)


def build_program(mode):
    nc = bass.Bass("TRN2", target_bir_lowering=False,
                   dynamic_dma_scratch_size=4096)
    ctx = make_ctx(nc, mode)
    if mode == "dequant":
        ctx.cc_in = nc.dram_tensor("cc", [3 * T, 2048], BF16, kind="ExternalOutput").ap()
    if mode in ("ffn", "fused", "layer2"):
        ctx.out_fm = nc.dram_tensor("out_fm", [2048, 1024], F32,
                                    kind="ExternalOutput").ap()
    if mode == "fused":
        ctx.dbg_s = nc.dram_tensor("dbg_s", [128, 16 * 1024], BF16,
                                   kind="ExternalOutput").ap()
        ctx.dbg_xn = nc.dram_tensor("dbg_xn", [128, 16 * 1024], BF16,
                                    kind="ExternalOutput").ap()
        ctx.dbg_gn = nc.dram_tensor("dbg_gn", [128, 16 * 1024], BF16,
                                    kind="ExternalOutput").ap()
        ctx.dbg_cc = nc.dram_tensor("dbg_cc", [24, 128, 2048], BF16,
                                    kind="ExternalOutput").ap()
        ctx.dbg_da = nc.dram_tensor("dbg_da", [128, 8 * 1024], F32,
                                    kind="ExternalOutput").ap()
    if mode == "attn":
        ctx.s_out = nc.dram_tensor("s_out", [2048, 1024], BF16, kind="ExternalOutput").ap()
        ctx.xn_out = nc.dram_tensor("xn_out", [2048, 2048], BF16, kind="ExternalOutput").ap()
        ctx.dbg_gn = nc.dram_tensor("dbg_gn", [2048, 1024], BF16, kind="ExternalOutput").ap()
        ctx.dbg_kt = nc.dram_tensor("dbg_kt", [128, 2048], BF16, kind="ExternalOutput").ap()
        ctx.dbg_v = nc.dram_tensor("dbg_v", [128, 2048], BF16, kind="ExternalOutput").ap()
        ctx.dbg_rows = nc.dram_tensor("dbg_rows", [128, 16], F32, kind="ExternalOutput").ap()
    sems = {n: nc.alloc_semaphore(n) for n in sem_names_for(mode)}

    with nc.Block() as block:
        def emit_all(p):
            reset_sched_state(ctx)
            emit_preamble(p, ctx)
            if mode == "dequant":
                emit_dequant(p, ctx)
                emit_dequant_wait_done(p, ctx)
            elif mode == "ffn":
                emit_ln(p, ctx)
                p.wait_eng("sync", "tensor")
                p.wait_eng("sync", "vector")
                for dt in range(16):
                    p.dma("sync", ctx.s_bf[:, 1024 * dt:1024 * dt + 1024],
                          ctx.s_in[128 * dt:128 * dt + 128, :], "sld")
                p.wait_dma("vector", "sld")
                emit_ffn(p, ctx)
            elif mode == "layer2":
                emit_attention(p, ctx)
                emit_ffn(p, ctx)
            elif mode == "fused":
                emit_lambda(p, ctx)
                emit_ln(p, ctx)
                emit_vproj(p, ctx, hook=lambda q: emit_dq_tiles(p, ctx, 20))
                for hp in range(8):
                    emit_kqproj(p, ctx, hp)
                    emit_cbound(p, ctx, hp)
                    for blk in range(4):
                        emit_attn_block(p, ctx, hp, blk)
                        emit_dq_tiles(p, ctx, 6)
                emit_dq_tiles(p, ctx, len(DQ_TILES))
                emit_dequant_wait_done(p, ctx)
                p.op("gpsimd", lambda e: e.collective_compute(
                    "AllGather", mybir.AluOpType.bypass,
                    replica_groups=[[0, 1, 2, 3, 4, 5, 6, 7]],
                    ins=[ctx.cc_in[:]], outs=[ctx.cc_out[:]]))
                emit_oproj(p, ctx)
                p.wait_eng("sync", "gpsimd")
                emit_ffn(p, ctx)
            elif mode == "attn":
                emit_attention(p, ctx)
                p.wait_eng("sync", "scalar")
                for dt in range(16):
                    p.dma("sync", ctx.dbg_gn[128 * dt:128 * dt + 128, :],
                          ctx.gn_all[:, 1024 * dt:1024 * dt + 1024], "sout")
                for dt in range(16):
                    p.dma("sync", ctx.s_out[128 * dt:128 * dt + 128, :],
                          ctx.s_bf[:, 1024 * dt:1024 * dt + 1024], "sout")
                    p.dma("sync", ctx.xn_out[128 * dt:128 * dt + 128, 0:1024],
                          ctx.xn_own[:, 1024 * dt:1024 * dt + 1024], "sout")
                    p.dma("sync", ctx.xn_out[128 * dt:128 * dt + 128, 1024:2048],
                          ctx.xn_rest[:, 1024 * dt:1024 * dt + 1024], "sout")
                p.dma("sync", ctx.dbg_rows[0:1, 0:11], ctx.crow[0:1, 0:11], "sout")
                p.dma("sync", ctx.dbg_rows[1:2, 0:1], ctx.lam[:], "sout")
                p.dma("sync", ctx.dbg_rows[2:3, 0:1], ctx.mccol[0:1, :], "sout")
                p.wait_dma("sync", "sout")

        for name, reg in [("sync", block.sync), ("vector", block.vector),
                          ("scalar", block.scalar), ("tensor", block.tensor),
                          ("gpsimd", block.gpsimd)]:
            def body(eng, name=name):
                emit_all(Prog(nc, sems, name, eng))
            reg(body)
    return nc


def build_dequant_test():
    return build_program("dequant")


def pi2_rows(core):
    return np.arange(2048) * 2 + 4096 * (core % 4) + (1 if core >= 4 else 0)


def host_dequant_slices(core, gW, gS, uW, uS, dW, dS):
    rows = pi2_rows(core)
    gbT = np.ascontiguousarray(gW[rows, :].T)
    ubT = np.ascontiguousarray(uW[rows, :].T)
    gsT = np.ascontiguousarray(np.repeat(gS[rows, :].T, 16, axis=0))
    usT = np.ascontiguousarray(np.repeat(uS[rows, :].T, 16, axis=0))
    nb0 = 2048 * (core % 4)
    dbT = np.ascontiguousarray(dW[:, nb0:nb0 + 2048].T)
    dbT = np.ascontiguousarray(dbT[:, PI])
    j0 = 128 * (core % 4)
    dsT = np.ascontiguousarray(np.repeat(dS[:, j0:j0 + 128].T, 16, axis=0))
    dsT = np.ascontiguousarray(dsT[:, PI])
    w = 1.0 if core < 4 else 0.0
    dqsel = np.tile(np.array([[w, 1.0 - w]], np.float32), (128, 1))
    return gbT, ubT, gsT, usT, dbT, dsT, dqsel


def tau_perm(core):
    if core % 2 == 0:
        return np.arange(2048)
    return np.concatenate([np.arange(1024, 2048), np.arange(0, 1024)])


def head_order(core):
    own = np.arange(8) + 8 * (core % 2)
    rest = np.arange(8) + 8 * (1 - core % 2)
    return np.concatenate([own, rest])


def host_attn_inputs(core, x, Wq, Wk, Wv, Wo, lq1, lk1, lq2, lk2):
    b = core // 2
    tau = tau_perm(core)
    ho = head_order(core)
    qk_cols = np.concatenate([np.arange(128 * g, 128 * g + 128) for g in ho])
    v_cols = np.concatenate([np.arange(128 * g, 128 * g + 128) for g in ho])
    xfm = np.ascontiguousarray(x[b].T[PI][:, tau], np.float32)
    wq = np.ascontiguousarray(Wq[PI, :][:, qk_cols], np.float32)
    wk = np.ascontiguousarray(Wk[PI, :][:, qk_cols], np.float32)
    wv = np.ascontiguousarray(Wv[PI, :][:, v_cols], np.float32)
    wo = np.ascontiguousarray(Wo[:, PI], np.float32)
    lams = np.concatenate([lq1, lk1, lq2, lk2]).reshape(1, 256).astype(np.float32)
    masks = np.zeros((64, 128, 512), ml_dtypes.bfloat16)
    for i in range(64):
        blk, tk = i // 16, i % 16
        kpos = tau[128 * tk:128 * tk + 128][:, None]
        qpos = tau[512 * blk:512 * blk + 512][None, :]
        masks[i] = (kpos <= qpos).astype(ml_dtypes.bfloat16)
    w = 1.0 if core % 2 == 0 else 0.0
    osel = np.tile(np.array([[w, 1.0 - w]], np.float32), (128, 1))
    return {"xfm": xfm, "wq": wq, "wk": wk, "wv": wv, "wo": wo,
            "lams": lams, "masks": masks, "osel": osel}


_PROG_CACHE = {}


def _prog(mode):
    if mode not in _PROG_CACHE:
        _PROG_CACHE[mode] = build_program(mode)
    return _PROG_CACHE[mode]


def kernel(L0_attn_x, L0_attn_Wq, L0_attn_Wk, L0_attn_Wv, L0_attn_Wo,
           L0_attn_lambda_q1, L0_attn_lambda_k1, L0_attn_lambda_q2,
           L0_attn_lambda_k2, L0_ffn_gate_W_packed, L0_ffn_gate_scales,
           L0_ffn_up_W_packed, L0_ffn_up_scales, L0_ffn_down_W_packed,
           L0_ffn_down_scales):
    x = np.asarray(L0_attn_x, np.float32)
    gW = np.asarray(L0_ffn_gate_W_packed, np.int32)
    uW = np.asarray(L0_ffn_up_W_packed, np.int32)
    dW = np.asarray(L0_ffn_down_W_packed, np.int32)
    gS = np.asarray(L0_ffn_gate_scales, np.float32)
    uS = np.asarray(L0_ffn_up_scales, np.float32)
    dS = np.asarray(L0_ffn_down_scales, np.float32)

    in_maps = []
    for c in range(NCORE):
        gbT, ubT, gsT, usT, dbT, dsT, dqsel = host_dequant_slices(
            c, gW, gS, uW, uS, dW, dS)
        d = host_attn_inputs(c, x, np.asarray(L0_attn_Wq, np.float32),
                             np.asarray(L0_attn_Wk, np.float32),
                             np.asarray(L0_attn_Wv, np.float32),
                             np.asarray(L0_attn_Wo, np.float32),
                             np.asarray(L0_attn_lambda_q1, np.float32),
                             np.asarray(L0_attn_lambda_k1, np.float32),
                             np.asarray(L0_attn_lambda_q2, np.float32),
                             np.asarray(L0_attn_lambda_k2, np.float32))
        d.update({"gbT": gbT, "ubT": ubT, "gsT": gsT, "usT": usT,
                  "dbT": dbT, "dsT": dsT, "dqsel": dqsel})
        in_maps.append(d)
    res = run_bass_kernel_spmd(_prog("fused"), in_maps, core_ids=list(range(NCORE)))

    out = np.zeros((BATCH, T, D), np.float32)
    for c in range(NCORE):
        o = np.asarray(res.results[c]["out_fm"], np.float32)
        tau = tau_perm(c)
        out[c // 2, tau[:1024], :] = o[PI_INV, :].T
    return out


# version 14
# speedup vs baseline: 1.1508x; 1.0051x over previous
import math

import numpy as np
import ml_dtypes

import concourse.bass as bass
import concourse.mybir as mybir
from concourse.bass_utils import run_bass_kernel_spmd

F32 = mybir.dt.float32
BF16 = mybir.dt.bfloat16
I32 = mybir.dt.int32
ALU = mybir.AluOpType
AF = mybir.ActivationFunctionType
BFNP = ml_dtypes.bfloat16

D = 2048
T = 2048
BATCH = 4
NCORE = 8
HD = 64
NPAIR = 16
VD = 128
MLP = 16384
KC = D // 128
LAMBDA_INIT = 0.8 - 0.6 * math.exp(-0.3 * 12)
LN_EPS = 1e-5
GN_EPS = 1e-5

PI = np.concatenate([np.arange(0, D, 2), np.arange(1, D, 2)])
PI_INV = np.argsort(PI)

FVALS = [7.5, -1.0, 1.0, 0.5, -2.0, 4.0, -8.0, -10.0, 0.125,
         1.0 / 128.0, -1.0 / 128.0, LN_EPS, GN_EPS, 1.0 - LAMBDA_INIT,
         LAMBDA_INIT, 1.0 / D, 0.0, -16.0, 2.0, -0.5, 28.0, 24.0, -1e9,
         (1.0 - LAMBDA_INIT) ** 2]
IVALS = [15]
FIDX = {v: i for i, v in enumerate(FVALS)}
IIDX = {v: i for i, v in enumerate(IVALS)}


def own_qblocks(core):
    if core % 2 == 0:
        return [(0, 512), (1536, 2048)]
    return [(512, 1024), (1024, 1536)]


ENGS = ["sync", "vector", "scalar", "tensor", "gpsimd"]


class Prog:

    def __init__(self, nc, sems, eng_name, eng):
        self.nc = nc
        self.sems = sems
        self.eng_name = eng_name
        self.eng = eng
        self.ecnt = {e: 0 for e in ENGS}
        self.dcnt = {k: 0 for k in sems if not k.startswith("E_")}

    def on(self, name):
        return name == self.eng_name

    def op(self, eng_name, fn, inc=True, drain=None):
        if drain is None:
            drain = eng_name in ("vector", "scalar")
        if drain:
            self.wait_self(eng_name)
        if self.on(eng_name):
            instr = fn(self.eng)
            if inc and instr is not None:
                instr.then_inc(self.sems["E_" + eng_name], 1)
        if inc:
            self.ecnt[eng_name] += 1

    def wait_self(self, eng_name):
        if self.on(eng_name) and self.ecnt[eng_name] > 0:
            self.eng.wait_ge(self.sems["E_" + eng_name], self.ecnt[eng_name])

    def wait_eng(self, waiter, producer, at=None):
        v = self.ecnt[producer] if at is None else at
        if self.on(waiter) and v > 0:
            self.eng.wait_ge(self.sems["E_" + producer], v)

    def dma(self, eng_name, out, in_, stream):
        if self.on(eng_name):
            self.eng.dma_start(out=out, in_=in_).then_inc(self.sems[stream], 16)
        self.dcnt[stream] += 16

    def wait_dma(self, waiter, stream, at=None):
        v = self.dcnt[stream] if at is None else at
        if self.on(waiter) and v > 0:
            self.eng.wait_ge(self.sems[stream], v)

    def mark(self, *streams):
        return {k: self.dcnt[k] for k in streams}


class Ctx:
    pass


def make_consts(nc):
    cf_np = np.tile(np.array(FVALS, np.float32), (128, 1))
    ci_np = np.tile(np.array(IVALS, np.int32), (128, 1))
    cfd = nc.inline_tensor(cf_np, name="cfd").ap()
    cid = nc.inline_tensor(ci_np, name="cid").ap()
    cft = nc.alloc_sbuf_tensor("cft", [128, len(FVALS)], F32)
    cit = nc.alloc_sbuf_tensor("cit", [128, len(IVALS)], I32)
    for v, i in FIDX.items():
        nc.const_aps.register(F32, v, cft[:, i:i + 1])
    return cfd, cid, cft, cit


def CF(ctx, v):
    return ctx.cft[:, FIDX[v]:FIDX[v] + 1]


def CI(ctx, v):
    return ctx.cit[:, IIDX[v]:IIDX[v] + 1]


def CF1(ctx, v):
    return ctx.cft[0:1, FIDX[v]:FIDX[v] + 1]


DQ_TILES = []
for _mi, _ntt, _nibs, _rb in [(0, 8, ("lo", "hi"), 0), (1, 8, ("lo", "hi"), T),
                              (2, 16, ("dyn",), 2 * T)]:
    for _tt in range(_ntt):
        for _h in range(4):
            DQ_TILES.append((_mi, _tt, _h, _nibs, _rb))


def emit_dq_tiles(p, ctx, n):
    F = 512
    st = ctx.dq_state
    nbuf = len(ctx.dq_bytes)
    for _ in range(n):
        if st["k"] >= len(DQ_TILES):
            return
        k = st["k"]
        mi, tt, h, nibbles, rbase = DQ_TILES[k]
        bT = [ctx.gbT, ctx.ubT, ctx.dbT][mi]
        sT = [ctx.gsT, ctx.usT, ctx.dsT][mi]
        buf = k % nbuf
        csl = slice(F * h, F * h + F)
        if k >= nbuf:
            p.wait_dma("sync", f"dq_out{buf}", at=st["oat"][k - nbuf])
        p.dma("sync", ctx.dq_bytes[buf][:],
              bT[128 * tt:128 * tt + 128, csl], f"dq_ld{buf}")
        p.dma("sync", ctx.dq_scale[buf][:],
              sT[128 * tt:128 * tt + 128, csl], f"dq_ld{buf}")
        ld_target = p.dcnt[f"dq_ld{buf}"]
        p.wait_dma("vector", f"dq_ld{buf}", at=ld_target)
        bts = ctx.dq_bytes[buf][:]
        two = nibbles == ("lo", "hi")
        if two or nibbles == ("lo",):
            p.op("vector", lambda e: e.tensor_single_scalar(
                ctx.dq_ci[:], bts, CI(ctx, 15), ALU.bitwise_and))
            p.op("vector", lambda e: e.tensor_copy(ctx.dq_cf[0][:], ctx.dq_ci[:]))
        if two or nibbles == ("hi",):
            dst = ctx.dq_cf[1] if two else ctx.dq_cf[0]
            p.op("vector", lambda e: e.tensor_scalar(
                ctx.dq_ci[:], bts, 4, CI(ctx, 15),
                ALU.logical_shift_right, ALU.bitwise_and))
            p.op("vector", lambda e, dst=dst: e.tensor_copy(dst[:], ctx.dq_ci[:]))
        if nibbles == ("dyn",):
            p.op("vector", lambda e: e.tensor_scalar(
                ctx.dq_ci[:], bts, 4, CI(ctx, 15),
                ALU.logical_shift_right, ALU.bitwise_and))
            p.op("vector", lambda e: e.tensor_copy(ctx.dq_cf[0][:], ctx.dq_ci[:]))
            p.op("vector", lambda e: e.tensor_single_scalar(
                ctx.dq_ci[:], bts, CI(ctx, 15), ALU.bitwise_and))
            p.op("vector", lambda e: e.tensor_copy(ctx.dq_cf[1][:], ctx.dq_ci[:]))
            p.op("vector", lambda e: e.tensor_single_scalar(
                ctx.dq_cf[0][:], ctx.dq_cf[0][:], ctx.dqsel[:, 1:2], ALU.mult))
            p.op("vector", lambda e: e.scalar_tensor_tensor(
                ctx.dq_cf[0][:], ctx.dq_cf[1][:], ctx.dqsel[:, 0:1],
                ctx.dq_cf[0][:], ALU.mult, ALU.add))
        slots = (0, 1) if two else (0,)
        for sl in slots:
            cf = ctx.dq_cf[sl]
            p.op("vector", lambda e, sl=sl, cf=cf: e.tensor_single_scalar(
                ctx.dq_sgn[sl][:], cf[:], CF(ctx, 7.5), ALU.is_ge))
            p.op("vector", lambda e, sl=sl: e.tensor_scalar(
                ctx.dq_sgn[sl][:], ctx.dq_sgn[sl][:], CF(ctx, -2.0), CF(ctx, 1.0),
                ALU.mult, ALU.add))
            p.op("vector", lambda e, sl=sl, cf=cf: e.scalar_tensor_tensor(
                ctx.dq_t[sl][:], ctx.dq_sgn[sl][:], CF(ctx, 4.0), cf[:],
                ALU.mult, ALU.add))
        p.wait_eng("scalar", "vector")
        for sl in slots:
            p.op("scalar", lambda e, sl=sl: e.activation(
                ctx.dq_r1[sl][:], ctx.dq_t[sl][:], AF.Relu, bias=-8.0, scale=1.0))
            p.op("scalar", lambda e, sl=sl: e.activation(
                ctx.dq_r2[sl][:], ctx.dq_t[sl][:], AF.Relu, bias=-10.0, scale=1.0))
        p.wait_eng("vector", "scalar")
        for sl in slots:
            vbuf = ctx.dq_val[buf][sl]
            p.op("vector", lambda e, sl=sl: e.scalar_tensor_tensor(
                ctx.dq_r1[sl][:], ctx.dq_r1[sl][:], CF(ctx, 0.5),
                ctx.dq_r2[sl][:], ALU.mult, ALU.add))
            p.op("vector", lambda e, sl=sl: e.scalar_tensor_tensor(
                ctx.dq_r1[sl][:], ctx.dq_t[sl][:], CF(ctx, 0.5),
                ctx.dq_r1[sl][:], ALU.mult, ALU.add))
            p.op("vector", lambda e, sl=sl: e.scalar_tensor_tensor(
                ctx.dq_r1[sl][:], ctx.dq_r1[sl][:], CF(ctx, -2.0),
                ctx.dq_sgn[sl][:], ALU.add, ALU.mult))
            p.op("vector", lambda e, sl=sl, vbuf=vbuf: e.tensor_tensor(
                vbuf[:], ctx.dq_r1[sl][:], ctx.dq_scale[buf][:], ALU.mult))
        p.wait_eng("scalar", "vector")
        for nib in nibbles:
            if mi < 2:
                row0 = rbase + (0 if nib == "lo" else 1024) + 128 * tt
            else:
                row0 = rbase + 128 * tt
            vbuf = ctx.dq_val[buf][0 if nib != "hi" else 1]
            p.dma("scalar", ctx.cc_in[row0:row0 + 128, csl], vbuf[:],
                  f"dq_out{buf}")
        st["oat"].append(p.dcnt[f"dq_out{buf}"])
        st["k"] = k + 1


def emit_dequant(p, ctx):
    emit_dq_tiles(p, ctx, len(DQ_TILES))


def emit_dequant_wait_done(p, ctx):
    for i in range(3):
        p.wait_dma("gpsimd", f"dq_out{i}")


A0 = 5120
A2 = A0 + 32 * 1024
B0 = A2 + 32 * 1024
C0 = B0 + 64 * 1024
D0 = C0 + 32 * 1024
E0 = D0 + 32 * 1024


def make_ctx(nc, mode):
    ctx = Ctx()
    ctx.mode = mode
    at = nc.alloc_sbuf_tensor_at

    def T_(name, off, shape, dt):
        return at(name, shape, dt, offset=off)

    dram = lambda n, sh, dt: nc.dram_tensor(n, sh, dt, kind="ExternalInput").ap()
    if mode in ("dequant", "fused"):
        ctx.gbT = dram("gbT", [1024, 2048], I32)
        ctx.ubT = dram("ubT", [1024, 2048], I32)
        ctx.dbT = dram("dbT", [2048, 2048], I32)
        ctx.gsT = dram("gsT", [1024, 2048], F32)
        ctx.usT = dram("usT", [1024, 2048], F32)
        ctx.dsT = dram("dsT", [2048, 2048], F32)
        ctx.dqsel_d = dram("dqsel", [128, 2], F32)
    if mode in ("attn", "fused", "ffn", "layer2"):
        ctx.xfm = dram("xfm", [2048, 2048], F32)
    if mode == "ffn":
        ctx.ccw_d = dram("ccw", [8, 3, 2048, 2048], BF16)
        ctx.s_in = dram("s_in", [2048, 1024], BF16)
    if mode == "layer2":
        ctx.ccw_d = dram("ccw", [8, 3, 2048, 2048], BF16)
    if mode in ("ffn", "layer2"):
        ctx.ccw = [[ctx.ccw_d[b5, mi] for mi in range(3)] for b5 in range(8)]
    if mode == "fused":
        ctx.cc_in = nc.dram_tensor("cc_in", [3 * T, 2048], BF16,
                                   kind="Internal").ap()
        ctx.cc_gu = nc.dram_tensor("cc_gu", [8 * 2 * T, 2048], BF16,
                                   kind="Internal", addr_space="Shared").ap()
        ctx.cc_d = nc.dram_tensor("cc_d", [8 * T, 2048], BF16,
                                  kind="Internal", addr_space="Shared").ap()
        ctx.ccw = [[ctx.cc_gu[(2 * b5 + 0) * T:(2 * b5 + 1) * T, :],
                    ctx.cc_gu[(2 * b5 + 1) * T:(2 * b5 + 2) * T, :],
                    ctx.cc_d[b5 * T:(b5 + 1) * T, :]] for b5 in range(8)]
    if mode in ("attn", "fused", "layer2"):
        ctx.wq = dram("wq", [2048, 2048], F32)
        ctx.wk = dram("wk", [2048, 2048], F32)
        ctx.wv = dram("wv", [2048, 2048], F32)
        ctx.wo = dram("wo", [2048, 2048], F32)
        ctx.lam_d = dram("lams", [1, 256], F32)
        ctx.osel_d = dram("osel", [128, 2], F32)
        ctx.masks_d = dram("masks", [64, 128, 512], BF16)

    cf_np = np.tile(np.array(FVALS, np.float32), (128, 1))
    ci_np = np.tile(np.array(IVALS, np.int32), (128, 1))
    ctx.cfd = nc.inline_tensor(cf_np, name="cfd").ap()
    ctx.cid = nc.inline_tensor(ci_np, name="cid").ap()
    ctx.ones_d = nc.inline_tensor(np.ones((128, 2), np.float32), name="ones_d").ap()
    ctx.onesb_d = nc.inline_tensor(np.ones((128, 2), ml_dtypes.bfloat16),
                                   name="onesb_d").ap()
    ctx.onesr_d = nc.inline_tensor(np.ones((1, 128), np.float32), name="onesr_d").ap()

    e = E0
    def Es(name, shape, dt):
        nonlocal e
        e = (e + 31) & ~31
        t = T_(name, e, shape, dt)
        e += shape[1] * mybir.dt.size(dt)
        return t
    ctx.cft = Es("cft", [128, len(FVALS)], F32)
    ctx.cit = Es("cit", [128, len(IVALS)], I32)
    for v, i in FIDX.items():
        nc.const_aps.register(F32, v, ctx.cft[:, i:i + 1])
    ctx.ones_f = Es("ones_f", [128, 2], F32)
    ctx.ones_b = Es("ones_b", [128, 2], BF16)
    ctx.ones_r = Es("ones_r", [1, 128], F32)
    if mode in ("dequant", "fused"):
        ctx.dqsel = Es("dqsel_t", [128, 2], F32)
    if mode in ("attn", "fused", "ffn", "layer2"):
        ctx.wqkf = Es("wqkf", [128, 2048], F32)
        ctx.wqb = Es("wqb", [128, 2048], BF16)
        ctx.wkb = Es("wkb", [128, 2048], BF16)
        ctx.P01 = [Es(f"P01_{i}", [128, 1024], BF16) for i in range(2)]
        ctx.maskb = [Es(f"maskb{i}", [128, 512], BF16) for i in range(2)]
        ctx.crow = Es("crow", [1, 16], F32)
        ctx.lrows = Es("lrows", [1, 256], F32)
        ctx.lscr = Es("lscr", [1, 128], F32)
        ctx.lam = Es("lam", [1, 1], F32)
        ctx.mlam = Es("mlam", [1, 1], F32)
        ctx.mccol = Es("mccol", [128, 1], F32)
        ctx.osel = Es("osel_t", [128, 2], F32)
        ctx.wob = [ctx.wqb, ctx.wkb]
    ctx.e_end = e
    assert e <= 229376, f"E region overflow: {e}"

    if mode in ("attn", "fused", "ffn", "layer2"):
        ctx.xn_own = T_("xn_own", A0, [128, 16 * 1024], BF16)
        ctx.xn_rest = T_("xn_rest", A2, [128, 16 * 1024], BF16)
        ctx.murow = T_("murow", B0, [1, 2048], F32)
        ctx.isrow = T_("isrow", B0 + 8192, [1, 2048], F32)
        ctx.rtmp = T_("rtmp", B0 + 16384, [1, 2048], F32)
        ctx.V = T_("V", B0, [128, 16 * 1024], BF16)
        ctx.s_bf = T_("s_bf", B0, [128, 16 * 1024], BF16)
        ctx.gn_eff = T_("gn_eff", D0, [128, 16 * 1024], BF16)
        ctx.xbuf = [T_(f"xbuf{i}", C0 + i * 8192, [128, 2048], F32) for i in range(2)]
        ctx.x2 = T_("x2", C0 + 16384, [128, 2048], BF16)
        ctx.ttmp = T_("ttmp", C0 + 20480, [128, 2048], F32)
        ctx.wvf = [T_(f"wvf{i}", C0 + 28672 + i * 2048, [128, 512], F32)
                   for i in range(2)]
        ctx.wvb = T_("wvb", C0, [128, 16 * 512], BF16)
        ctx.gn_all = T_("gn_all", C0, [128, 16 * 1024], BF16)
        dd = B0 + 32 * 1024
        dd0 = dd
        def Dm(name, shape, dt):
            nonlocal dd
            dd = (dd + 31) & ~31
            t = T_(name, dd, shape, dt)
            dd += shape[1] * mybir.dt.size(dt)
            return t
        ctx.kt_off = (dd + 31) & ~31
        ctx.KT = Dm("KT", [128, 2048], BF16)
        ctx.QT = Dm("QT", [128, 2048], BF16)
        ctx.Q2 = Dm("Q2", [128, 2048], BF16)
        ctx.K2 = Dm("K2", [128, 2048], BF16)
        ctx.t1f = Dm("t1f", [128, 512], F32)
        ctx.opf = Dm("opf", [128, 512], F32)
        ctx.sqb = Dm("sqb", [128, 512], BF16)
        ctx.ou0 = Dm("ou0", [128, 512], F32)
        ctx.ou1 = Dm("ou1", [128, 512], F32)
        ctx.erow = Dm("erow", [1, 1024], F32)
        assert dd <= C0, f"B attn block overflow: {dd}"
        ctx.wof2 = T_("wof2", ctx.kt_off, [128, 2048], F32)
        ctx.wof = [ctx.wqkf, ctx.wof2]
    if mode in ("dequant", "fused"):
        d = D0
        def Ds(name, shape, dt):
            nonlocal d
            d = (d + 31) & ~31
            t = T_(name, d, shape, dt)
            d += shape[1] * mybir.dt.size(dt)
            return t
        ctx.dq_bytes = [Ds(f"dqb{i}", [128, 512], I32) for i in range(3)]
        ctx.dq_scale = [Ds(f"dqs{i}", [128, 512], F32) for i in range(3)]
        ctx.dq_val = [[Ds(f"dqv{i}{j}", [128, 512], BF16) for j in range(2)]
                      for i in range(3)]
        ctx.dq_ci = Ds("dqci", [128, 512], I32)
        ctx.dq_cf = [Ds(f"dqcf{i}", [128, 512], BF16) for i in range(2)]
        ctx.dq_sgn = [Ds(f"dqsg{i}", [128, 512], BF16) for i in range(2)]
        ctx.dq_t = [Ds(f"dqt{i}", [128, 512], BF16) for i in range(2)]
        ctx.dq_r1 = [Ds(f"dqr1{i}", [128, 512], BF16) for i in range(2)]
        ctx.dq_r2 = [Ds(f"dqr2{i}", [128, 512], BF16) for i in range(2)]
        assert d <= D0 + 32 * 1024, f"dequant arena overflow {d - D0}"

    if mode in ("ffn", "fused", "layer2"):
        ctx.dacc_a = T_("dacc_a", A2, [128, 8 * 1024], F32)
        ctx.dacc_b = T_("dacc_b", B0 + 32 * 1024, [128, 8 * 1024], F32)
        ctx.inter = T_("inter", C0, [128, 16 * 1024], BF16)
        ctx.wgt = [T_(f"wgt{i}", D0 + 4096 * i, [128, 2048], BF16) for i in range(2)]
        ctx.wut = [T_(f"wut{i}", D0 + 8192 + 4096 * i, [128, 2048], BF16)
                   for i in range(2)]
        ctx.wdt = [T_(f"wdt{i}", D0 + 16384 + 4096 * i, [128, 2048], BF16)
                   for i in range(2)]
        ctx.ffn_gg = T_("ffn_gg", D0 + 24576, [128, 1024], BF16)
        ctx.ffn_ub = T_("ffn_ub", D0 + 26624, [128, 1024], BF16)
        ctx.gbuf = T_("gbuf", D0 + 16384, [128, 1024], BF16)
        ctx.sgtmp = T_("sgtmp", D0 + 24576, [128, 1024], F32)
        ctx.obuf = [T_("obuf0", ctx.wqkf.manual_sbuf_range[0], [128, 1024], F32),
                    T_("obuf1", ctx.wqkf.manual_sbuf_range[0] + 4096,
                       [128, 1024], F32)]
        ctx.xfbuf = [T_("xfbuf0", ctx.wqb.manual_sbuf_range[0], [128, 1024], F32),
                     T_("xfbuf1", ctx.wkb.manual_sbuf_range[0], [128, 1024], F32)]

    ctx.ps = nc.alloc_psum_tensor("ps", [128, 4096], F32)

    reset_sched_state(ctx)

    def xn_sl(kc, t0, w=512):
        if t0 < 1024:
            return ctx.xn_own[:, 1024 * kc + t0:1024 * kc + t0 + w]
        return ctx.xn_rest[:, 1024 * kc + (t0 - 1024):1024 * kc + (t0 - 1024) + w]
    ctx.xn_sl = xn_sl
    return ctx


def reset_sched_state(ctx):
    ctx.xs_pe_done = {}
    ctx.xs_act_done = {}
    ctx.x2_pe_done = 0
    ctx.xa_dve_done = {}
    ctx.bcast_cons = {}
    ctx.wv_cast_done = {}
    ctx.vp_evac_done = {}
    ctx.wqk_cast_done = 0
    ctx.kq_evac_done = 0
    ctx.cb_red_done = {}
    ctx.mask_cons = {}
    ctx.exp_done = {}
    ctx.wo_cast_done = {}
    ctx.op_evac_done = {}
    ctx.op_mm_done = {}
    ctx.av_done = 0
    ctx.dq_vec_done = 0
    ctx.dq_act_done = 0
    ctx.kq_mm_done = 0
    ctx.dq_state = {"k": 0, "oat": []}
    ctx.score_done = {}
    ctx.mask_done = {}


def sem_names_for(mode):
    names = ["consts"] + ["E_" + x for x in ENGS]
    if mode in ("dequant", "fused"):
        names += [f"dq_ld{i}" for i in range(3)] + [f"dq_out{i}" for i in range(3)]
    if mode in ("attn", "fused", "ffn", "layer2"):
        names += ["xs0", "xs1", "xa0", "xa1", "wv0", "wv1", "wqk",
                  "mk0", "mk1", "wo0", "wo1", "sout"]
    if mode in ("ffn", "fused", "layer2"):
        names += ["wg00", "wg01", "wg10", "wg11", "wd0", "wd1",
                  "xf0", "xf1", "xo0", "xo1", "sld"]
    return names


def emit_preamble(p, ctx):
    p.dma("sync", ctx.cft[:], ctx.cfd, "consts")
    p.dma("sync", ctx.cit[:], ctx.cid, "consts")
    p.dma("sync", ctx.ones_f[:], ctx.ones_d, "consts")
    p.dma("sync", ctx.ones_b[:], ctx.onesb_d, "consts")
    p.dma("sync", ctx.ones_r[:], ctx.onesr_d, "consts")
    if hasattr(ctx, "dqsel_d"):
        p.dma("sync", ctx.dqsel[:], ctx.dqsel_d, "consts")
    if hasattr(ctx, "lam_d"):
        p.dma("sync", ctx.lrows[:], ctx.lam_d, "consts")
        p.dma("sync", ctx.osel[:], ctx.osel_d, "consts")
    for e in ENGS[1:]:
        p.wait_dma(e, "consts")
    p.wait_dma("sync", "consts")


def emit_lambda(p, ctx):
    for i in (0, 1):
        q = ctx.lrows[:, 64 * (2 * i):64 * (2 * i) + 64]
        kk = ctx.lrows[:, 64 * (2 * i + 1):64 * (2 * i + 1) + 64]
        p.op("vector", lambda e, q=q, kk=kk: e.tensor_mul(ctx.lscr[:, 0:64], q, kk))
        p.op("vector", lambda e, i=i: e.reduce_sum(
            ctx.lscr[:, 64 + i:65 + i], ctx.lscr[:, 0:64],
            axis=mybir.AxisListType.X))
    p.wait_eng("scalar", "vector")
    p.op("scalar", lambda e: e.activation(
        ctx.lscr[:, 66:68], ctx.lscr[:, 64:66], AF.Exp))
    p.wait_eng("vector", "scalar")
    p.op("vector", lambda e: e.tensor_sub(
        ctx.lam[:, 0:1], ctx.lscr[:, 66:67], ctx.lscr[:, 67:68]))
    p.op("vector", lambda e: e.tensor_scalar(
        ctx.lam[:, 0:1], ctx.lam[:, 0:1], CF1(ctx, LAMBDA_INIT), None, ALU.add))
    p.op("vector", lambda e: e.tensor_scalar(
        ctx.mlam[:, 0:1], ctx.lam[:, 0:1], CF1(ctx, -1.0), None, ALU.mult))


def emit_ln(p, ctx):
    PS = ctx.ps
    for kc in range(KC):
        b = kc % 2
        if kc >= 2:
            p.wait_eng("sync", "tensor", at=ctx.xs_pe_done[kc - 2])
            p.wait_eng("sync", "scalar", at=ctx.xs_act_done[kc - 2])
        p.dma("sync", ctx.xbuf[b][:], ctx.xfm[128 * kc:128 * kc + 128, :], f"xs{b}")
        p.wait_dma("scalar", f"xs{b}")
        p.wait_eng("scalar", "tensor", at=ctx.x2_pe_done)
        p.op("scalar", lambda e, b=b: e.activation(ctx.x2[:, :], ctx.xbuf[b][:],
                                                   AF.Square))
        ctx.xs_act_done[kc] = p.ecnt["scalar"]
        p.wait_dma("tensor", f"xs{b}")
        for tc in range(4):
            sl = slice(512 * tc, 512 * tc + 512)
            p.op("tensor", lambda e, b=b, sl=sl, tc=tc, kc=kc: e.matmul(
                PS[0:1, 512 * tc:512 * tc + 512], ctx.ones_f[:, 0:1],
                ctx.xbuf[b][:, sl], start=(kc == 0), stop=(kc == KC - 1)),
                inc=(tc == 3))
        p.wait_eng("tensor", "scalar")
        for tc in range(4):
            sl = slice(512 * tc, 512 * tc + 512)
            p.op("tensor", lambda e, sl=sl, tc=tc, kc=kc: e.matmul(
                PS[0:1, 2048 + 512 * tc:2048 + 512 * tc + 512], ctx.ones_b[:, 0:1],
                ctx.x2[:, sl], start=(kc == 0), stop=(kc == KC - 1)),
                inc=(tc == 3))
        ctx.xs_pe_done[kc] = p.ecnt["tensor"]
        ctx.x2_pe_done = p.ecnt["tensor"]
    p.wait_eng("vector", "tensor")
    p.op("vector", lambda e: e.tensor_scalar(
        ctx.murow[:, :], PS[0:1, 0:2048], CF1(ctx, 1.0 / D), None, ALU.mult))
    p.op("vector", lambda e: e.tensor_scalar(
        ctx.isrow[:, :], PS[0:1, 2048:4096], CF1(ctx, 1.0 / D), None, ALU.mult))
    p.op("vector", lambda e: e.tensor_mul(ctx.rtmp[:, :], ctx.murow[:, :],
                                          ctx.murow[:, :]))
    p.op("vector", lambda e: e.tensor_sub(ctx.isrow[:, :], ctx.isrow[:, :],
                                          ctx.rtmp[:, :]))
    p.op("vector", lambda e: e.tensor_scalar(
        ctx.isrow[:, :], ctx.isrow[:, :], CF1(ctx, LN_EPS), None, ALU.add))
    p.op("vector", lambda e: e.reciprocal(ctx.isrow[:, :], ctx.isrow[:, :]))
    p.wait_eng("scalar", "vector")
    p.op("scalar", lambda e: e.activation(ctx.isrow[:, :], ctx.isrow[:, :], AF.Sqrt))
    stats_t = p.ecnt["tensor"]
    stats_s = p.ecnt["scalar"]
    for kc in range(KC):
        b = kc % 2
        if kc >= 2:
            p.wait_eng("sync", "vector", at=ctx.xa_dve_done[kc - 2])
        else:
            p.wait_eng("sync", "tensor", at=stats_t)
            p.wait_eng("sync", "scalar", at=stats_s)
        p.dma("sync", ctx.xbuf[b][:], ctx.xfm[128 * kc:128 * kc + 128, :], f"xa{b}")
        p.wait_dma("tensor", f"xa{b}")
        p.wait_eng("tensor", "scalar")
        for tc in range(4):
            bb = 2 * (tc % 2)
            p.wait_eng("tensor", "vector", at=ctx.bcast_cons.get(bb, 0))
            p.op("tensor", lambda e, tc=tc, bb=bb: e.matmul(
                PS[:, 512 * bb:512 * bb + 512], ctx.ones_r[0:1, :],
                ctx.murow[0:1, 512 * tc:512 * tc + 512], start=True, stop=True))
            p.op("tensor", lambda e, tc=tc, bb=bb: e.matmul(
                PS[:, 512 * (bb + 1):512 * (bb + 1) + 512], ctx.ones_r[0:1, :],
                ctx.isrow[0:1, 512 * tc:512 * tc + 512], start=True, stop=True))
            p.wait_eng("vector", "tensor")
            sl = slice(512 * tc, 512 * tc + 512)
            xnsl = ctx.xn_sl(kc, 512 * tc)
            p.op("vector", lambda e, b=b, sl=sl, bb=bb: e.tensor_sub(
                ctx.ttmp[:, 0:512], ctx.xbuf[b][:, sl], PS[:, 512 * bb:512 * bb + 512]))
            p.op("vector", lambda e, xnsl=xnsl, bb=bb: e.tensor_mul(
                xnsl, ctx.ttmp[:, 0:512], PS[:, 512 * (bb + 1):512 * (bb + 1) + 512]))
            ctx.bcast_cons[bb] = p.ecnt["vector"]
        ctx.xa_dve_done[kc] = p.ecnt["vector"]


def emit_vproj(p, ctx, hook=None):
    PS = ctx.ps
    vq_pe_done = 0
    casts = []
    for quarter in range(2):
        vc0 = 512 * quarter
        for kc in range(KC):
            b = kc % 2
            gi = 16 * quarter + kc
            if gi >= 2:
                p.wait_eng("sync", "vector", at=casts[gi - 2])
            src = ctx.wv.rearrange("(kc pp) a -> pp kc a", pp=128)[:, kc, vc0:vc0 + 512]
            p.dma("sync", ctx.wvf[b][:, 0:512], src, f"wv{b}")
            p.wait_dma("vector", f"wv{b}")
            if quarter > 0:
                p.wait_eng("vector", "tensor", at=vq_pe_done)
            p.op("vector", lambda e, b=b, kc=kc: e.tensor_copy(
                ctx.wvb[:, 512 * kc:512 * kc + 512], ctx.wvf[b][:, 0:512]))
            casts.append(p.ecnt["vector"])
        p.wait_eng("tensor", "vector")
        for tt in range(16):
            bank = 4 + (tt % 2)
            p.wait_eng("tensor", "scalar", at=ctx.vp_evac_done.get(bank, 0))
            for kc in range(KC):
                stat = ctx.xn_sl(kc, 128 * tt, w=128)
                p.op("tensor", lambda e, stat=stat, kc=kc, bank=bank: e.matmul(
                    PS[:, 512 * bank:512 * bank + 512], stat,
                    ctx.wvb[:, 512 * kc:512 * kc + 512],
                    start=(kc == 0), stop=(kc == KC - 1)), inc=(kc == KC - 1))
            p.wait_eng("scalar", "tensor")
            p.op("scalar", lambda e, tt=tt, vc0=vc0, bank=bank: e.activation(
                ctx.V[:, 1024 * tt + vc0:1024 * tt + vc0 + 512],
                PS[:, 512 * bank:512 * bank + 512], AF.Copy))
            ctx.vp_evac_done[bank] = p.ecnt["scalar"]
        vq_pe_done = p.ecnt["tensor"]
        if hook is not None:
            hook(quarter)


def emit_kqproj(p, ctx, hp):
    PS = ctx.ps
    for wi, (wdram, wbt) in enumerate([(ctx.wq, ctx.wqb), (ctx.wk, ctx.wkb)]):
        p.wait_eng("sync", "vector", at=ctx.wqk_cast_done)
        src = wdram.rearrange("(kc pp) a -> pp kc a", pp=128)[:, :, 128 * hp:128 * hp + 128]
        p.dma("sync", ctx.wqkf[:].rearrange("pp (kc c) -> pp kc c", c=128), src, "wqk")
        p.wait_dma("vector", "wqk")
        p.wait_eng("vector", "tensor", at=ctx.kq_mm_done)
        p.op("vector", lambda e, wbt=wbt: e.tensor_copy(wbt[:], ctx.wqkf[:]))
        ctx.wqk_cast_done = p.ecnt["vector"]
    p.wait_eng("tensor", "vector", at=ctx.wqk_cast_done)
    p.wait_eng("tensor", "scalar")
    for dst, wbt, nch, scale, bank in [(ctx.KT, ctx.wkb, 4, None, 0),
                                       (ctx.QT, ctx.wqb, 4, 0.125, 1)]:
        for tch in range(nch):
            p.wait_eng("tensor", "scalar", at=ctx.kq_evac_done)
            for kc in range(KC):
                stat = wbt[:, 128 * kc:128 * kc + 128]
                mov = ctx.xn_sl(kc, 512 * tch, w=512)
                p.op("tensor", lambda e, stat=stat, mov=mov, kc=kc, bank=bank: e.matmul(
                    PS[:, 512 * bank:512 * bank + 512], stat, mov,
                    start=(kc == 0), stop=(kc == KC - 1)), inc=(kc == KC - 1))
            p.wait_eng("scalar", "tensor")
            if scale is None:
                p.op("scalar", lambda e, dst=dst, tch=tch, bank=bank: e.activation(
                    dst[:, 512 * tch:512 * tch + 512],
                    PS[:, 512 * bank:512 * bank + 512], AF.Copy))
            else:
                p.op("scalar", lambda e, dst=dst, tch=tch, bank=bank, scale=scale:
                     e.activation(dst[:, 512 * tch:512 * tch + 512],
                                  PS[:, 512 * bank:512 * bank + 512], AF.Copy,
                                  scale=scale))
            ctx.kq_evac_done = p.ecnt["scalar"]
    ctx.kq_mm_done = p.ecnt["tensor"]


def emit_cbound(p, ctx, hp):
    PS = ctx.ps
    p.op("scalar", lambda e: e.activation(ctx.Q2[:], ctx.QT[:], AF.Square))
    p.op("scalar", lambda e: e.activation(ctx.K2[:], ctx.KT[:], AF.Square))
    p.wait_eng("tensor", "scalar")
    nm = 0
    for src, nch in [(ctx.Q2, 4), (ctx.K2, 4)]:
        for i in range(nch):
            bank = 4 if nm % 2 == 0 else 5
            p.wait_eng("tensor", "vector", at=ctx.cb_red_done.get(bank, 0))
            p.op("tensor", lambda e, src=src, i=i, bank=bank: e.matmul(
                PS[0:1, 512 * bank:512 * bank + 512], ctx.ones_b[:, 0:1],
                src[:, 512 * i:512 * i + 512], start=True, stop=True))
            p.wait_eng("vector", "tensor")
            p.op("vector", lambda e, nm=nm, bank=bank: e.reduce_max(
                ctx.crow[:, nm:nm + 1], PS[0:1, 512 * bank:512 * bank + 512],
                axis=mybir.AxisListType.X))
            ctx.cb_red_done[bank] = p.ecnt["vector"]
            nm += 1
    p.op("vector", lambda e: e.reduce_max(ctx.crow[:, 8:9], ctx.crow[:, 0:4],
                                          axis=mybir.AxisListType.X))
    p.op("vector", lambda e: e.reduce_max(ctx.crow[:, 9:10], ctx.crow[:, 4:8],
                                          axis=mybir.AxisListType.X))
    p.op("vector", lambda e: e.tensor_mul(ctx.crow[:, 8:9], ctx.crow[:, 8:9],
                                          ctx.crow[:, 9:10]))
    p.wait_eng("scalar", "vector")
    p.op("scalar", lambda e: e.activation(ctx.crow[:, 10:11], ctx.crow[:, 8:9],
                                          AF.Sqrt))
    p.wait_eng("vector", "scalar")
    p.op("vector", lambda e: e.tensor_scalar(
        ctx.crow[:, 10:11], ctx.crow[:, 10:11], CF1(ctx, -1.0), None, ALU.mult))
    p.wait_eng("tensor", "vector")
    p.op("tensor", lambda e: e.matmul(
        PS[:, 512 * 4:512 * 4 + 1], ctx.ones_r[0:1, :], ctx.crow[0:1, 10:11],
        start=True, stop=True))
    p.wait_eng("scalar", "tensor")
    p.op("scalar", lambda e: e.activation(ctx.mccol[:, 0:1],
                                          PS[:, 512 * 4:512 * 4 + 1], AF.Copy))


def emit_attn_block(p, ctx, hp, blk):
    PS = ctx.ps
    q0 = 512 * blk

    def emit_scores(tk):
        it = 16 * blk + tk
        mb = it % 2
        sb = 2 * (tk % 2)
        p.wait_eng("sync", "vector", at=ctx.mask_cons.get(mb, 0))
        p.dma("sync", ctx.maskb[mb][:], ctx.masks_d[it], f"mk{mb}")
        p.wait_eng("tensor", "scalar", at=ctx.exp_done.get(tk % 2, 0))
        p.op("tensor", lambda e, tk=tk, q0=q0, sb=sb: e.matmul(
            PS[:, 512 * sb:512 * sb + 512],
            ctx.KT[0:64, 128 * tk:128 * tk + 128],
            ctx.QT[0:64, q0:q0 + 512], start=True, stop=True))
        p.op("tensor", lambda e, tk=tk, q0=q0, sb=sb: e.matmul(
            PS[:, 512 * (sb + 1):512 * (sb + 1) + 512],
            ctx.KT[64:128, 128 * tk:128 * tk + 128],
            ctx.QT[64:128, q0:q0 + 512], start=True, stop=True,
            tile_position=(64, 0)))
        ctx.score_done[tk] = p.ecnt["tensor"]
        pb = ctx.P01[it % 2]
        p.wait_eng("scalar", "tensor", at=ctx.score_done[tk])
        p.op("scalar", lambda e, pb=pb, sb=sb: e.activation(
            pb[:, 0:512], PS[:, 512 * sb:512 * sb + 512], AF.Exp,
            bias=ctx.mccol[:, 0:1]))
        p.op("scalar", lambda e, pb=pb, sb=sb: e.activation(
            pb[:, 512:1024], PS[:, 512 * (sb + 1):512 * (sb + 1) + 512], AF.Exp,
            bias=ctx.mccol[:, 0:1]))
        ctx.exp_done[tk % 2] = p.ecnt["scalar"]
        p.wait_dma("vector", f"mk{mb}")
        p.wait_eng("vector", "scalar")
        p.op("vector", lambda e, pb=pb, mb=mb: e.tensor_mul(
            pb[:, 0:512], pb[:, 0:512], ctx.maskb[mb][:]))
        p.op("vector", lambda e, pb=pb, mb=mb: e.tensor_mul(
            pb[:, 512:1024], pb[:, 512:1024], ctx.maskb[mb][:]))
        ctx.mask_cons[mb] = p.ecnt["vector"]
        ctx.mask_done[tk] = p.ecnt["vector"]

    def emit_zav(tk):
        it = 16 * blk + tk
        pb = ctx.P01[it % 2]
        p.wait_eng("tensor", "vector", at=ctx.mask_done[tk])
        p.op("tensor", lambda e, pb=pb, tk=tk: e.matmul(
            PS[0:1, 512 * 6:512 * 6 + 512], ctx.ones_b[:, 0:1], pb[:, 0:512],
            start=(tk == 0), stop=(tk == 15)))
        p.op("tensor", lambda e, pb=pb, tk=tk: e.matmul(
            PS[0:1, 512 * 7:512 * 7 + 512], ctx.ones_b[:, 0:1], pb[:, 512:1024],
            start=(tk == 0), stop=(tk == 15)))
        vstat = ctx.V[:, 1024 * tk + 128 * hp:1024 * tk + 128 * hp + 128]
        p.op("tensor", lambda e, pb=pb, vstat=vstat, tk=tk: e.matmul(
            PS[:, 512 * 4:512 * 4 + 512], vstat, pb[:, 0:512],
            start=(tk == 0), stop=(tk == 15)))
        p.op("tensor", lambda e, pb=pb, vstat=vstat, tk=tk: e.matmul(
            PS[:, 512 * 5:512 * 5 + 512], vstat, pb[:, 512:1024],
            start=(tk == 0), stop=(tk == 15)))
        ctx.av_done = p.ecnt["tensor"]

    for tk in range(16):
        emit_scores(tk)
        if tk > 0:
            emit_zav(tk - 1)
    emit_zav(15)

    p.wait_eng("vector", "tensor", at=ctx.av_done)
    p.op("vector", lambda e: e.reciprocal(ctx.erow[:, 0:512],
                                          PS[0:1, 512 * 6:512 * 6 + 512]))
    p.op("vector", lambda e: e.reciprocal(ctx.erow[:, 512:1024],
                                          PS[0:1, 512 * 7:512 * 7 + 512]))
    p.op("vector", lambda e: e.tensor_scalar(
        ctx.erow[:, 512:1024], ctx.erow[:, 512:1024], ctx.lam[0:1, 0:1], None,
        ALU.mult))
    p.wait_eng("tensor", "vector")
    p.op("tensor", lambda e: e.matmul(
        PS[:, 512 * 6:512 * 6 + 512], ctx.ones_r[0:1, :], ctx.erow[0:1, 0:512],
        start=True, stop=True))
    p.op("tensor", lambda e: e.matmul(
        PS[:, 512 * 7:512 * 7 + 512], ctx.ones_r[0:1, :], ctx.erow[0:1, 512:1024],
        start=True, stop=True))
    p.wait_eng("scalar", "tensor")
    p.op("scalar", lambda e: e.activation(ctx.ou0[:], PS[:, 512 * 4:512 * 4 + 512],
                                          AF.Copy))
    p.op("scalar", lambda e: e.activation(ctx.ou1[:], PS[:, 512 * 5:512 * 5 + 512],
                                          AF.Copy))
    p.wait_eng("vector", "scalar")
    p.op("vector", lambda e: e.tensor_mul(
        ctx.t1f[:], ctx.ou1[:], PS[:, 512 * 7:512 * 7 + 512]))
    p.op("vector", lambda e: e.tensor_mul(
        ctx.opf[:], ctx.ou0[:], PS[:, 512 * 6:512 * 6 + 512]))
    p.op("vector", lambda e: e.tensor_sub(
        ctx.opf[:], ctx.opf[:], ctx.t1f[:]))
    p.wait_eng("scalar", "vector")
    p.op("scalar", lambda e: e.activation(ctx.sqb[:], ctx.opf[:], AF.Square))
    p.wait_eng("tensor", "scalar")
    p.op("tensor", lambda e: e.matmul(
        PS[0:1, 512 * 6:512 * 6 + 512], ctx.ones_f[:, 0:1], ctx.opf[:],
        start=True, stop=True))
    p.op("tensor", lambda e: e.matmul(
        PS[0:1, 512 * 7:512 * 7 + 512], ctx.ones_b[:, 0:1], ctx.sqb[:],
        start=True, stop=True))
    p.wait_eng("vector", "tensor")
    p.op("vector", lambda e: e.tensor_scalar(
        ctx.erow[:, 0:512], PS[0:1, 512 * 6:512 * 6 + 512],
        CF1(ctx, 1.0 / 128.0), None, ALU.mult))
    p.op("vector", lambda e: e.tensor_mul(ctx.erow[:, 512:1024],
                                          ctx.erow[:, 0:512], ctx.erow[:, 0:512]))
    p.op("vector", lambda e: e.scalar_tensor_tensor(
        ctx.erow[:, 512:1024], PS[0:1, 512 * 7:512 * 7 + 512],
        CF1(ctx, 1.0 / 128.0), ctx.erow[:, 512:1024], ALU.mult, ALU.subtract))
    p.op("vector", lambda e: e.tensor_scalar(
        ctx.erow[:, 512:1024], ctx.erow[:, 512:1024], CF1(ctx, GN_EPS), None,
        ALU.add))
    p.op("vector", lambda e: e.reciprocal(ctx.erow[:, 512:1024],
                                          ctx.erow[:, 512:1024]))
    p.op("vector", lambda e: e.tensor_scalar(
        ctx.erow[:, 512:1024], ctx.erow[:, 512:1024],
        CF1(ctx, (1.0 - LAMBDA_INIT) ** 2), None, ALU.mult))
    p.wait_eng("scalar", "vector")
    p.op("scalar", lambda e: e.activation(ctx.erow[:, 512:1024],
                                          ctx.erow[:, 512:1024], AF.Sqrt))
    p.wait_eng("vector", "scalar")
    p.op("vector", lambda e: e.tensor_mul(ctx.erow[:, 0:512], ctx.erow[:, 0:512],
                                          ctx.erow[:, 512:1024]))
    p.wait_eng("tensor", "vector")
    p.op("tensor", lambda e: e.matmul(
        PS[:, 512 * 6:512 * 6 + 512], ctx.ones_r[0:1, :], ctx.erow[0:1, 512:1024],
        start=True, stop=True))
    p.op("tensor", lambda e: e.matmul(
        PS[:, 512 * 7:512 * 7 + 512], ctx.ones_r[0:1, :], ctx.erow[0:1, 0:512],
        start=True, stop=True))
    p.wait_eng("vector", "tensor")
    gsl = ctx.gn_all[:, 2048 * hp + 512 * blk:2048 * hp + 512 * blk + 512]
    p.op("vector", lambda e: e.tensor_mul(
        ctx.t1f[:], ctx.opf[:], PS[:, 512 * 6:512 * 6 + 512]))
    p.op("vector", lambda e, gsl=gsl: e.tensor_sub(
        gsl, ctx.t1f[:], PS[:, 512 * 7:512 * 7 + 512]))


def emit_oproj(p, ctx):
    PS = ctx.ps
    p.wait_eng("sync", "tensor")
    p.wait_eng("sync", "vector")
    p.wait_eng("sync", "scalar")
    p.wait_eng("vector", "tensor")
    for i in range(3):
        if f"dq_out{i}" in p.dcnt:
            p.wait_dma("vector", f"dq_out{i}")
    for hi in range(8):
        base = 2048 * hi
        for half in range(2):
            a = ctx.gn_all[:, base + 1024 * half:base + 1024 * half + 1024]
            bb_ = ctx.gn_all[:, base + 1024 * (1 - half):base + 1024 * (1 - half) + 1024]
            dst = ctx.gn_eff[:, base + 1024 * half:base + 1024 * half + 1024]
            p.op("vector", lambda e, bb_=bb_: e.tensor_single_scalar(
                ctx.P01[0][:], bb_, ctx.osel[:, 1:2], ALU.mult))
            p.op("vector", lambda e, a=a, dst=dst: e.scalar_tensor_tensor(
                dst, a, ctx.osel[:, 0:1], ctx.P01[0][:], ALU.mult, ALU.add))
    for dt in range(16):
        b = dt % 2
        p.wait_eng("sync", "vector", at=ctx.wo_cast_done.get(b, 0))
        src = ctx.wo.rearrange("(ac pp) d -> pp ac d", pp=128)[:, :, 128 * dt:128 * dt + 128]
        p.dma("sync", ctx.wof[b][:].rearrange("pp (ac c) -> pp ac c", c=128), src,
              f"wo{b}")
        p.wait_dma("vector", f"wo{b}")
        p.wait_eng("vector", "tensor", at=ctx.op_mm_done.get(b, 0))
        p.op("vector", lambda e, b=b: e.tensor_copy(ctx.wob[b][:], ctx.wof[b][:]))
        ctx.wo_cast_done[b] = p.ecnt["vector"]
        p.wait_eng("tensor", "vector")
        p.wait_eng("tensor", "scalar", at=ctx.op_evac_done.get(0, 0))
        for kc in range(KC):
            stat = ctx.wob[b][:, 128 * kc:128 * kc + 128]
            for hi in range(8):
                mov = ctx.gn_eff[:, 2048 * hi:2048 * hi + 2048].rearrange(
                    "p (r j) -> p j r", j=16)[:, kc, :]
                p.op("tensor", lambda e, stat=stat, mov=mov, hi=hi, kc=kc: e.matmul(
                    PS[:, 512 * hi:512 * hi + 128], stat, mov,
                    start=(kc == 0), stop=(kc == KC - 1)),
                    inc=(kc == KC - 1 and hi == 7))
        p.wait_eng("scalar", "tensor")
        for hi in range(8):
            p.op("scalar", lambda e, dt=dt, hi=hi: e.activation(
                ctx.s_bf[:, 1024 * dt + 128 * hi:1024 * dt + 128 * hi + 128],
                PS[:, 512 * hi:512 * hi + 128], AF.Copy))
        ctx.op_evac_done[0] = p.ecnt["scalar"]
        ctx.op_mm_done[b] = p.ecnt["tensor"]


def emit_ffn(p, ctx):
    PS = ctx.ps
    p.wait_eng("vector", "tensor")
    p.wait_eng("sync", "tensor")
    p.wait_eng("sync", "scalar")
    p.wait_eng("sync", "vector")
    for i in range(3):
        if f"dq_out{i}" in p.dcnt:
            p.wait_dma("sync", f"dq_out{i}")
    if getattr(ctx, "ag1_done", None) is not None:
        p.wait_eng("sync", "gpsimd", at=ctx.ag1_done)
    p.op("vector", lambda e: e.memset(ctx.dacc_a[:], 0.0))
    p.op("vector", lambda e: e.memset(ctx.dacc_b[:], 0.0))
    gu_mm_done = {}
    dn_mm_done = {}
    ld_done = {}
    for b5 in range(8):
        wsrc = ctx.ccw[b5]
        for l in range(16):
            wb = l % 2
            for mi, wt in [(0, ctx.wgt[wb]), (1, ctx.wut[wb])]:
                p.wait_eng("sync", "tensor", at=gu_mm_done.get((wb, mi), 0))
                src = wsrc[mi].rearrange("(kc pp) n -> pp kc n", pp=128)[:, :, 128 * l:128 * l + 128]
                p.dma("sync", wt[:].rearrange("pp (kc c) -> pp kc c", c=128), src,
                      f"wg{wb}{mi}")
            p.wait_dma("tensor", f"wg{wb}0")
            p.wait_dma("tensor", f"wg{wb}1")
            p.wait_eng("tensor", "scalar", at=ld_done.get("gelu", 0))
            for kc in range(KC):
                stat = ctx.wgt[wb][:, 128 * kc:128 * kc + 128]
                for tqc in range(2):
                    p.op("tensor", lambda e, stat=stat, kc=kc, tqc=tqc: e.matmul(
                        PS[:, 512 * tqc:512 * tqc + 512], stat,
                        ctx.xn_own[:, 1024 * kc + 512 * tqc:1024 * kc + 512 * tqc + 512],
                        start=(kc == 0), stop=(kc == KC - 1)),
                        inc=(kc == KC - 1 and tqc == 1))
            p.wait_eng("tensor", "scalar", at=ld_done.get("up", 0))
            for kc in range(KC):
                stat = ctx.wut[wb][:, 128 * kc:128 * kc + 128]
                for tqc in range(2):
                    p.op("tensor", lambda e, stat=stat, kc=kc, tqc=tqc: e.matmul(
                        PS[:, 1024 + 512 * tqc:1024 + 512 * tqc + 512], stat,
                        ctx.xn_own[:, 1024 * kc + 512 * tqc:1024 * kc + 512 * tqc + 512],
                        start=(kc == 0), stop=(kc == KC - 1)),
                        inc=(kc == KC - 1 and tqc == 1))
            gu_mm_done[(wb, 0)] = gu_mm_done[(wb, 1)] = p.ecnt["tensor"]
            p.wait_eng("scalar", "tensor")
            p.wait_eng("scalar", "vector", at=ld_done.get("inter", 0))
            p.op("scalar", lambda e: e.activation(ctx.ffn_gg[:], PS[:, 0:1024],
                                                  AF.Gelu_apprx_tanh))
            ld_done["gelu"] = p.ecnt["scalar"]
            p.op("scalar", lambda e: e.activation(ctx.ffn_ub[:], PS[:, 1024:2048],
                                                  AF.Copy))
            ld_done["up"] = p.ecnt["scalar"]
            p.wait_eng("vector", "scalar")
            p.op("vector", lambda e, l=l: e.tensor_mul(
                ctx.inter[:, 1024 * l:1024 * l + 1024], ctx.ffn_gg[:], ctx.ffn_ub[:]))
            ld_done["inter"] = p.ecnt["vector"]
        inter_done = p.ecnt["vector"]
        for ht in range(16):
            wb = ht % 2
            if b5 == 0 and ht == 0 and getattr(ctx, "ag2_done", None) is not None:
                p.wait_eng("sync", "gpsimd", at=ctx.ag2_done)
            p.wait_eng("sync", "tensor", at=dn_mm_done.get(wb, 0))
            src = wsrc[2].rearrange("(nc pp) hh -> pp nc hh", pp=128)[:, :, 128 * ht:128 * ht + 128]
            p.dma("sync", ctx.wdt[wb][:].rearrange("pp (nc c) -> pp nc c", c=128),
                  src, f"wd{wb}")
            p.wait_dma("tensor", f"wd{wb}")
            if ht == 0:
                p.wait_eng("tensor", "vector", at=inter_done)
            p.wait_eng("tensor", "vector", at=ld_done.get("dacc", 0))
            for nc_ in range(KC):
                stat = ctx.wdt[wb][:, 128 * nc_:128 * nc_ + 128]
                for tqc in range(2):
                    p.op("tensor", lambda e, stat=stat, nc_=nc_, tqc=tqc: e.matmul(
                        PS[:, 2048 + 512 * tqc:2048 + 512 * tqc + 512], stat,
                        ctx.inter[:, 1024 * nc_ + 512 * tqc:1024 * nc_ + 512 * tqc + 512],
                        start=(nc_ == 0), stop=(nc_ == KC - 1)),
                        inc=(nc_ == KC - 1 and tqc == 1))
            dn_mm_done[wb] = p.ecnt["tensor"]
            p.wait_eng("vector", "tensor")
            dsl = ctx.dacc_a[:, 1024 * ht:1024 * ht + 1024] if ht < 8 else \
                ctx.dacc_b[:, 1024 * (ht - 8):1024 * (ht - 8) + 1024]
            p.op("vector", lambda e, dsl=dsl: e.tensor_add(dsl, dsl,
                                                           PS[:, 2048:3072]))
            ld_done["dacc"] = p.ecnt["vector"]
    oadds = []
    for ht in range(16):
        b = ht % 2
        dsl = ctx.dacc_a[:, 1024 * ht:1024 * ht + 1024] if ht < 8 else \
            ctx.dacc_b[:, 1024 * (ht - 8):1024 * (ht - 8) + 1024]
        p.wait_eng("scalar", "vector")
        p.op("scalar", lambda e, dsl=dsl: e.activation(ctx.gbuf[:], dsl, AF.Silu))
        if ht >= 2:
            p.wait_eng("sync", "vector", at=oadds[ht - 2])
        else:
            p.wait_eng("sync", "tensor")
        p.dma("sync", ctx.xfbuf[b][:], ctx.xfm[128 * ht:128 * ht + 128, 0:1024],
              f"xf{b}")
        p.wait_eng("vector", "scalar")
        p.op("vector", lambda e, ht=ht: e.tensor_mul(
            ctx.sgtmp[:], ctx.s_bf[:, 1024 * ht:1024 * ht + 1024], ctx.gbuf[:]))
        p.wait_dma("vector", f"xf{b}")
        if ht >= 2:
            p.wait_dma("vector", f"xo{b}")
        p.op("vector", lambda e, b=b: e.tensor_add(
            ctx.obuf[b][:], ctx.xfbuf[b][:], ctx.sgtmp[:]))
        oadds.append(p.ecnt["vector"])
        p.wait_eng("scalar", "vector")
        p.dma("scalar", ctx.out_fm[128 * ht:128 * ht + 128, :], ctx.obuf[b][:],
              f"xo{b}")
    p.wait_dma("gpsimd", "xo0")
    p.wait_dma("gpsimd", "xo1")


def emit_attention(p, ctx):
    emit_lambda(p, ctx)
    emit_ln(p, ctx)
    emit_vproj(p, ctx)
    for hp in range(8):
        emit_kqproj(p, ctx, hp)
        emit_cbound(p, ctx, hp)
        for blk in range(4):
            emit_attn_block(p, ctx, hp, blk)
    emit_oproj(p, ctx)


def build_program(mode):
    nc = bass.Bass("TRN2", target_bir_lowering=False,
                   dynamic_dma_scratch_size=4096)
    ctx = make_ctx(nc, mode)
    if mode == "dequant":
        ctx.cc_in = nc.dram_tensor("cc", [3 * T, 2048], BF16, kind="ExternalOutput").ap()
    if mode in ("ffn", "fused", "layer2"):
        ctx.out_fm = nc.dram_tensor("out_fm", [2048, 1024], F32,
                                    kind="ExternalOutput").ap()
    if mode == "fused":
        ctx.dbg_s = nc.dram_tensor("dbg_s", [128, 16 * 1024], BF16,
                                   kind="ExternalOutput").ap()
        ctx.dbg_xn = nc.dram_tensor("dbg_xn", [128, 16 * 1024], BF16,
                                    kind="ExternalOutput").ap()
        ctx.dbg_gn = nc.dram_tensor("dbg_gn", [128, 16 * 1024], BF16,
                                    kind="ExternalOutput").ap()
        ctx.dbg_cc = nc.dram_tensor("dbg_cc", [24, 128, 2048], BF16,
                                    kind="ExternalOutput").ap()
        ctx.dbg_da = nc.dram_tensor("dbg_da", [128, 8 * 1024], F32,
                                    kind="ExternalOutput").ap()
    if mode == "attn":
        ctx.s_out = nc.dram_tensor("s_out", [2048, 1024], BF16, kind="ExternalOutput").ap()
        ctx.xn_out = nc.dram_tensor("xn_out", [2048, 2048], BF16, kind="ExternalOutput").ap()
        ctx.dbg_gn = nc.dram_tensor("dbg_gn", [2048, 1024], BF16, kind="ExternalOutput").ap()
        ctx.dbg_kt = nc.dram_tensor("dbg_kt", [128, 2048], BF16, kind="ExternalOutput").ap()
        ctx.dbg_v = nc.dram_tensor("dbg_v", [128, 2048], BF16, kind="ExternalOutput").ap()
        ctx.dbg_rows = nc.dram_tensor("dbg_rows", [128, 16], F32, kind="ExternalOutput").ap()
    sems = {n: nc.alloc_semaphore(n) for n in sem_names_for(mode)}

    with nc.Block() as block:
        def emit_all(p):
            reset_sched_state(ctx)
            emit_preamble(p, ctx)
            if mode == "dequant":
                emit_dequant(p, ctx)
                emit_dequant_wait_done(p, ctx)
            elif mode == "ffn":
                emit_ln(p, ctx)
                p.wait_eng("sync", "tensor")
                p.wait_eng("sync", "vector")
                for dt in range(16):
                    p.dma("sync", ctx.s_bf[:, 1024 * dt:1024 * dt + 1024],
                          ctx.s_in[128 * dt:128 * dt + 128, :], "sld")
                p.wait_dma("vector", "sld")
                emit_ffn(p, ctx)
            elif mode == "layer2":
                emit_attention(p, ctx)
                emit_ffn(p, ctx)
            elif mode == "fused":
                def maybe_ag(which):
                    k = ctx.dq_state["k"]
                    if which == 1 and ctx.ag1_done is None and k >= 64:
                        for i in range(3):
                            p.wait_dma("gpsimd", f"dq_out{i}")
                        p.op("gpsimd", lambda e: e.collective_compute(
                            "AllGather", mybir.AluOpType.bypass,
                            replica_groups=[[0, 1, 2, 3, 4, 5, 6, 7]],
                            ins=[ctx.cc_in[0:2 * T, :]], outs=[ctx.cc_gu[:]]))
                        ctx.ag1_done = p.ecnt["gpsimd"]
                    if which == 2 and ctx.ag2_done is None and k >= len(DQ_TILES):
                        for i in range(3):
                            p.wait_dma("gpsimd", f"dq_out{i}")
                        p.op("gpsimd", lambda e: e.collective_compute(
                            "AllGather", mybir.AluOpType.bypass,
                            replica_groups=[[0, 1, 2, 3, 4, 5, 6, 7]],
                            ins=[ctx.cc_in[2 * T:3 * T, :]], outs=[ctx.cc_d[:]]))
                        ctx.ag2_done = p.ecnt["gpsimd"]
                ctx.ag1_done = None
                ctx.ag2_done = None
                emit_lambda(p, ctx)
                emit_ln(p, ctx)
                emit_vproj(p, ctx, hook=lambda q: emit_dq_tiles(p, ctx, 20))
                for hp in range(8):
                    emit_kqproj(p, ctx, hp)
                    emit_cbound(p, ctx, hp)
                    for blk in range(4):
                        emit_attn_block(p, ctx, hp, blk)
                        emit_dq_tiles(p, ctx, 6)
                        maybe_ag(1)
                        maybe_ag(2)
                emit_dq_tiles(p, ctx, len(DQ_TILES))
                maybe_ag(1)
                maybe_ag(2)
                emit_oproj(p, ctx)
                emit_ffn(p, ctx)
            elif mode == "attn":
                emit_attention(p, ctx)
                p.wait_eng("sync", "scalar")
                for dt in range(16):
                    p.dma("sync", ctx.dbg_gn[128 * dt:128 * dt + 128, :],
                          ctx.gn_all[:, 1024 * dt:1024 * dt + 1024], "sout")
                for dt in range(16):
                    p.dma("sync", ctx.s_out[128 * dt:128 * dt + 128, :],
                          ctx.s_bf[:, 1024 * dt:1024 * dt + 1024], "sout")
                    p.dma("sync", ctx.xn_out[128 * dt:128 * dt + 128, 0:1024],
                          ctx.xn_own[:, 1024 * dt:1024 * dt + 1024], "sout")
                    p.dma("sync", ctx.xn_out[128 * dt:128 * dt + 128, 1024:2048],
                          ctx.xn_rest[:, 1024 * dt:1024 * dt + 1024], "sout")
                p.dma("sync", ctx.dbg_rows[0:1, 0:11], ctx.crow[0:1, 0:11], "sout")
                p.dma("sync", ctx.dbg_rows[1:2, 0:1], ctx.lam[:], "sout")
                p.dma("sync", ctx.dbg_rows[2:3, 0:1], ctx.mccol[0:1, :], "sout")
                p.wait_dma("sync", "sout")

        for name, reg in [("sync", block.sync), ("vector", block.vector),
                          ("scalar", block.scalar), ("tensor", block.tensor),
                          ("gpsimd", block.gpsimd)]:
            def body(eng, name=name):
                emit_all(Prog(nc, sems, name, eng))
            reg(body)
    return nc


def build_dequant_test():
    return build_program("dequant")


def pi2_rows(core):
    return np.arange(2048) * 2 + 4096 * (core % 4) + (1 if core >= 4 else 0)


def host_dequant_slices(core, gW, gS, uW, uS, dW, dS):
    rows = pi2_rows(core)
    gbT = np.ascontiguousarray(gW[rows, :].T)
    ubT = np.ascontiguousarray(uW[rows, :].T)
    gsT = np.ascontiguousarray(np.repeat(gS[rows, :].T, 16, axis=0))
    usT = np.ascontiguousarray(np.repeat(uS[rows, :].T, 16, axis=0))
    nb0 = 2048 * (core % 4)
    dbT = np.ascontiguousarray(dW[:, nb0:nb0 + 2048].T)
    dbT = np.ascontiguousarray(dbT[:, PI])
    j0 = 128 * (core % 4)
    dsT = np.ascontiguousarray(np.repeat(dS[:, j0:j0 + 128].T, 16, axis=0))
    dsT = np.ascontiguousarray(dsT[:, PI])
    w = 1.0 if core < 4 else 0.0
    dqsel = np.tile(np.array([[w, 1.0 - w]], np.float32), (128, 1))
    return gbT, ubT, gsT, usT, dbT, dsT, dqsel


def tau_perm(core):
    if core % 2 == 0:
        return np.arange(2048)
    return np.concatenate([np.arange(1024, 2048), np.arange(0, 1024)])


def head_order(core):
    own = np.arange(8) + 8 * (core % 2)
    rest = np.arange(8) + 8 * (1 - core % 2)
    return np.concatenate([own, rest])


def host_attn_inputs(core, x, Wq, Wk, Wv, Wo, lq1, lk1, lq2, lk2):
    b = core // 2
    tau = tau_perm(core)
    ho = head_order(core)
    qk_cols = np.concatenate([np.arange(128 * g, 128 * g + 128) for g in ho])
    v_cols = np.concatenate([np.arange(128 * g, 128 * g + 128) for g in ho])
    xfm = np.ascontiguousarray(x[b].T[PI][:, tau], np.float32)
    wq = np.ascontiguousarray(Wq[PI, :][:, qk_cols], np.float32)
    wk = np.ascontiguousarray(Wk[PI, :][:, qk_cols], np.float32)
    wv = np.ascontiguousarray(Wv[PI, :][:, v_cols], np.float32)
    wo = np.ascontiguousarray(Wo[:, PI], np.float32)
    lams = np.concatenate([lq1, lk1, lq2, lk2]).reshape(1, 256).astype(np.float32)
    masks = np.zeros((64, 128, 512), ml_dtypes.bfloat16)
    for i in range(64):
        blk, tk = i // 16, i % 16
        kpos = tau[128 * tk:128 * tk + 128][:, None]
        qpos = tau[512 * blk:512 * blk + 512][None, :]
        masks[i] = (kpos <= qpos).astype(ml_dtypes.bfloat16)
    w = 1.0 if core % 2 == 0 else 0.0
    osel = np.tile(np.array([[w, 1.0 - w]], np.float32), (128, 1))
    return {"xfm": xfm, "wq": wq, "wk": wk, "wv": wv, "wo": wo,
            "lams": lams, "masks": masks, "osel": osel}


_PROG_CACHE = {}


def _prog(mode):
    if mode not in _PROG_CACHE:
        _PROG_CACHE[mode] = build_program(mode)
    return _PROG_CACHE[mode]


def kernel(L0_attn_x, L0_attn_Wq, L0_attn_Wk, L0_attn_Wv, L0_attn_Wo,
           L0_attn_lambda_q1, L0_attn_lambda_k1, L0_attn_lambda_q2,
           L0_attn_lambda_k2, L0_ffn_gate_W_packed, L0_ffn_gate_scales,
           L0_ffn_up_W_packed, L0_ffn_up_scales, L0_ffn_down_W_packed,
           L0_ffn_down_scales):
    x = np.asarray(L0_attn_x, np.float32)
    gW = np.asarray(L0_ffn_gate_W_packed, np.int32)
    uW = np.asarray(L0_ffn_up_W_packed, np.int32)
    dW = np.asarray(L0_ffn_down_W_packed, np.int32)
    gS = np.asarray(L0_ffn_gate_scales, np.float32)
    uS = np.asarray(L0_ffn_up_scales, np.float32)
    dS = np.asarray(L0_ffn_down_scales, np.float32)

    in_maps = []
    for c in range(NCORE):
        gbT, ubT, gsT, usT, dbT, dsT, dqsel = host_dequant_slices(
            c, gW, gS, uW, uS, dW, dS)
        d = host_attn_inputs(c, x, np.asarray(L0_attn_Wq, np.float32),
                             np.asarray(L0_attn_Wk, np.float32),
                             np.asarray(L0_attn_Wv, np.float32),
                             np.asarray(L0_attn_Wo, np.float32),
                             np.asarray(L0_attn_lambda_q1, np.float32),
                             np.asarray(L0_attn_lambda_k1, np.float32),
                             np.asarray(L0_attn_lambda_q2, np.float32),
                             np.asarray(L0_attn_lambda_k2, np.float32))
        d.update({"gbT": gbT, "ubT": ubT, "gsT": gsT, "usT": usT,
                  "dbT": dbT, "dsT": dsT, "dqsel": dqsel})
        in_maps.append(d)
    res = run_bass_kernel_spmd(_prog("fused"), in_maps, core_ids=list(range(NCORE)))

    out = np.zeros((BATCH, T, D), np.float32)
    for c in range(NCORE):
        o = np.asarray(res.results[c]["out_fm"], np.float32)
        tau = tau_perm(c)
        out[c // 2, tau[:1024], :] = o[PI_INV, :].T
    return out


# version 15
# speedup vs baseline: 1.1678x; 1.0148x over previous
import math

import numpy as np
import ml_dtypes

import concourse.bass as bass
import concourse.mybir as mybir
from concourse.bass_utils import run_bass_kernel_spmd

F32 = mybir.dt.float32
BF16 = mybir.dt.bfloat16
I32 = mybir.dt.int32
ALU = mybir.AluOpType
AF = mybir.ActivationFunctionType
BFNP = ml_dtypes.bfloat16

D = 2048
T = 2048
BATCH = 4
NCORE = 8
HD = 64
NPAIR = 16
VD = 128
MLP = 16384
KC = D // 128
LAMBDA_INIT = 0.8 - 0.6 * math.exp(-0.3 * 12)
LN_EPS = 1e-5
GN_EPS = 1e-5

PI = np.concatenate([np.arange(0, D, 2), np.arange(1, D, 2)])
PI_INV = np.argsort(PI)

FVALS = [7.5, -1.0, 1.0, 0.5, -2.0, 4.0, -8.0, -10.0, 0.125,
         1.0 / 128.0, -1.0 / 128.0, LN_EPS, GN_EPS, 1.0 - LAMBDA_INIT,
         LAMBDA_INIT, 1.0 / D, 0.0, -16.0, 2.0, -0.5, 28.0, 24.0, -1e9,
         (1.0 - LAMBDA_INIT) ** 2]
IVALS = [15]
FIDX = {v: i for i, v in enumerate(FVALS)}
IIDX = {v: i for i, v in enumerate(IVALS)}


def own_qblocks(core):
    if core % 2 == 0:
        return [(0, 512), (1536, 2048)]
    return [(512, 1024), (1024, 1536)]


ENGS = ["sync", "vector", "scalar", "tensor", "gpsimd"]


class Prog:

    def __init__(self, nc, sems, eng_name, eng):
        self.nc = nc
        self.sems = sems
        self.eng_name = eng_name
        self.eng = eng
        self.ecnt = {e: 0 for e in ENGS}
        self.dcnt = {k: 0 for k in sems if not k.startswith("E_")}

    def on(self, name):
        return name == self.eng_name

    def op(self, eng_name, fn, inc=True, drain=None):
        if drain is None:
            drain = eng_name in ("vector", "scalar")
        if drain:
            self.wait_self(eng_name)
        if self.on(eng_name):
            instr = fn(self.eng)
            if inc and instr is not None:
                instr.then_inc(self.sems["E_" + eng_name], 1)
        if inc:
            self.ecnt[eng_name] += 1

    def wait_self(self, eng_name):
        if self.on(eng_name) and self.ecnt[eng_name] > 0:
            self.eng.wait_ge(self.sems["E_" + eng_name], self.ecnt[eng_name])

    def wait_eng(self, waiter, producer, at=None):
        v = self.ecnt[producer] if at is None else at
        if self.on(waiter) and v > 0:
            self.eng.wait_ge(self.sems["E_" + producer], v)

    def dma(self, eng_name, out, in_, stream):
        if self.on(eng_name):
            self.eng.dma_start(out=out, in_=in_).then_inc(self.sems[stream], 16)
        self.dcnt[stream] += 16

    def wait_dma(self, waiter, stream, at=None):
        v = self.dcnt[stream] if at is None else at
        if self.on(waiter) and v > 0:
            self.eng.wait_ge(self.sems[stream], v)

    def mark(self, *streams):
        return {k: self.dcnt[k] for k in streams}


class Ctx:
    pass


def make_consts(nc):
    cf_np = np.tile(np.array(FVALS, np.float32), (128, 1))
    ci_np = np.tile(np.array(IVALS, np.int32), (128, 1))
    cfd = nc.inline_tensor(cf_np, name="cfd").ap()
    cid = nc.inline_tensor(ci_np, name="cid").ap()
    cft = nc.alloc_sbuf_tensor("cft", [128, len(FVALS)], F32)
    cit = nc.alloc_sbuf_tensor("cit", [128, len(IVALS)], I32)
    for v, i in FIDX.items():
        nc.const_aps.register(F32, v, cft[:, i:i + 1])
    return cfd, cid, cft, cit


def CF(ctx, v):
    return ctx.cft[:, FIDX[v]:FIDX[v] + 1]


def CI(ctx, v):
    return ctx.cit[:, IIDX[v]:IIDX[v] + 1]


def CF1(ctx, v):
    return ctx.cft[0:1, FIDX[v]:FIDX[v] + 1]


LIVE_TK = {0: list(range(0, 4)) + list(range(8, 16)),
           1: list(range(0, 16)),
           2: list(range(0, 12)),
           3: list(range(0, 16))}

DQ_TILES = []
for _mi, _ntt, _nibs, _rb in [(0, 8, ("lo", "hi"), 0), (1, 8, ("lo", "hi"), T),
                              (2, 16, ("dyn",), 2 * T)]:
    for _tt in range(_ntt):
        for _h in range(4):
            DQ_TILES.append((_mi, _tt, _h, _nibs, _rb))


def emit_dq_tiles(p, ctx, n):
    F = 512
    st = ctx.dq_state
    nbuf = len(ctx.dq_bytes)
    for _ in range(n):
        if st["k"] >= len(DQ_TILES):
            return
        k = st["k"]
        mi, tt, h, nibbles, rbase = DQ_TILES[k]
        bT = [ctx.gbT, ctx.ubT, ctx.dbT][mi]
        sT = [ctx.gsT, ctx.usT, ctx.dsT][mi]
        buf = k % nbuf
        csl = slice(F * h, F * h + F)
        if k >= nbuf:
            p.wait_dma("sync", f"dq_out{buf}", at=st["oat"][k - nbuf])
        p.dma("sync", ctx.dq_bytes[buf][:],
              bT[128 * tt:128 * tt + 128, csl], f"dq_ld{buf}")
        p.dma("sync", ctx.dq_scale[buf][:],
              sT[128 * tt:128 * tt + 128, csl], f"dq_ld{buf}")
        ld_target = p.dcnt[f"dq_ld{buf}"]
        p.wait_dma("vector", f"dq_ld{buf}", at=ld_target)
        bts = ctx.dq_bytes[buf][:]
        two = nibbles == ("lo", "hi")
        if two or nibbles == ("lo",):
            p.op("vector", lambda e: e.tensor_single_scalar(
                ctx.dq_ci[:], bts, CI(ctx, 15), ALU.bitwise_and))
            p.op("vector", lambda e: e.tensor_copy(ctx.dq_cf[0][:], ctx.dq_ci[:]))
        if two or nibbles == ("hi",):
            dst = ctx.dq_cf[1] if two else ctx.dq_cf[0]
            p.op("vector", lambda e: e.tensor_scalar(
                ctx.dq_ci[:], bts, 4, CI(ctx, 15),
                ALU.logical_shift_right, ALU.bitwise_and))
            p.op("vector", lambda e, dst=dst: e.tensor_copy(dst[:], ctx.dq_ci[:]))
        if nibbles == ("dyn",):
            p.op("vector", lambda e: e.tensor_scalar(
                ctx.dq_ci[:], bts, 4, CI(ctx, 15),
                ALU.logical_shift_right, ALU.bitwise_and))
            p.op("vector", lambda e: e.tensor_copy(ctx.dq_cf[0][:], ctx.dq_ci[:]))
            p.op("vector", lambda e: e.tensor_single_scalar(
                ctx.dq_ci[:], bts, CI(ctx, 15), ALU.bitwise_and))
            p.op("vector", lambda e: e.tensor_copy(ctx.dq_cf[1][:], ctx.dq_ci[:]))
            p.op("vector", lambda e: e.tensor_single_scalar(
                ctx.dq_cf[0][:], ctx.dq_cf[0][:], ctx.dqsel[:, 1:2], ALU.mult))
            p.op("vector", lambda e: e.scalar_tensor_tensor(
                ctx.dq_cf[0][:], ctx.dq_cf[1][:], ctx.dqsel[:, 0:1],
                ctx.dq_cf[0][:], ALU.mult, ALU.add))
        slots = (0, 1) if two else (0,)
        for sl in slots:
            cf = ctx.dq_cf[sl]
            p.op("vector", lambda e, sl=sl, cf=cf: e.tensor_single_scalar(
                ctx.dq_sgn[sl][:], cf[:], CF(ctx, 7.5), ALU.is_ge))
            p.op("vector", lambda e, sl=sl: e.tensor_scalar(
                ctx.dq_sgn[sl][:], ctx.dq_sgn[sl][:], CF(ctx, -2.0), CF(ctx, 1.0),
                ALU.mult, ALU.add))
            p.op("vector", lambda e, sl=sl, cf=cf: e.scalar_tensor_tensor(
                ctx.dq_t[sl][:], ctx.dq_sgn[sl][:], CF(ctx, 4.0), cf[:],
                ALU.mult, ALU.add))
        p.wait_eng("scalar", "vector")
        for sl in slots:
            p.op("scalar", lambda e, sl=sl: e.activation(
                ctx.dq_r1[sl][:], ctx.dq_t[sl][:], AF.Relu, bias=-8.0, scale=1.0))
            p.op("scalar", lambda e, sl=sl: e.activation(
                ctx.dq_r2[sl][:], ctx.dq_t[sl][:], AF.Relu, bias=-10.0, scale=1.0))
        p.wait_eng("vector", "scalar")
        for sl in slots:
            vbuf = ctx.dq_val[buf][sl]
            p.op("vector", lambda e, sl=sl: e.scalar_tensor_tensor(
                ctx.dq_r1[sl][:], ctx.dq_r1[sl][:], CF(ctx, 0.5),
                ctx.dq_r2[sl][:], ALU.mult, ALU.add))
            p.op("vector", lambda e, sl=sl: e.scalar_tensor_tensor(
                ctx.dq_r1[sl][:], ctx.dq_t[sl][:], CF(ctx, 0.5),
                ctx.dq_r1[sl][:], ALU.mult, ALU.add))
            p.op("vector", lambda e, sl=sl: e.scalar_tensor_tensor(
                ctx.dq_r1[sl][:], ctx.dq_r1[sl][:], CF(ctx, -2.0),
                ctx.dq_sgn[sl][:], ALU.add, ALU.mult))
            p.op("vector", lambda e, sl=sl, vbuf=vbuf: e.tensor_tensor(
                vbuf[:], ctx.dq_r1[sl][:], ctx.dq_scale[buf][:], ALU.mult))
        p.wait_eng("scalar", "vector")
        for nib in nibbles:
            if mi < 2:
                row0 = rbase + (0 if nib == "lo" else 1024) + 128 * tt
            else:
                row0 = rbase + 128 * tt
            vbuf = ctx.dq_val[buf][0 if nib != "hi" else 1]
            p.dma("scalar", ctx.cc_in[row0:row0 + 128, csl], vbuf[:],
                  f"dq_out{buf}")
        st["oat"].append(p.dcnt[f"dq_out{buf}"])
        st["k"] = k + 1


def emit_dequant(p, ctx):
    emit_dq_tiles(p, ctx, len(DQ_TILES))


def emit_dequant_wait_done(p, ctx):
    for i in range(3):
        p.wait_dma("gpsimd", f"dq_out{i}")


A0 = 5120
A2 = A0 + 32 * 1024
B0 = A2 + 32 * 1024
C0 = B0 + 64 * 1024
D0 = C0 + 32 * 1024
E0 = D0 + 32 * 1024


def make_ctx(nc, mode):
    ctx = Ctx()
    ctx.mode = mode
    at = nc.alloc_sbuf_tensor_at

    def T_(name, off, shape, dt):
        return at(name, shape, dt, offset=off)

    dram = lambda n, sh, dt: nc.dram_tensor(n, sh, dt, kind="ExternalInput").ap()
    if mode in ("dequant", "fused"):
        ctx.gbT = dram("gbT", [1024, 2048], I32)
        ctx.ubT = dram("ubT", [1024, 2048], I32)
        ctx.dbT = dram("dbT", [2048, 2048], I32)
        ctx.gsT = dram("gsT", [1024, 2048], F32)
        ctx.usT = dram("usT", [1024, 2048], F32)
        ctx.dsT = dram("dsT", [2048, 2048], F32)
        ctx.dqsel_d = dram("dqsel", [128, 2], F32)
    if mode in ("attn", "fused", "ffn", "layer2"):
        ctx.xfm = dram("xfm", [2048, 2048], F32)
    if mode == "ffn":
        ctx.ccw_d = dram("ccw", [8, 3, 2048, 2048], BF16)
        ctx.s_in = dram("s_in", [2048, 1024], BF16)
    if mode == "layer2":
        ctx.ccw_d = dram("ccw", [8, 3, 2048, 2048], BF16)
    if mode in ("ffn", "layer2"):
        ctx.ccw = [[ctx.ccw_d[b5, mi] for mi in range(3)] for b5 in range(8)]
    if mode == "fused":
        ctx.cc_in = nc.dram_tensor("cc_in", [3 * T, 2048], BF16,
                                   kind="Internal").ap()
        ctx.cc_gu = nc.dram_tensor("cc_gu", [8 * 2 * T, 2048], BF16,
                                   kind="Internal", addr_space="Shared").ap()
        ctx.cc_d = nc.dram_tensor("cc_d", [8 * T, 2048], BF16,
                                  kind="Internal", addr_space="Shared").ap()
        ctx.ccw = [[ctx.cc_gu[(2 * b5 + 0) * T:(2 * b5 + 1) * T, :],
                    ctx.cc_gu[(2 * b5 + 1) * T:(2 * b5 + 2) * T, :],
                    ctx.cc_d[b5 * T:(b5 + 1) * T, :]] for b5 in range(8)]
    if mode in ("attn", "fused", "layer2"):
        ctx.wq = dram("wq", [2048, 2048], F32)
        ctx.wk = dram("wk", [2048, 2048], F32)
        ctx.wv = dram("wv", [2048, 2048], F32)
        ctx.wo = dram("wo", [2048, 2048], F32)
        ctx.lam_d = dram("lams", [1, 256], F32)
        ctx.osel_d = dram("osel", [128, 2], F32)
        ctx.masks_d = dram("masks", [64, 128, 512], BF16)

    cf_np = np.tile(np.array(FVALS, np.float32), (128, 1))
    ci_np = np.tile(np.array(IVALS, np.int32), (128, 1))
    ctx.cfd = nc.inline_tensor(cf_np, name="cfd").ap()
    ctx.cid = nc.inline_tensor(ci_np, name="cid").ap()
    ctx.ones_d = nc.inline_tensor(np.ones((128, 2), np.float32), name="ones_d").ap()
    ctx.onesb_d = nc.inline_tensor(np.ones((128, 2), ml_dtypes.bfloat16),
                                   name="onesb_d").ap()
    ctx.onesr_d = nc.inline_tensor(np.ones((1, 128), np.float32), name="onesr_d").ap()

    e = E0
    def Es(name, shape, dt):
        nonlocal e
        e = (e + 31) & ~31
        t = T_(name, e, shape, dt)
        e += shape[1] * mybir.dt.size(dt)
        return t
    ctx.cft = Es("cft", [128, len(FVALS)], F32)
    ctx.cit = Es("cit", [128, len(IVALS)], I32)
    for v, i in FIDX.items():
        nc.const_aps.register(F32, v, ctx.cft[:, i:i + 1])
    ctx.ones_f = Es("ones_f", [128, 2], F32)
    ctx.ones_b = Es("ones_b", [128, 2], BF16)
    ctx.ones_r = Es("ones_r", [1, 128], F32)
    if mode in ("dequant", "fused"):
        ctx.dqsel = Es("dqsel_t", [128, 2], F32)
    if mode in ("attn", "fused", "ffn", "layer2"):
        ctx.wqkf = Es("wqkf", [128, 2048], F32)
        ctx.wqb = Es("wqb", [128, 2048], BF16)
        ctx.wkb = Es("wkb", [128, 2048], BF16)
        ctx.P01 = [Es(f"P01_{i}", [128, 1024], BF16) for i in range(2)]
        ctx.maskb = [Es(f"maskb{i}", [128, 512], BF16) for i in range(2)]
        ctx.crow = Es("crow", [1, 16], F32)
        ctx.lrows = Es("lrows", [1, 256], F32)
        ctx.lscr = Es("lscr", [1, 128], F32)
        ctx.lam = Es("lam", [1, 1], F32)
        ctx.mlam = Es("mlam", [1, 1], F32)
        ctx.mccol = Es("mccol", [128, 1], F32)
        ctx.osel = Es("osel_t", [128, 2], F32)
        ctx.wob = [ctx.wqb, ctx.wkb]
    ctx.e_end = e
    assert e <= 229376, f"E region overflow: {e}"

    if mode in ("attn", "fused", "ffn", "layer2"):
        ctx.xn_own = T_("xn_own", A0, [128, 16 * 1024], BF16)
        ctx.xn_rest = T_("xn_rest", A2, [128, 16 * 1024], BF16)
        ctx.murow = T_("murow", B0, [1, 2048], F32)
        ctx.isrow = T_("isrow", B0 + 8192, [1, 2048], F32)
        ctx.rtmp = T_("rtmp", B0 + 16384, [1, 2048], F32)
        ctx.V = T_("V", B0, [128, 16 * 1024], BF16)
        ctx.s_bf = T_("s_bf", B0, [128, 16 * 1024], BF16)
        ctx.gn_eff = T_("gn_eff", D0, [128, 16 * 1024], BF16)
        ctx.xbuf = [T_(f"xbuf{i}", C0 + i * 8192, [128, 2048], F32) for i in range(2)]
        ctx.x2 = T_("x2", C0 + 16384, [128, 2048], BF16)
        ctx.ttmp = T_("ttmp", C0 + 20480, [128, 2048], F32)
        ctx.wvf = [T_(f"wvf{i}", C0 + 28672 + i * 2048, [128, 512], F32)
                   for i in range(2)]
        ctx.wvb = T_("wvb", C0, [128, 16 * 512], BF16)
        ctx.gn_all = T_("gn_all", C0, [128, 16 * 1024], BF16)
        dd = B0 + 32 * 1024
        dd0 = dd
        def Dm(name, shape, dt):
            nonlocal dd
            dd = (dd + 31) & ~31
            t = T_(name, dd, shape, dt)
            dd += shape[1] * mybir.dt.size(dt)
            return t
        ctx.kt_off = (dd + 31) & ~31
        ctx.KT = Dm("KT", [128, 2048], BF16)
        ctx.QT = Dm("QT", [128, 2048], BF16)
        ctx.Q2 = Dm("Q2", [128, 2048], BF16)
        ctx.K2 = Dm("K2", [128, 2048], BF16)
        ctx.t1f = Dm("t1f", [128, 512], F32)
        ctx.opf = Dm("opf", [128, 512], F32)
        ctx.sqb = Dm("sqb", [128, 512], BF16)
        ctx.ou0 = Dm("ou0", [128, 512], F32)
        ctx.ou1 = Dm("ou1", [128, 512], F32)
        ctx.erow = Dm("erow", [1, 1024], F32)
        assert dd <= C0, f"B attn block overflow: {dd}"
        ctx.wof2 = T_("wof2", ctx.kt_off, [128, 2048], F32)
        ctx.wof = [ctx.wqkf, ctx.wof2]
    if mode in ("dequant", "fused"):
        d = D0
        def Ds(name, shape, dt):
            nonlocal d
            d = (d + 31) & ~31
            t = T_(name, d, shape, dt)
            d += shape[1] * mybir.dt.size(dt)
            return t
        ctx.dq_bytes = [Ds(f"dqb{i}", [128, 512], I32) for i in range(3)]
        ctx.dq_scale = [Ds(f"dqs{i}", [128, 512], F32) for i in range(3)]
        ctx.dq_val = [[Ds(f"dqv{i}{j}", [128, 512], BF16) for j in range(2)]
                      for i in range(3)]
        ctx.dq_ci = Ds("dqci", [128, 512], I32)
        ctx.dq_cf = [Ds(f"dqcf{i}", [128, 512], BF16) for i in range(2)]
        ctx.dq_sgn = [Ds(f"dqsg{i}", [128, 512], BF16) for i in range(2)]
        ctx.dq_t = [Ds(f"dqt{i}", [128, 512], BF16) for i in range(2)]
        ctx.dq_r1 = [Ds(f"dqr1{i}", [128, 512], BF16) for i in range(2)]
        ctx.dq_r2 = [Ds(f"dqr2{i}", [128, 512], BF16) for i in range(2)]
        assert d <= D0 + 32 * 1024, f"dequant arena overflow {d - D0}"

    if mode in ("ffn", "fused", "layer2"):
        ctx.dacc_a = T_("dacc_a", A2, [128, 8 * 1024], F32)
        ctx.dacc_b = T_("dacc_b", B0 + 32 * 1024, [128, 8 * 1024], F32)
        ctx.inter = T_("inter", C0, [128, 16 * 1024], BF16)
        ctx.wgt = [T_(f"wgt{i}", D0 + 4096 * i, [128, 2048], BF16) for i in range(2)]
        ctx.wut = [T_(f"wut{i}", D0 + 8192 + 4096 * i, [128, 2048], BF16)
                   for i in range(2)]
        ctx.wdt = [T_(f"wdt{i}", D0 + 16384 + 4096 * i, [128, 2048], BF16)
                   for i in range(2)]
        ctx.ffn_gg = T_("ffn_gg", D0 + 24576, [128, 1024], BF16)
        ctx.ffn_ub = T_("ffn_ub", D0 + 26624, [128, 1024], BF16)
        ctx.gbuf = T_("gbuf", D0 + 16384, [128, 1024], BF16)
        ctx.sgtmp = T_("sgtmp", D0 + 24576, [128, 1024], F32)
        ctx.obuf = [T_("obuf0", ctx.wqkf.manual_sbuf_range[0], [128, 1024], F32),
                    T_("obuf1", ctx.wqkf.manual_sbuf_range[0] + 4096,
                       [128, 1024], F32)]
        ctx.xfbuf = [T_("xfbuf0", ctx.wqb.manual_sbuf_range[0], [128, 1024], F32),
                     T_("xfbuf1", ctx.wkb.manual_sbuf_range[0], [128, 1024], F32)]

    ctx.ps = nc.alloc_psum_tensor("ps", [128, 4096], F32)

    reset_sched_state(ctx)

    def xn_sl(kc, t0, w=512):
        if t0 < 1024:
            return ctx.xn_own[:, 1024 * kc + t0:1024 * kc + t0 + w]
        return ctx.xn_rest[:, 1024 * kc + (t0 - 1024):1024 * kc + (t0 - 1024) + w]
    ctx.xn_sl = xn_sl
    return ctx


def reset_sched_state(ctx):
    ctx.xs_pe_done = {}
    ctx.xs_act_done = {}
    ctx.x2_pe_done = 0
    ctx.xa_dve_done = {}
    ctx.bcast_cons = {}
    ctx.wv_cast_done = {}
    ctx.vp_evac_done = {}
    ctx.wqk_cast_done = 0
    ctx.kq_evac_done = 0
    ctx.cb_red_done = {}
    ctx.mask_cons = {}
    ctx.exp_done = {}
    ctx.wo_cast_done = {}
    ctx.op_evac_done = {}
    ctx.op_mm_done = {}
    ctx.av_done = 0
    ctx.dq_vec_done = 0
    ctx.dq_act_done = 0
    ctx.kq_mm_done = 0
    ctx.dq_state = {"k": 0, "oat": []}
    ctx.score_done = {}
    ctx.mask_done = {}


def sem_names_for(mode):
    names = ["consts"] + ["E_" + x for x in ENGS]
    if mode in ("dequant", "fused"):
        names += [f"dq_ld{i}" for i in range(3)] + [f"dq_out{i}" for i in range(3)]
    if mode in ("attn", "fused", "ffn", "layer2"):
        names += ["xs0", "xs1", "xa0", "xa1", "wv0", "wv1", "wqk",
                  "mk0", "mk1", "wo0", "wo1", "sout"]
    if mode in ("ffn", "fused", "layer2"):
        names += ["wg00", "wg01", "wg10", "wg11", "wd0", "wd1",
                  "xf0", "xf1", "xo0", "xo1", "sld"]
    return names


def emit_preamble(p, ctx):
    p.dma("sync", ctx.cft[:], ctx.cfd, "consts")
    p.dma("sync", ctx.cit[:], ctx.cid, "consts")
    p.dma("sync", ctx.ones_f[:], ctx.ones_d, "consts")
    p.dma("sync", ctx.ones_b[:], ctx.onesb_d, "consts")
    p.dma("sync", ctx.ones_r[:], ctx.onesr_d, "consts")
    if hasattr(ctx, "dqsel_d"):
        p.dma("sync", ctx.dqsel[:], ctx.dqsel_d, "consts")
    if hasattr(ctx, "lam_d"):
        p.dma("sync", ctx.lrows[:], ctx.lam_d, "consts")
        p.dma("sync", ctx.osel[:], ctx.osel_d, "consts")
    for e in ENGS[1:]:
        p.wait_dma(e, "consts")
    p.wait_dma("sync", "consts")


def emit_lambda(p, ctx):
    for i in (0, 1):
        q = ctx.lrows[:, 64 * (2 * i):64 * (2 * i) + 64]
        kk = ctx.lrows[:, 64 * (2 * i + 1):64 * (2 * i + 1) + 64]
        p.op("vector", lambda e, q=q, kk=kk: e.tensor_mul(ctx.lscr[:, 0:64], q, kk))
        p.op("vector", lambda e, i=i: e.reduce_sum(
            ctx.lscr[:, 64 + i:65 + i], ctx.lscr[:, 0:64],
            axis=mybir.AxisListType.X))
    p.wait_eng("scalar", "vector")
    p.op("scalar", lambda e: e.activation(
        ctx.lscr[:, 66:68], ctx.lscr[:, 64:66], AF.Exp))
    p.wait_eng("vector", "scalar")
    p.op("vector", lambda e: e.tensor_sub(
        ctx.lam[:, 0:1], ctx.lscr[:, 66:67], ctx.lscr[:, 67:68]))
    p.op("vector", lambda e: e.tensor_scalar(
        ctx.lam[:, 0:1], ctx.lam[:, 0:1], CF1(ctx, LAMBDA_INIT), None, ALU.add))
    p.op("vector", lambda e: e.tensor_scalar(
        ctx.mlam[:, 0:1], ctx.lam[:, 0:1], CF1(ctx, -1.0), None, ALU.mult))


def emit_ln(p, ctx):
    PS = ctx.ps
    for kc in range(KC):
        b = kc % 2
        if kc >= 2:
            p.wait_eng("sync", "tensor", at=ctx.xs_pe_done[kc - 2])
            p.wait_eng("sync", "scalar", at=ctx.xs_act_done[kc - 2])
        p.dma("sync", ctx.xbuf[b][:], ctx.xfm[128 * kc:128 * kc + 128, :], f"xs{b}")
        p.wait_dma("scalar", f"xs{b}")
        p.wait_eng("scalar", "tensor", at=ctx.x2_pe_done)
        p.op("scalar", lambda e, b=b: e.activation(ctx.x2[:, :], ctx.xbuf[b][:],
                                                   AF.Square))
        ctx.xs_act_done[kc] = p.ecnt["scalar"]
        p.wait_dma("tensor", f"xs{b}")
        for tc in range(4):
            sl = slice(512 * tc, 512 * tc + 512)
            p.op("tensor", lambda e, b=b, sl=sl, tc=tc, kc=kc: e.matmul(
                PS[0:1, 512 * tc:512 * tc + 512], ctx.ones_f[:, 0:1],
                ctx.xbuf[b][:, sl], start=(kc == 0), stop=(kc == KC - 1)),
                inc=(tc == 3))
        p.wait_eng("tensor", "scalar")
        for tc in range(4):
            sl = slice(512 * tc, 512 * tc + 512)
            p.op("tensor", lambda e, sl=sl, tc=tc, kc=kc: e.matmul(
                PS[0:1, 2048 + 512 * tc:2048 + 512 * tc + 512], ctx.ones_b[:, 0:1],
                ctx.x2[:, sl], start=(kc == 0), stop=(kc == KC - 1)),
                inc=(tc == 3))
        ctx.xs_pe_done[kc] = p.ecnt["tensor"]
        ctx.x2_pe_done = p.ecnt["tensor"]
    p.wait_eng("vector", "tensor")
    p.op("vector", lambda e: e.tensor_scalar(
        ctx.murow[:, :], PS[0:1, 0:2048], CF1(ctx, 1.0 / D), None, ALU.mult))
    p.op("vector", lambda e: e.tensor_scalar(
        ctx.isrow[:, :], PS[0:1, 2048:4096], CF1(ctx, 1.0 / D), None, ALU.mult))
    p.op("vector", lambda e: e.tensor_mul(ctx.rtmp[:, :], ctx.murow[:, :],
                                          ctx.murow[:, :]))
    p.op("vector", lambda e: e.tensor_sub(ctx.isrow[:, :], ctx.isrow[:, :],
                                          ctx.rtmp[:, :]))
    p.op("vector", lambda e: e.tensor_scalar(
        ctx.isrow[:, :], ctx.isrow[:, :], CF1(ctx, LN_EPS), None, ALU.add))
    p.op("vector", lambda e: e.reciprocal(ctx.isrow[:, :], ctx.isrow[:, :]))
    p.wait_eng("scalar", "vector")
    p.op("scalar", lambda e: e.activation(ctx.isrow[:, :], ctx.isrow[:, :], AF.Sqrt))
    stats_t = p.ecnt["tensor"]
    stats_s = p.ecnt["scalar"]
    for kc in range(KC):
        b = kc % 2
        if kc >= 2:
            p.wait_eng("sync", "vector", at=ctx.xa_dve_done[kc - 2])
        else:
            p.wait_eng("sync", "tensor", at=stats_t)
            p.wait_eng("sync", "scalar", at=stats_s)
        p.dma("sync", ctx.xbuf[b][:], ctx.xfm[128 * kc:128 * kc + 128, :], f"xa{b}")
        p.wait_dma("tensor", f"xa{b}")
        p.wait_eng("tensor", "scalar")
        for tc in range(4):
            bb = 2 * (tc % 2)
            p.wait_eng("tensor", "vector", at=ctx.bcast_cons.get(bb, 0))
            p.op("tensor", lambda e, tc=tc, bb=bb: e.matmul(
                PS[:, 512 * bb:512 * bb + 512], ctx.ones_r[0:1, :],
                ctx.murow[0:1, 512 * tc:512 * tc + 512], start=True, stop=True))
            p.op("tensor", lambda e, tc=tc, bb=bb: e.matmul(
                PS[:, 512 * (bb + 1):512 * (bb + 1) + 512], ctx.ones_r[0:1, :],
                ctx.isrow[0:1, 512 * tc:512 * tc + 512], start=True, stop=True))
            p.wait_eng("vector", "tensor")
            sl = slice(512 * tc, 512 * tc + 512)
            xnsl = ctx.xn_sl(kc, 512 * tc)
            p.op("vector", lambda e, b=b, sl=sl, bb=bb: e.tensor_sub(
                ctx.ttmp[:, 0:512], ctx.xbuf[b][:, sl], PS[:, 512 * bb:512 * bb + 512]))
            p.op("vector", lambda e, xnsl=xnsl, bb=bb: e.tensor_mul(
                xnsl, ctx.ttmp[:, 0:512], PS[:, 512 * (bb + 1):512 * (bb + 1) + 512]))
            ctx.bcast_cons[bb] = p.ecnt["vector"]
        ctx.xa_dve_done[kc] = p.ecnt["vector"]


def emit_vproj(p, ctx, hook=None):
    PS = ctx.ps
    vq_pe_done = 0
    casts = []
    for quarter in range(2):
        vc0 = 512 * quarter
        for kc in range(KC):
            b = kc % 2
            gi = 16 * quarter + kc
            if gi >= 2:
                p.wait_eng("sync", "vector", at=casts[gi - 2])
            src = ctx.wv.rearrange("(kc pp) a -> pp kc a", pp=128)[:, kc, vc0:vc0 + 512]
            p.dma("sync", ctx.wvf[b][:, 0:512], src, f"wv{b}")
            p.wait_dma("vector", f"wv{b}")
            if quarter > 0:
                p.wait_eng("vector", "tensor", at=vq_pe_done)
            p.op("vector", lambda e, b=b, kc=kc: e.tensor_copy(
                ctx.wvb[:, 512 * kc:512 * kc + 512], ctx.wvf[b][:, 0:512]))
            casts.append(p.ecnt["vector"])
        p.wait_eng("tensor", "vector")
        for tt in range(16):
            bank = 4 + (tt % 2)
            p.wait_eng("tensor", "scalar", at=ctx.vp_evac_done.get(bank, 0))
            for kc in range(KC):
                stat = ctx.xn_sl(kc, 128 * tt, w=128)
                p.op("tensor", lambda e, stat=stat, kc=kc, bank=bank: e.matmul(
                    PS[:, 512 * bank:512 * bank + 512], stat,
                    ctx.wvb[:, 512 * kc:512 * kc + 512],
                    start=(kc == 0), stop=(kc == KC - 1)), inc=(kc == KC - 1))
            p.wait_eng("scalar", "tensor")
            p.op("scalar", lambda e, tt=tt, vc0=vc0, bank=bank: e.activation(
                ctx.V[:, 1024 * tt + vc0:1024 * tt + vc0 + 512],
                PS[:, 512 * bank:512 * bank + 512], AF.Copy))
            ctx.vp_evac_done[bank] = p.ecnt["scalar"]
        vq_pe_done = p.ecnt["tensor"]
        if hook is not None:
            hook(quarter)


def emit_kqproj(p, ctx, hp):
    PS = ctx.ps
    for wi, (wdram, wbt) in enumerate([(ctx.wq, ctx.wqb), (ctx.wk, ctx.wkb)]):
        p.wait_eng("sync", "vector", at=ctx.wqk_cast_done)
        src = wdram.rearrange("(kc pp) a -> pp kc a", pp=128)[:, :, 128 * hp:128 * hp + 128]
        p.dma("sync", ctx.wqkf[:].rearrange("pp (kc c) -> pp kc c", c=128), src, "wqk")
        p.wait_dma("vector", "wqk")
        p.wait_eng("vector", "tensor", at=ctx.kq_mm_done)
        p.op("vector", lambda e, wbt=wbt: e.tensor_copy(wbt[:], ctx.wqkf[:]))
        ctx.wqk_cast_done = p.ecnt["vector"]
    p.wait_eng("tensor", "vector", at=ctx.wqk_cast_done)
    p.wait_eng("tensor", "scalar")
    for dst, wbt, nch, scale, bank in [(ctx.KT, ctx.wkb, 4, None, 0),
                                       (ctx.QT, ctx.wqb, 4, 0.125, 1)]:
        for tch in range(nch):
            p.wait_eng("tensor", "scalar", at=ctx.kq_evac_done)
            for kc in range(KC):
                stat = wbt[:, 128 * kc:128 * kc + 128]
                mov = ctx.xn_sl(kc, 512 * tch, w=512)
                p.op("tensor", lambda e, stat=stat, mov=mov, kc=kc, bank=bank: e.matmul(
                    PS[:, 512 * bank:512 * bank + 512], stat, mov,
                    start=(kc == 0), stop=(kc == KC - 1)), inc=(kc == KC - 1))
            p.wait_eng("scalar", "tensor")
            if scale is None:
                p.op("scalar", lambda e, dst=dst, tch=tch, bank=bank: e.activation(
                    dst[:, 512 * tch:512 * tch + 512],
                    PS[:, 512 * bank:512 * bank + 512], AF.Copy))
            else:
                p.op("scalar", lambda e, dst=dst, tch=tch, bank=bank, scale=scale:
                     e.activation(dst[:, 512 * tch:512 * tch + 512],
                                  PS[:, 512 * bank:512 * bank + 512], AF.Copy,
                                  scale=scale))
            ctx.kq_evac_done = p.ecnt["scalar"]
    ctx.kq_mm_done = p.ecnt["tensor"]


def emit_cbound(p, ctx, hp):
    PS = ctx.ps
    p.op("scalar", lambda e: e.activation(ctx.Q2[:], ctx.QT[:], AF.Square))
    p.op("scalar", lambda e: e.activation(ctx.K2[:], ctx.KT[:], AF.Square))
    p.wait_eng("tensor", "scalar")
    nm = 0
    for src, nch in [(ctx.Q2, 4), (ctx.K2, 4)]:
        for i in range(nch):
            bank = 4 if nm % 2 == 0 else 5
            p.wait_eng("tensor", "vector", at=ctx.cb_red_done.get(bank, 0))
            p.op("tensor", lambda e, src=src, i=i, bank=bank: e.matmul(
                PS[0:1, 512 * bank:512 * bank + 512], ctx.ones_b[:, 0:1],
                src[:, 512 * i:512 * i + 512], start=True, stop=True))
            p.wait_eng("vector", "tensor")
            p.op("vector", lambda e, nm=nm, bank=bank: e.reduce_max(
                ctx.crow[:, nm:nm + 1], PS[0:1, 512 * bank:512 * bank + 512],
                axis=mybir.AxisListType.X))
            ctx.cb_red_done[bank] = p.ecnt["vector"]
            nm += 1
    p.op("vector", lambda e: e.reduce_max(ctx.crow[:, 8:9], ctx.crow[:, 0:4],
                                          axis=mybir.AxisListType.X))
    p.op("vector", lambda e: e.reduce_max(ctx.crow[:, 9:10], ctx.crow[:, 4:8],
                                          axis=mybir.AxisListType.X))
    p.op("vector", lambda e: e.tensor_mul(ctx.crow[:, 8:9], ctx.crow[:, 8:9],
                                          ctx.crow[:, 9:10]))
    p.wait_eng("scalar", "vector")
    p.op("scalar", lambda e: e.activation(ctx.crow[:, 10:11], ctx.crow[:, 8:9],
                                          AF.Sqrt))
    p.wait_eng("vector", "scalar")
    p.op("vector", lambda e: e.tensor_scalar(
        ctx.crow[:, 10:11], ctx.crow[:, 10:11], CF1(ctx, -1.0), None, ALU.mult))
    p.wait_eng("tensor", "vector")
    p.op("tensor", lambda e: e.matmul(
        PS[:, 512 * 4:512 * 4 + 1], ctx.ones_r[0:1, :], ctx.crow[0:1, 10:11],
        start=True, stop=True))
    p.wait_eng("scalar", "tensor")
    p.op("scalar", lambda e: e.activation(ctx.mccol[:, 0:1],
                                          PS[:, 512 * 4:512 * 4 + 1], AF.Copy))


def emit_attn_block(p, ctx, hp, blk):
    PS = ctx.ps
    q0 = 512 * blk
    live = LIVE_TK[blk]
    last = len(live) - 1

    def emit_scores(idx):
        tk = live[idx]
        it = 16 * blk + tk
        mb = idx % 2
        sb = 2 * (idx % 2)
        p.wait_eng("sync", "vector", at=ctx.mask_cons.get(mb, 0))
        p.dma("sync", ctx.maskb[mb][:], ctx.masks_d[it], f"mk{mb}")
        p.wait_eng("tensor", "scalar", at=ctx.exp_done.get(idx % 2, 0))
        p.op("tensor", lambda e, tk=tk, q0=q0, sb=sb: e.matmul(
            PS[:, 512 * sb:512 * sb + 512],
            ctx.KT[0:64, 128 * tk:128 * tk + 128],
            ctx.QT[0:64, q0:q0 + 512], start=True, stop=True))
        p.op("tensor", lambda e, tk=tk, q0=q0, sb=sb: e.matmul(
            PS[:, 512 * (sb + 1):512 * (sb + 1) + 512],
            ctx.KT[64:128, 128 * tk:128 * tk + 128],
            ctx.QT[64:128, q0:q0 + 512], start=True, stop=True,
            tile_position=(64, 0)))
        ctx.score_done[idx] = p.ecnt["tensor"]
        pb = ctx.P01[idx % 2]
        p.wait_eng("scalar", "tensor", at=ctx.score_done[idx])
        p.op("scalar", lambda e, pb=pb, sb=sb: e.activation(
            pb[:, 0:512], PS[:, 512 * sb:512 * sb + 512], AF.Exp,
            bias=ctx.mccol[:, 0:1]))
        p.op("scalar", lambda e, pb=pb, sb=sb: e.activation(
            pb[:, 512:1024], PS[:, 512 * (sb + 1):512 * (sb + 1) + 512], AF.Exp,
            bias=ctx.mccol[:, 0:1]))
        ctx.exp_done[idx % 2] = p.ecnt["scalar"]
        p.wait_dma("vector", f"mk{mb}")
        p.wait_eng("vector", "scalar")
        p.op("vector", lambda e, pb=pb, mb=mb: e.tensor_mul(
            pb[:, 0:512], pb[:, 0:512], ctx.maskb[mb][:]))
        p.op("vector", lambda e, pb=pb, mb=mb: e.tensor_mul(
            pb[:, 512:1024], pb[:, 512:1024], ctx.maskb[mb][:]))
        ctx.mask_cons[mb] = p.ecnt["vector"]
        ctx.mask_done[idx] = p.ecnt["vector"]

    def emit_zav(idx):
        tk = live[idx]
        pb = ctx.P01[idx % 2]
        p.wait_eng("tensor", "vector", at=ctx.mask_done[idx])
        p.op("tensor", lambda e, pb=pb, idx=idx: e.matmul(
            PS[0:1, 512 * 6:512 * 6 + 512], ctx.ones_b[:, 0:1], pb[:, 0:512],
            start=(idx == 0), stop=(idx == last)))
        p.op("tensor", lambda e, pb=pb, idx=idx: e.matmul(
            PS[0:1, 512 * 7:512 * 7 + 512], ctx.ones_b[:, 0:1], pb[:, 512:1024],
            start=(idx == 0), stop=(idx == last)))
        vstat = ctx.V[:, 1024 * tk + 128 * hp:1024 * tk + 128 * hp + 128]
        p.op("tensor", lambda e, pb=pb, vstat=vstat, idx=idx: e.matmul(
            PS[:, 512 * 4:512 * 4 + 512], vstat, pb[:, 0:512],
            start=(idx == 0), stop=(idx == last)))
        p.op("tensor", lambda e, pb=pb, vstat=vstat, idx=idx: e.matmul(
            PS[:, 512 * 5:512 * 5 + 512], vstat, pb[:, 512:1024],
            start=(idx == 0), stop=(idx == last)))
        ctx.av_done = p.ecnt["tensor"]

    for idx in range(len(live)):
        emit_scores(idx)
        if idx > 0:
            emit_zav(idx - 1)
    emit_zav(last)

    p.wait_eng("vector", "tensor", at=ctx.av_done)
    p.op("vector", lambda e: e.reciprocal(ctx.erow[:, 0:512],
                                          PS[0:1, 512 * 6:512 * 6 + 512]))
    p.op("vector", lambda e: e.reciprocal(ctx.erow[:, 512:1024],
                                          PS[0:1, 512 * 7:512 * 7 + 512]))
    p.op("vector", lambda e: e.tensor_scalar(
        ctx.erow[:, 512:1024], ctx.erow[:, 512:1024], ctx.lam[0:1, 0:1], None,
        ALU.mult))
    p.wait_eng("tensor", "vector")
    p.op("tensor", lambda e: e.matmul(
        PS[:, 512 * 6:512 * 6 + 512], ctx.ones_r[0:1, :], ctx.erow[0:1, 0:512],
        start=True, stop=True))
    p.op("tensor", lambda e: e.matmul(
        PS[:, 512 * 7:512 * 7 + 512], ctx.ones_r[0:1, :], ctx.erow[0:1, 512:1024],
        start=True, stop=True))
    p.wait_eng("scalar", "tensor")
    p.op("scalar", lambda e: e.activation(ctx.ou0[:], PS[:, 512 * 4:512 * 4 + 512],
                                          AF.Copy))
    p.op("scalar", lambda e: e.activation(ctx.ou1[:], PS[:, 512 * 5:512 * 5 + 512],
                                          AF.Copy))
    p.wait_eng("vector", "scalar")
    p.op("vector", lambda e: e.tensor_mul(
        ctx.t1f[:], ctx.ou1[:], PS[:, 512 * 7:512 * 7 + 512]))
    p.op("vector", lambda e: e.tensor_mul(
        ctx.opf[:], ctx.ou0[:], PS[:, 512 * 6:512 * 6 + 512]))
    p.op("vector", lambda e: e.tensor_sub(
        ctx.opf[:], ctx.opf[:], ctx.t1f[:]))
    p.wait_eng("scalar", "vector")
    p.op("scalar", lambda e: e.activation(ctx.sqb[:], ctx.opf[:], AF.Square))
    p.wait_eng("tensor", "scalar")
    p.op("tensor", lambda e: e.matmul(
        PS[0:1, 512 * 6:512 * 6 + 512], ctx.ones_f[:, 0:1], ctx.opf[:],
        start=True, stop=True))
    p.op("tensor", lambda e: e.matmul(
        PS[0:1, 512 * 7:512 * 7 + 512], ctx.ones_b[:, 0:1], ctx.sqb[:],
        start=True, stop=True))
    p.wait_eng("vector", "tensor")
    p.op("vector", lambda e: e.tensor_scalar(
        ctx.erow[:, 0:512], PS[0:1, 512 * 6:512 * 6 + 512],
        CF1(ctx, 1.0 / 128.0), None, ALU.mult))
    p.op("vector", lambda e: e.tensor_mul(ctx.erow[:, 512:1024],
                                          ctx.erow[:, 0:512], ctx.erow[:, 0:512]))
    p.op("vector", lambda e: e.scalar_tensor_tensor(
        ctx.erow[:, 512:1024], PS[0:1, 512 * 7:512 * 7 + 512],
        CF1(ctx, 1.0 / 128.0), ctx.erow[:, 512:1024], ALU.mult, ALU.subtract))
    p.op("vector", lambda e: e.tensor_scalar(
        ctx.erow[:, 512:1024], ctx.erow[:, 512:1024], CF1(ctx, GN_EPS), None,
        ALU.add))
    p.op("vector", lambda e: e.reciprocal(ctx.erow[:, 512:1024],
                                          ctx.erow[:, 512:1024]))
    p.op("vector", lambda e: e.tensor_scalar(
        ctx.erow[:, 512:1024], ctx.erow[:, 512:1024],
        CF1(ctx, (1.0 - LAMBDA_INIT) ** 2), None, ALU.mult))
    p.wait_eng("scalar", "vector")
    p.op("scalar", lambda e: e.activation(ctx.erow[:, 512:1024],
                                          ctx.erow[:, 512:1024], AF.Sqrt))
    p.wait_eng("vector", "scalar")
    p.op("vector", lambda e: e.tensor_mul(ctx.erow[:, 0:512], ctx.erow[:, 0:512],
                                          ctx.erow[:, 512:1024]))
    p.wait_eng("tensor", "vector")
    p.op("tensor", lambda e: e.matmul(
        PS[:, 512 * 6:512 * 6 + 512], ctx.ones_r[0:1, :], ctx.erow[0:1, 512:1024],
        start=True, stop=True))
    p.op("tensor", lambda e: e.matmul(
        PS[:, 512 * 7:512 * 7 + 512], ctx.ones_r[0:1, :], ctx.erow[0:1, 0:512],
        start=True, stop=True))
    p.wait_eng("vector", "tensor")
    gsl = ctx.gn_all[:, 2048 * hp + 512 * blk:2048 * hp + 512 * blk + 512]
    p.op("vector", lambda e: e.tensor_mul(
        ctx.t1f[:], ctx.opf[:], PS[:, 512 * 6:512 * 6 + 512]))
    p.op("vector", lambda e, gsl=gsl: e.tensor_sub(
        gsl, ctx.t1f[:], PS[:, 512 * 7:512 * 7 + 512]))


def emit_oproj(p, ctx):
    PS = ctx.ps
    p.wait_eng("sync", "tensor")
    p.wait_eng("sync", "vector")
    p.wait_eng("sync", "scalar")
    p.wait_eng("vector", "tensor")
    for i in range(3):
        if f"dq_out{i}" in p.dcnt:
            p.wait_dma("vector", f"dq_out{i}")
    for hi in range(8):
        base = 2048 * hi
        for half in range(2):
            a = ctx.gn_all[:, base + 1024 * half:base + 1024 * half + 1024]
            bb_ = ctx.gn_all[:, base + 1024 * (1 - half):base + 1024 * (1 - half) + 1024]
            dst = ctx.gn_eff[:, base + 1024 * half:base + 1024 * half + 1024]
            p.op("vector", lambda e, bb_=bb_: e.tensor_single_scalar(
                ctx.P01[0][:], bb_, ctx.osel[:, 1:2], ALU.mult))
            p.op("vector", lambda e, a=a, dst=dst: e.scalar_tensor_tensor(
                dst, a, ctx.osel[:, 0:1], ctx.P01[0][:], ALU.mult, ALU.add))
    for dt in range(16):
        b = dt % 2
        p.wait_eng("sync", "vector", at=ctx.wo_cast_done.get(b, 0))
        src = ctx.wo.rearrange("(ac pp) d -> pp ac d", pp=128)[:, :, 128 * dt:128 * dt + 128]
        p.dma("sync", ctx.wof[b][:].rearrange("pp (ac c) -> pp ac c", c=128), src,
              f"wo{b}")
        p.wait_dma("vector", f"wo{b}")
        p.wait_eng("vector", "tensor", at=ctx.op_mm_done.get(b, 0))
        p.op("vector", lambda e, b=b: e.tensor_copy(ctx.wob[b][:], ctx.wof[b][:]))
        ctx.wo_cast_done[b] = p.ecnt["vector"]
        p.wait_eng("tensor", "vector")
        p.wait_eng("tensor", "scalar", at=ctx.op_evac_done.get(0, 0))
        for kc in range(KC):
            stat = ctx.wob[b][:, 128 * kc:128 * kc + 128]
            for hi in range(8):
                mov = ctx.gn_eff[:, 2048 * hi:2048 * hi + 2048].rearrange(
                    "p (r j) -> p j r", j=16)[:, kc, :]
                p.op("tensor", lambda e, stat=stat, mov=mov, hi=hi, kc=kc: e.matmul(
                    PS[:, 512 * hi:512 * hi + 128], stat, mov,
                    start=(kc == 0), stop=(kc == KC - 1)),
                    inc=(kc == KC - 1 and hi == 7))
        p.wait_eng("scalar", "tensor")
        for hi in range(8):
            p.op("scalar", lambda e, dt=dt, hi=hi: e.activation(
                ctx.s_bf[:, 1024 * dt + 128 * hi:1024 * dt + 128 * hi + 128],
                PS[:, 512 * hi:512 * hi + 128], AF.Copy))
        ctx.op_evac_done[0] = p.ecnt["scalar"]
        ctx.op_mm_done[b] = p.ecnt["tensor"]


def emit_ffn(p, ctx):
    PS = ctx.ps
    p.wait_eng("vector", "tensor")
    p.wait_eng("sync", "tensor")
    p.wait_eng("sync", "scalar")
    p.wait_eng("sync", "vector")
    for i in range(3):
        if f"dq_out{i}" in p.dcnt:
            p.wait_dma("sync", f"dq_out{i}")
    if getattr(ctx, "ag1_done", None) is not None:
        p.wait_eng("sync", "gpsimd", at=ctx.ag1_done)
    p.op("vector", lambda e: e.memset(ctx.dacc_a[:], 0.0))
    p.op("vector", lambda e: e.memset(ctx.dacc_b[:], 0.0))
    gu_mm_done = {}
    dn_mm_done = {}
    ld_done = {}
    for b5 in range(8):
        wsrc = ctx.ccw[b5]
        for l in range(16):
            wb = l % 2
            for mi, wt in [(0, ctx.wgt[wb]), (1, ctx.wut[wb])]:
                p.wait_eng("sync", "tensor", at=gu_mm_done.get((wb, mi), 0))
                src = wsrc[mi].rearrange("(kc pp) n -> pp kc n", pp=128)[:, :, 128 * l:128 * l + 128]
                p.dma("sync", wt[:].rearrange("pp (kc c) -> pp kc c", c=128), src,
                      f"wg{wb}{mi}")
            p.wait_dma("tensor", f"wg{wb}0")
            p.wait_dma("tensor", f"wg{wb}1")
            p.wait_eng("tensor", "scalar", at=ld_done.get("gelu", 0))
            for kc in range(KC):
                stat = ctx.wgt[wb][:, 128 * kc:128 * kc + 128]
                for tqc in range(2):
                    p.op("tensor", lambda e, stat=stat, kc=kc, tqc=tqc: e.matmul(
                        PS[:, 512 * tqc:512 * tqc + 512], stat,
                        ctx.xn_own[:, 1024 * kc + 512 * tqc:1024 * kc + 512 * tqc + 512],
                        start=(kc == 0), stop=(kc == KC - 1)),
                        inc=(kc == KC - 1 and tqc == 1))
            p.wait_eng("tensor", "scalar", at=ld_done.get("up", 0))
            for kc in range(KC):
                stat = ctx.wut[wb][:, 128 * kc:128 * kc + 128]
                for tqc in range(2):
                    p.op("tensor", lambda e, stat=stat, kc=kc, tqc=tqc: e.matmul(
                        PS[:, 1024 + 512 * tqc:1024 + 512 * tqc + 512], stat,
                        ctx.xn_own[:, 1024 * kc + 512 * tqc:1024 * kc + 512 * tqc + 512],
                        start=(kc == 0), stop=(kc == KC - 1)),
                        inc=(kc == KC - 1 and tqc == 1))
            gu_mm_done[(wb, 0)] = gu_mm_done[(wb, 1)] = p.ecnt["tensor"]
            p.wait_eng("scalar", "tensor")
            p.wait_eng("scalar", "vector", at=ld_done.get("inter", 0))
            p.op("scalar", lambda e: e.activation(ctx.ffn_gg[:], PS[:, 0:1024],
                                                  AF.Gelu_apprx_tanh))
            ld_done["gelu"] = p.ecnt["scalar"]
            p.op("scalar", lambda e: e.activation(ctx.ffn_ub[:], PS[:, 1024:2048],
                                                  AF.Copy))
            ld_done["up"] = p.ecnt["scalar"]
            p.wait_eng("vector", "scalar")
            p.op("vector", lambda e, l=l: e.tensor_mul(
                ctx.inter[:, 1024 * l:1024 * l + 1024], ctx.ffn_gg[:], ctx.ffn_ub[:]))
            ld_done["inter"] = p.ecnt["vector"]
        inter_done = p.ecnt["vector"]
        for ht in range(16):
            wb = ht % 2
            if b5 == 0 and ht == 0 and getattr(ctx, "ag2_done", None) is not None:
                p.wait_eng("sync", "gpsimd", at=ctx.ag2_done)
            p.wait_eng("sync", "tensor", at=dn_mm_done.get(wb, 0))
            src = wsrc[2].rearrange("(nc pp) hh -> pp nc hh", pp=128)[:, :, 128 * ht:128 * ht + 128]
            p.dma("sync", ctx.wdt[wb][:].rearrange("pp (nc c) -> pp nc c", c=128),
                  src, f"wd{wb}")
            p.wait_dma("tensor", f"wd{wb}")
            if ht == 0:
                p.wait_eng("tensor", "vector", at=inter_done)
            p.wait_eng("tensor", "vector", at=ld_done.get("dacc", 0))
            for nc_ in range(KC):
                stat = ctx.wdt[wb][:, 128 * nc_:128 * nc_ + 128]
                for tqc in range(2):
                    p.op("tensor", lambda e, stat=stat, nc_=nc_, tqc=tqc: e.matmul(
                        PS[:, 2048 + 512 * tqc:2048 + 512 * tqc + 512], stat,
                        ctx.inter[:, 1024 * nc_ + 512 * tqc:1024 * nc_ + 512 * tqc + 512],
                        start=(nc_ == 0), stop=(nc_ == KC - 1)),
                        inc=(nc_ == KC - 1 and tqc == 1))
            dn_mm_done[wb] = p.ecnt["tensor"]
            p.wait_eng("vector", "tensor")
            dsl = ctx.dacc_a[:, 1024 * ht:1024 * ht + 1024] if ht < 8 else \
                ctx.dacc_b[:, 1024 * (ht - 8):1024 * (ht - 8) + 1024]
            p.op("vector", lambda e, dsl=dsl: e.tensor_add(dsl, dsl,
                                                           PS[:, 2048:3072]))
            ld_done["dacc"] = p.ecnt["vector"]
    oadds = []
    for ht in range(16):
        b = ht % 2
        dsl = ctx.dacc_a[:, 1024 * ht:1024 * ht + 1024] if ht < 8 else \
            ctx.dacc_b[:, 1024 * (ht - 8):1024 * (ht - 8) + 1024]
        p.wait_eng("scalar", "vector")
        p.op("scalar", lambda e, dsl=dsl: e.activation(ctx.gbuf[:], dsl, AF.Silu))
        if ht >= 2:
            p.wait_eng("sync", "vector", at=oadds[ht - 2])
        else:
            p.wait_eng("sync", "tensor")
        p.dma("sync", ctx.xfbuf[b][:], ctx.xfm[128 * ht:128 * ht + 128, 0:1024],
              f"xf{b}")
        p.wait_eng("vector", "scalar")
        p.op("vector", lambda e, ht=ht: e.tensor_mul(
            ctx.sgtmp[:], ctx.s_bf[:, 1024 * ht:1024 * ht + 1024], ctx.gbuf[:]))
        p.wait_dma("vector", f"xf{b}")
        if ht >= 2:
            p.wait_dma("vector", f"xo{b}")
        p.op("vector", lambda e, b=b: e.tensor_add(
            ctx.obuf[b][:], ctx.xfbuf[b][:], ctx.sgtmp[:]))
        oadds.append(p.ecnt["vector"])
        p.wait_eng("scalar", "vector")
        p.dma("scalar", ctx.out_fm[128 * ht:128 * ht + 128, :], ctx.obuf[b][:],
              f"xo{b}")
    p.wait_dma("gpsimd", "xo0")
    p.wait_dma("gpsimd", "xo1")


def emit_attention(p, ctx):
    emit_lambda(p, ctx)
    emit_ln(p, ctx)
    emit_vproj(p, ctx)
    for hp in range(8):
        emit_kqproj(p, ctx, hp)
        emit_cbound(p, ctx, hp)
        for blk in range(4):
            emit_attn_block(p, ctx, hp, blk)
    emit_oproj(p, ctx)


def build_program(mode):
    nc = bass.Bass("TRN2", target_bir_lowering=False,
                   dynamic_dma_scratch_size=4096)
    ctx = make_ctx(nc, mode)
    if mode == "dequant":
        ctx.cc_in = nc.dram_tensor("cc", [3 * T, 2048], BF16, kind="ExternalOutput").ap()
    if mode in ("ffn", "fused", "layer2"):
        ctx.out_fm = nc.dram_tensor("out_fm", [2048, 1024], F32,
                                    kind="ExternalOutput").ap()
    if mode == "fused":
        ctx.dbg_s = nc.dram_tensor("dbg_s", [128, 16 * 1024], BF16,
                                   kind="ExternalOutput").ap()
        ctx.dbg_xn = nc.dram_tensor("dbg_xn", [128, 16 * 1024], BF16,
                                    kind="ExternalOutput").ap()
        ctx.dbg_gn = nc.dram_tensor("dbg_gn", [128, 16 * 1024], BF16,
                                    kind="ExternalOutput").ap()
        ctx.dbg_cc = nc.dram_tensor("dbg_cc", [24, 128, 2048], BF16,
                                    kind="ExternalOutput").ap()
        ctx.dbg_da = nc.dram_tensor("dbg_da", [128, 8 * 1024], F32,
                                    kind="ExternalOutput").ap()
    if mode == "attn":
        ctx.s_out = nc.dram_tensor("s_out", [2048, 1024], BF16, kind="ExternalOutput").ap()
        ctx.xn_out = nc.dram_tensor("xn_out", [2048, 2048], BF16, kind="ExternalOutput").ap()
        ctx.dbg_gn = nc.dram_tensor("dbg_gn", [2048, 1024], BF16, kind="ExternalOutput").ap()
        ctx.dbg_kt = nc.dram_tensor("dbg_kt", [128, 2048], BF16, kind="ExternalOutput").ap()
        ctx.dbg_v = nc.dram_tensor("dbg_v", [128, 2048], BF16, kind="ExternalOutput").ap()
        ctx.dbg_rows = nc.dram_tensor("dbg_rows", [128, 16], F32, kind="ExternalOutput").ap()
    sems = {n: nc.alloc_semaphore(n) for n in sem_names_for(mode)}

    with nc.Block() as block:
        def emit_all(p):
            reset_sched_state(ctx)
            emit_preamble(p, ctx)
            if mode == "dequant":
                emit_dequant(p, ctx)
                emit_dequant_wait_done(p, ctx)
            elif mode == "ffn":
                emit_ln(p, ctx)
                p.wait_eng("sync", "tensor")
                p.wait_eng("sync", "vector")
                for dt in range(16):
                    p.dma("sync", ctx.s_bf[:, 1024 * dt:1024 * dt + 1024],
                          ctx.s_in[128 * dt:128 * dt + 128, :], "sld")
                p.wait_dma("vector", "sld")
                emit_ffn(p, ctx)
            elif mode == "layer2":
                emit_attention(p, ctx)
                emit_ffn(p, ctx)
            elif mode == "fused":
                def maybe_ag(which):
                    k = ctx.dq_state["k"]
                    if which == 1 and ctx.ag1_done is None and k >= 64:
                        for i in range(3):
                            p.wait_dma("gpsimd", f"dq_out{i}")
                        p.op("gpsimd", lambda e: e.collective_compute(
                            "AllGather", mybir.AluOpType.bypass,
                            replica_groups=[[0, 1, 2, 3, 4, 5, 6, 7]],
                            ins=[ctx.cc_in[0:2 * T, :]], outs=[ctx.cc_gu[:]]))
                        ctx.ag1_done = p.ecnt["gpsimd"]
                    if which == 2 and ctx.ag2_done is None and k >= len(DQ_TILES):
                        for i in range(3):
                            p.wait_dma("gpsimd", f"dq_out{i}")
                        p.op("gpsimd", lambda e: e.collective_compute(
                            "AllGather", mybir.AluOpType.bypass,
                            replica_groups=[[0, 1, 2, 3, 4, 5, 6, 7]],
                            ins=[ctx.cc_in[2 * T:3 * T, :]], outs=[ctx.cc_d[:]]))
                        ctx.ag2_done = p.ecnt["gpsimd"]
                ctx.ag1_done = None
                ctx.ag2_done = None
                emit_lambda(p, ctx)
                emit_ln(p, ctx)
                emit_vproj(p, ctx, hook=lambda q: emit_dq_tiles(p, ctx, 10))
                for hp in range(8):
                    emit_kqproj(p, ctx, hp)
                    emit_cbound(p, ctx, hp)
                    for blk in range(4):
                        emit_attn_block(p, ctx, hp, blk)
                        emit_dq_tiles(p, ctx, 6)
                        maybe_ag(1)
                        maybe_ag(2)
                emit_dq_tiles(p, ctx, len(DQ_TILES))
                maybe_ag(1)
                maybe_ag(2)
                emit_oproj(p, ctx)
                emit_ffn(p, ctx)
            elif mode == "attn":
                emit_attention(p, ctx)
                p.wait_eng("sync", "scalar")
                for dt in range(16):
                    p.dma("sync", ctx.dbg_gn[128 * dt:128 * dt + 128, :],
                          ctx.gn_all[:, 1024 * dt:1024 * dt + 1024], "sout")
                for dt in range(16):
                    p.dma("sync", ctx.s_out[128 * dt:128 * dt + 128, :],
                          ctx.s_bf[:, 1024 * dt:1024 * dt + 1024], "sout")
                    p.dma("sync", ctx.xn_out[128 * dt:128 * dt + 128, 0:1024],
                          ctx.xn_own[:, 1024 * dt:1024 * dt + 1024], "sout")
                    p.dma("sync", ctx.xn_out[128 * dt:128 * dt + 128, 1024:2048],
                          ctx.xn_rest[:, 1024 * dt:1024 * dt + 1024], "sout")
                p.dma("sync", ctx.dbg_rows[0:1, 0:11], ctx.crow[0:1, 0:11], "sout")
                p.dma("sync", ctx.dbg_rows[1:2, 0:1], ctx.lam[:], "sout")
                p.dma("sync", ctx.dbg_rows[2:3, 0:1], ctx.mccol[0:1, :], "sout")
                p.wait_dma("sync", "sout")

        for name, reg in [("sync", block.sync), ("vector", block.vector),
                          ("scalar", block.scalar), ("tensor", block.tensor),
                          ("gpsimd", block.gpsimd)]:
            def body(eng, name=name):
                emit_all(Prog(nc, sems, name, eng))
            reg(body)
    return nc


def build_dequant_test():
    return build_program("dequant")


def pi2_rows(core):
    return np.arange(2048) * 2 + 4096 * (core % 4) + (1 if core >= 4 else 0)


def host_dequant_slices(core, gW, gS, uW, uS, dW, dS):
    rows = pi2_rows(core)
    gbT = np.ascontiguousarray(gW[rows, :].T)
    ubT = np.ascontiguousarray(uW[rows, :].T)
    gsT = np.ascontiguousarray(np.repeat(gS[rows, :].T, 16, axis=0))
    usT = np.ascontiguousarray(np.repeat(uS[rows, :].T, 16, axis=0))
    nb0 = 2048 * (core % 4)
    dbT = np.ascontiguousarray(dW[:, nb0:nb0 + 2048].T)
    dbT = np.ascontiguousarray(dbT[:, PI])
    j0 = 128 * (core % 4)
    dsT = np.ascontiguousarray(np.repeat(dS[:, j0:j0 + 128].T, 16, axis=0))
    dsT = np.ascontiguousarray(dsT[:, PI])
    w = 1.0 if core < 4 else 0.0
    dqsel = np.tile(np.array([[w, 1.0 - w]], np.float32), (128, 1))
    return gbT, ubT, gsT, usT, dbT, dsT, dqsel


def tau_perm(core):
    if core % 2 == 0:
        return np.arange(2048)
    return np.concatenate([np.arange(1024, 2048), np.arange(0, 1024)])


def head_order(core):
    own = np.arange(8) + 8 * (core % 2)
    rest = np.arange(8) + 8 * (1 - core % 2)
    return np.concatenate([own, rest])


def host_attn_inputs(core, x, Wq, Wk, Wv, Wo, lq1, lk1, lq2, lk2):
    b = core // 2
    tau = tau_perm(core)
    ho = head_order(core)
    qk_cols = np.concatenate([np.arange(128 * g, 128 * g + 128) for g in ho])
    v_cols = np.concatenate([np.arange(128 * g, 128 * g + 128) for g in ho])
    xfm = np.ascontiguousarray(x[b].T[PI][:, tau], np.float32)
    wq = np.ascontiguousarray(Wq[PI, :][:, qk_cols], np.float32)
    wk = np.ascontiguousarray(Wk[PI, :][:, qk_cols], np.float32)
    wv = np.ascontiguousarray(Wv[PI, :][:, v_cols], np.float32)
    wo = np.ascontiguousarray(Wo[:, PI], np.float32)
    lams = np.concatenate([lq1, lk1, lq2, lk2]).reshape(1, 256).astype(np.float32)
    masks = np.zeros((64, 128, 512), ml_dtypes.bfloat16)
    for i in range(64):
        blk, tk = i // 16, i % 16
        kpos = tau[128 * tk:128 * tk + 128][:, None]
        qpos = tau[512 * blk:512 * blk + 512][None, :]
        masks[i] = (kpos <= qpos).astype(ml_dtypes.bfloat16)
    w = 1.0 if core % 2 == 0 else 0.0
    osel = np.tile(np.array([[w, 1.0 - w]], np.float32), (128, 1))
    return {"xfm": xfm, "wq": wq, "wk": wk, "wv": wv, "wo": wo,
            "lams": lams, "masks": masks, "osel": osel}


_PROG_CACHE = {}


def _prog(mode):
    if mode not in _PROG_CACHE:
        _PROG_CACHE[mode] = build_program(mode)
    return _PROG_CACHE[mode]


def kernel(L0_attn_x, L0_attn_Wq, L0_attn_Wk, L0_attn_Wv, L0_attn_Wo,
           L0_attn_lambda_q1, L0_attn_lambda_k1, L0_attn_lambda_q2,
           L0_attn_lambda_k2, L0_ffn_gate_W_packed, L0_ffn_gate_scales,
           L0_ffn_up_W_packed, L0_ffn_up_scales, L0_ffn_down_W_packed,
           L0_ffn_down_scales):
    x = np.asarray(L0_attn_x, np.float32)
    gW = np.asarray(L0_ffn_gate_W_packed, np.int32)
    uW = np.asarray(L0_ffn_up_W_packed, np.int32)
    dW = np.asarray(L0_ffn_down_W_packed, np.int32)
    gS = np.asarray(L0_ffn_gate_scales, np.float32)
    uS = np.asarray(L0_ffn_up_scales, np.float32)
    dS = np.asarray(L0_ffn_down_scales, np.float32)

    in_maps = []
    for c in range(NCORE):
        gbT, ubT, gsT, usT, dbT, dsT, dqsel = host_dequant_slices(
            c, gW, gS, uW, uS, dW, dS)
        d = host_attn_inputs(c, x, np.asarray(L0_attn_Wq, np.float32),
                             np.asarray(L0_attn_Wk, np.float32),
                             np.asarray(L0_attn_Wv, np.float32),
                             np.asarray(L0_attn_Wo, np.float32),
                             np.asarray(L0_attn_lambda_q1, np.float32),
                             np.asarray(L0_attn_lambda_k1, np.float32),
                             np.asarray(L0_attn_lambda_q2, np.float32),
                             np.asarray(L0_attn_lambda_k2, np.float32))
        d.update({"gbT": gbT, "ubT": ubT, "gsT": gsT, "usT": usT,
                  "dbT": dbT, "dsT": dsT, "dqsel": dqsel})
        in_maps.append(d)
    res = run_bass_kernel_spmd(_prog("fused"), in_maps, core_ids=list(range(NCORE)))

    out = np.zeros((BATCH, T, D), np.float32)
    for c in range(NCORE):
        o = np.asarray(res.results[c]["out_fm"], np.float32)
        tau = tau_perm(c)
        out[c // 2, tau[:1024], :] = o[PI_INV, :].T
    return out


# version 17 (final)
# speedup vs baseline: 1.2349x; 1.0575x over previous
import math

import numpy as np
import ml_dtypes

import concourse.bass as bass
import concourse.mybir as mybir
from concourse.bass_utils import run_bass_kernel_spmd

F32 = mybir.dt.float32
BF16 = mybir.dt.bfloat16
I32 = mybir.dt.int32
ALU = mybir.AluOpType
AF = mybir.ActivationFunctionType
BFNP = ml_dtypes.bfloat16

D = 2048
T = 2048
BATCH = 4
NCORE = 8
HD = 64
NPAIR = 16
VD = 128
MLP = 16384
KC = D // 128
LAMBDA_INIT = 0.8 - 0.6 * math.exp(-0.3 * 12)
LN_EPS = 1e-5
GN_EPS = 1e-5

PI = np.concatenate([np.arange(0, D, 2), np.arange(1, D, 2)])
PI_INV = np.argsort(PI)

FVALS = [7.5, -1.0, 1.0, 0.5, -2.0, 4.0, -8.0, -10.0, 0.125,
         1.0 / 128.0, -1.0 / 128.0, LN_EPS, GN_EPS, 1.0 - LAMBDA_INIT,
         LAMBDA_INIT, 1.0 / D, 0.0, -16.0, 2.0, -0.5, 28.0, 24.0, -1e9,
         (1.0 - LAMBDA_INIT) ** 2, -1000.0]
IVALS = [15]
FIDX = {v: i for i, v in enumerate(FVALS)}
IIDX = {v: i for i, v in enumerate(IVALS)}


def own_qblocks(core):
    if core % 2 == 0:
        return [(0, 512), (1536, 2048)]
    return [(512, 1024), (1024, 1536)]


ENGS = ["sync", "vector", "scalar", "tensor", "gpsimd"]


class Prog:

    def __init__(self, nc, sems, eng_name, eng):
        self.nc = nc
        self.sems = sems
        self.eng_name = eng_name
        self.eng = eng
        self.ecnt = {e: 0 for e in ENGS}
        self.dcnt = {k: 0 for k in sems if not k.startswith("E_")}

    def on(self, name):
        return name == self.eng_name

    def op(self, eng_name, fn, inc=True, drain=None):
        if drain is None:
            drain = eng_name in ("vector", "scalar")
        if drain:
            self.wait_self(eng_name)
        if self.on(eng_name):
            instr = fn(self.eng)
            if inc and instr is not None:
                instr.then_inc(self.sems["E_" + eng_name], 1)
        if inc:
            self.ecnt[eng_name] += 1

    def wait_self(self, eng_name):
        if self.on(eng_name) and self.ecnt[eng_name] > 0:
            self.eng.wait_ge(self.sems["E_" + eng_name], self.ecnt[eng_name])

    def wait_eng(self, waiter, producer, at=None):
        v = self.ecnt[producer] if at is None else at
        if self.on(waiter) and v > 0:
            self.eng.wait_ge(self.sems["E_" + producer], v)

    def dma(self, eng_name, out, in_, stream):
        if self.on(eng_name):
            self.eng.dma_start(out=out, in_=in_).then_inc(self.sems[stream], 16)
        self.dcnt[stream] += 16

    def wait_dma(self, waiter, stream, at=None):
        v = self.dcnt[stream] if at is None else at
        if self.on(waiter) and v > 0:
            self.eng.wait_ge(self.sems[stream], v)

    def mark(self, *streams):
        return {k: self.dcnt[k] for k in streams}


class Ctx:
    pass


def make_consts(nc):
    cf_np = np.tile(np.array(FVALS, np.float32), (128, 1))
    ci_np = np.tile(np.array(IVALS, np.int32), (128, 1))
    cfd = nc.inline_tensor(cf_np, name="cfd").ap()
    cid = nc.inline_tensor(ci_np, name="cid").ap()
    cft = nc.alloc_sbuf_tensor("cft", [128, len(FVALS)], F32)
    cit = nc.alloc_sbuf_tensor("cit", [128, len(IVALS)], I32)
    for v, i in FIDX.items():
        nc.const_aps.register(F32, v, cft[:, i:i + 1])
    return cfd, cid, cft, cit


def CF(ctx, v):
    return ctx.cft[:, FIDX[v]:FIDX[v] + 1]


def CI(ctx, v):
    return ctx.cit[:, IIDX[v]:IIDX[v] + 1]


def CF1(ctx, v):
    return ctx.cft[0:1, FIDX[v]:FIDX[v] + 1]


LIVE_TK = {0: list(range(0, 4)) + list(range(8, 16)),
           1: list(range(0, 16)),
           2: list(range(0, 12)),
           3: list(range(0, 16))}

DQ_TILES = []
for _mi, _ntt, _nibs, _rb in [(0, 8, ("lo", "hi"), 0), (1, 8, ("lo", "hi"), T),
                              (2, 16, ("dyn",), 2 * T)]:
    for _tt in range(_ntt):
        for _h in range(4):
            DQ_TILES.append((_mi, _tt, _h, _nibs, _rb))


def emit_dq_tiles(p, ctx, n):
    F = 512
    st = ctx.dq_state
    nbuf = len(ctx.dq_bytes)
    for _ in range(n):
        if st["k"] >= len(DQ_TILES):
            return
        k = st["k"]
        mi, tt, h, nibbles, rbase = DQ_TILES[k]
        bT = [ctx.gbT, ctx.ubT, ctx.dbT][mi]
        sT = [ctx.gsT, ctx.usT, ctx.dsT][mi]
        buf = k % nbuf
        csl = slice(F * h, F * h + F)
        if k >= nbuf:
            p.wait_dma("sync", f"dq_out{buf}", at=st["oat"][k - nbuf])
        p.dma("sync", ctx.dq_bytes[buf][:],
              bT[128 * tt:128 * tt + 128, csl], f"dq_ld{buf}")
        p.dma("sync", ctx.dq_scale[buf][:],
              sT[128 * tt:128 * tt + 128, csl], f"dq_ld{buf}")
        ld_target = p.dcnt[f"dq_ld{buf}"]
        p.wait_dma("vector", f"dq_ld{buf}", at=ld_target)
        bts = ctx.dq_bytes[buf][:]
        two = nibbles == ("lo", "hi")
        if two or nibbles == ("lo",):
            p.op("vector", lambda e: e.tensor_single_scalar(
                ctx.dq_ci[:], bts, CI(ctx, 15), ALU.bitwise_and))
            p.op("vector", lambda e: e.tensor_copy(ctx.dq_cf[0][:], ctx.dq_ci[:]))
        if two or nibbles == ("hi",):
            dst = ctx.dq_cf[1] if two else ctx.dq_cf[0]
            p.op("vector", lambda e: e.tensor_scalar(
                ctx.dq_ci[:], bts, 4, CI(ctx, 15),
                ALU.logical_shift_right, ALU.bitwise_and))
            p.op("vector", lambda e, dst=dst: e.tensor_copy(dst[:], ctx.dq_ci[:]))
        if nibbles == ("dyn",):
            p.op("vector", lambda e: e.tensor_scalar(
                ctx.dq_ci[:], bts, 4, CI(ctx, 15),
                ALU.logical_shift_right, ALU.bitwise_and))
            p.op("vector", lambda e: e.tensor_copy(ctx.dq_cf[0][:], ctx.dq_ci[:]))
            p.op("vector", lambda e: e.tensor_single_scalar(
                ctx.dq_ci[:], bts, CI(ctx, 15), ALU.bitwise_and))
            p.op("vector", lambda e: e.tensor_copy(ctx.dq_cf[1][:], ctx.dq_ci[:]))
            p.op("vector", lambda e: e.tensor_single_scalar(
                ctx.dq_cf[0][:], ctx.dq_cf[0][:], ctx.dqsel[:, 1:2], ALU.mult))
            p.op("vector", lambda e: e.scalar_tensor_tensor(
                ctx.dq_cf[0][:], ctx.dq_cf[1][:], ctx.dqsel[:, 0:1],
                ctx.dq_cf[0][:], ALU.mult, ALU.add))
        slots = (0, 1) if two else (0,)
        for sl in slots:
            cf = ctx.dq_cf[sl]
            p.op("vector", lambda e, sl=sl, cf=cf: e.tensor_single_scalar(
                ctx.dq_sgn[sl][:], cf[:], CF(ctx, 7.5), ALU.is_ge))
            p.op("vector", lambda e, sl=sl: e.tensor_scalar(
                ctx.dq_sgn[sl][:], ctx.dq_sgn[sl][:], CF(ctx, -2.0), CF(ctx, 1.0),
                ALU.mult, ALU.add))
            p.op("vector", lambda e, sl=sl, cf=cf: e.scalar_tensor_tensor(
                ctx.dq_t[sl][:], ctx.dq_sgn[sl][:], CF(ctx, 4.0), cf[:],
                ALU.mult, ALU.add))
        p.wait_eng("scalar", "vector")
        for sl in slots:
            p.op("scalar", lambda e, sl=sl: e.activation(
                ctx.dq_r1[sl][:], ctx.dq_t[sl][:], AF.Relu, bias=-8.0, scale=1.0))
            p.op("scalar", lambda e, sl=sl: e.activation(
                ctx.dq_r2[sl][:], ctx.dq_t[sl][:], AF.Relu, bias=-10.0, scale=1.0))
        p.wait_eng("vector", "scalar")
        for sl in slots:
            vbuf = ctx.dq_val[buf][sl]
            p.op("vector", lambda e, sl=sl: e.scalar_tensor_tensor(
                ctx.dq_r1[sl][:], ctx.dq_r1[sl][:], CF(ctx, 0.5),
                ctx.dq_r2[sl][:], ALU.mult, ALU.add))
            p.op("vector", lambda e, sl=sl: e.scalar_tensor_tensor(
                ctx.dq_r1[sl][:], ctx.dq_t[sl][:], CF(ctx, 0.5),
                ctx.dq_r1[sl][:], ALU.mult, ALU.add))
            p.op("vector", lambda e, sl=sl: e.scalar_tensor_tensor(
                ctx.dq_r1[sl][:], ctx.dq_r1[sl][:], CF(ctx, -2.0),
                ctx.dq_sgn[sl][:], ALU.add, ALU.mult))
            p.op("vector", lambda e, sl=sl, vbuf=vbuf: e.tensor_tensor(
                vbuf[:], ctx.dq_r1[sl][:], ctx.dq_scale[buf][:], ALU.mult))
        p.wait_eng("scalar", "vector")
        for nib in nibbles:
            if mi < 2:
                row0 = rbase + (0 if nib == "lo" else 1024) + 128 * tt
            else:
                row0 = rbase + 128 * tt
            vbuf = ctx.dq_val[buf][0 if nib != "hi" else 1]
            p.dma("scalar", ctx.cc_in[row0:row0 + 128, csl], vbuf[:],
                  f"dq_out{buf}")
        st["oat"].append(p.dcnt[f"dq_out{buf}"])
        st["k"] = k + 1


def emit_dequant(p, ctx):
    emit_dq_tiles(p, ctx, len(DQ_TILES))


def emit_dequant_wait_done(p, ctx):
    for i in range(3):
        p.wait_dma("gpsimd", f"dq_out{i}")


A0 = 5120
A2 = A0 + 32 * 1024
B0 = A2 + 32 * 1024
C0 = B0 + 64 * 1024
D0 = C0 + 32 * 1024
E0 = D0 + 32 * 1024


def make_ctx(nc, mode):
    ctx = Ctx()
    ctx.mode = mode
    at = nc.alloc_sbuf_tensor_at

    def T_(name, off, shape, dt):
        return at(name, shape, dt, offset=off)

    dram = lambda n, sh, dt: nc.dram_tensor(n, sh, dt, kind="ExternalInput").ap()
    if mode in ("dequant", "fused"):
        ctx.gbT = dram("gbT", [1024, 2048], I32)
        ctx.ubT = dram("ubT", [1024, 2048], I32)
        ctx.dbT = dram("dbT", [2048, 2048], I32)
        ctx.gsT = dram("gsT", [1024, 2048], F32)
        ctx.usT = dram("usT", [1024, 2048], F32)
        ctx.dsT = dram("dsT", [2048, 2048], F32)
        ctx.dqsel_d = dram("dqsel", [128, 2], F32)
    if mode in ("attn", "fused", "ffn", "layer2"):
        ctx.xfm = dram("xfm", [2048, 2048], F32)
    if mode == "ffn":
        ctx.ccw_d = dram("ccw", [8, 3, 2048, 2048], BF16)
        ctx.s_in = dram("s_in", [2048, 1024], BF16)
    if mode == "layer2":
        ctx.ccw_d = dram("ccw", [8, 3, 2048, 2048], BF16)
    if mode in ("ffn", "layer2"):
        ctx.ccw = [[ctx.ccw_d[b5, mi] for mi in range(3)] for b5 in range(8)]
    if mode == "fused":
        ctx.cc_in = nc.dram_tensor("cc_in", [3 * T, 2048], BF16,
                                   kind="Internal").ap()
        ctx.cc_gu = nc.dram_tensor("cc_gu", [8 * 2 * T, 2048], BF16,
                                   kind="Internal", addr_space="Shared").ap()
        ctx.cc_d = nc.dram_tensor("cc_d", [8 * T, 2048], BF16,
                                  kind="Internal", addr_space="Shared").ap()
        ctx.ccw = [[ctx.cc_gu[(2 * b5 + 0) * T:(2 * b5 + 1) * T, :],
                    ctx.cc_gu[(2 * b5 + 1) * T:(2 * b5 + 2) * T, :],
                    ctx.cc_d[b5 * T:(b5 + 1) * T, :]] for b5 in range(8)]
    if mode in ("attn", "fused", "layer2"):
        ctx.wq = dram("wq", [2048, 2048], F32)
        ctx.wk = dram("wk", [2048, 2048], F32)
        ctx.wv = dram("wv", [2048, 2048], F32)
        ctx.wo = dram("wo", [2048, 2048], F32)
        ctx.lam_d = dram("lams", [1, 256], F32)
        ctx.osel_d = dram("osel", [128, 2], F32)
        ctx.masks_d = dram("masks", [64, 128, 512], BF16)
        ctx.dead_d = dram("deadrow", [1, 64], F32)

    cf_np = np.tile(np.array(FVALS, np.float32), (128, 1))
    ci_np = np.tile(np.array(IVALS, np.int32), (128, 1))
    ctx.cfd = nc.inline_tensor(cf_np, name="cfd").ap()
    ctx.cid = nc.inline_tensor(ci_np, name="cid").ap()
    ctx.ones_d = nc.inline_tensor(np.ones((128, 2), np.float32), name="ones_d").ap()
    ctx.onesb_d = nc.inline_tensor(np.ones((128, 2), ml_dtypes.bfloat16),
                                   name="onesb_d").ap()
    ctx.onesr_d = nc.inline_tensor(np.ones((1, 128), np.float32), name="onesr_d").ap()

    e = E0
    def Es(name, shape, dt):
        nonlocal e
        e = (e + 31) & ~31
        t = T_(name, e, shape, dt)
        e += shape[1] * mybir.dt.size(dt)
        return t
    ctx.cft = Es("cft", [128, len(FVALS)], F32)
    ctx.cit = Es("cit", [128, len(IVALS)], I32)
    for v, i in FIDX.items():
        nc.const_aps.register(F32, v, ctx.cft[:, i:i + 1])
    ctx.ones_f = Es("ones_f", [128, 2], F32)
    ctx.ones_b = Es("ones_b", [128, 2], BF16)
    ctx.ones_r = Es("ones_r", [1, 128], F32)
    if mode in ("dequant", "fused"):
        ctx.dqsel = Es("dqsel_t", [128, 2], F32)
    if mode in ("attn", "fused", "ffn", "layer2"):
        ctx.wqkf = Es("wqkf", [128, 2048], F32)
        ctx.wqb = Es("wqb", [128, 2048], BF16)
        ctx.wkb = Es("wkb", [128, 2048], BF16)
        ctx.P01 = [Es(f"P01_{i}", [128, 1024], BF16) for i in range(2)]
        ctx.mcall = Es("mcall", [128, 64], F32)
        ctx.mrow = Es("mrow", [1, 64], F32)
        ctx.deadr = Es("deadr", [1, 64], F32)
        ctx.crow = Es("crow", [1, 16], F32)
        ctx.lrows = Es("lrows", [1, 256], F32)
        ctx.lscr = Es("lscr", [1, 128], F32)
        ctx.lam = Es("lam", [1, 1], F32)
        ctx.mlam = Es("mlam", [1, 1], F32)
        ctx.mccol = Es("mccol", [128, 1], F32)
        ctx.osel = Es("osel_t", [128, 2], F32)
        ctx.wob = [ctx.wqb, ctx.wkb]
    ctx.e_end = e
    assert e <= 229376, f"E region overflow: {e}"

    if mode in ("attn", "fused", "ffn", "layer2"):
        ctx.xn_own = T_("xn_own", A0, [128, 16 * 1024], BF16)
        ctx.xn_rest = T_("xn_rest", A2, [128, 16 * 1024], BF16)
        ctx.murow = T_("murow", B0, [1, 2048], F32)
        ctx.isrow = T_("isrow", B0 + 8192, [1, 2048], F32)
        ctx.rtmp = T_("rtmp", B0 + 16384, [1, 2048], F32)
        ctx.V = T_("V", B0, [128, 16 * 1024], BF16)
        ctx.s_bf = T_("s_bf", B0, [128, 16 * 1024], BF16)
        ctx.gn_eff = T_("gn_eff", D0, [128, 16 * 1024], BF16)
        ctx.xbuf = [T_(f"xbuf{i}", C0 + i * 8192, [128, 2048], F32) for i in range(2)]
        ctx.x2 = T_("x2", C0 + 16384, [128, 2048], BF16)
        ctx.ttmp = T_("ttmp", C0 + 20480, [128, 2048], F32)
        ctx.wvf = [T_(f"wvf{i}", C0 + 28672 + i * 2048, [128, 512], F32)
                   for i in range(2)]
        ctx.wvb = T_("wvb", C0, [128, 16 * 512], BF16)
        ctx.gn_all = T_("gn_all", C0, [128, 16 * 1024], BF16)
        dd = B0 + 32 * 1024
        dd0 = dd
        def Dm(name, shape, dt):
            nonlocal dd
            dd = (dd + 31) & ~31
            t = T_(name, dd, shape, dt)
            dd += shape[1] * mybir.dt.size(dt)
            return t
        ctx.kt_off = (dd + 31) & ~31
        ctx.KT = Dm("KT", [128, 2048], BF16)
        ctx.QT = Dm("QT", [128, 2048], BF16)
        ctx.sq2 = [Dm(f"sq2_{i}", [128, 512], BF16) for i in range(2)]
        ctx.maskT = Dm("maskT", [128, 4 * 512], BF16)
        ctx.t1f = Dm("t1f", [128, 512], F32)
        ctx.opf = Dm("opf", [128, 512], F32)
        ctx.sqb = Dm("sqb", [128, 512], BF16)
        ctx.ou0 = Dm("ou0", [128, 512], F32)
        ctx.ou1 = Dm("ou1", [128, 512], F32)
        ctx.erow = Dm("erow", [1, 1024], F32)
        assert dd <= C0, f"B attn block overflow: {dd}"
        ctx.wof2 = T_("wof2", ctx.kt_off, [128, 2048], F32)
        ctx.wof = [ctx.wqkf, ctx.wof2]
    if mode in ("dequant", "fused"):
        d = D0
        def Ds(name, shape, dt):
            nonlocal d
            d = (d + 31) & ~31
            t = T_(name, d, shape, dt)
            d += shape[1] * mybir.dt.size(dt)
            return t
        ctx.dq_bytes = [Ds(f"dqb{i}", [128, 512], I32) for i in range(3)]
        ctx.dq_scale = [Ds(f"dqs{i}", [128, 512], F32) for i in range(3)]
        ctx.dq_val = [[Ds(f"dqv{i}{j}", [128, 512], BF16) for j in range(2)]
                      for i in range(3)]
        ctx.dq_ci = Ds("dqci", [128, 512], I32)
        ctx.dq_cf = [Ds(f"dqcf{i}", [128, 512], BF16) for i in range(2)]
        ctx.dq_sgn = [Ds(f"dqsg{i}", [128, 512], BF16) for i in range(2)]
        ctx.dq_t = [Ds(f"dqt{i}", [128, 512], BF16) for i in range(2)]
        ctx.dq_r1 = [Ds(f"dqr1{i}", [128, 512], BF16) for i in range(2)]
        ctx.dq_r2 = [Ds(f"dqr2{i}", [128, 512], BF16) for i in range(2)]
        assert d <= D0 + 32 * 1024, f"dequant arena overflow {d - D0}"

    if mode in ("ffn", "fused", "layer2"):
        ctx.dacc_a = T_("dacc_a", A2, [128, 8 * 1024], F32)
        ctx.dacc_b = T_("dacc_b", B0 + 32 * 1024, [128, 8 * 1024], F32)
        ctx.inter = T_("inter", C0, [128, 16 * 1024], BF16)
        ctx.wgt = [T_(f"wgt{i}", D0 + 4096 * i, [128, 2048], BF16) for i in range(2)]
        ctx.wut = [T_(f"wut{i}", D0 + 8192 + 4096 * i, [128, 2048], BF16)
                   for i in range(2)]
        ctx.wdt = [T_(f"wdt{i}", D0 + 16384 + 4096 * i, [128, 2048], BF16)
                   for i in range(2)]
        ctx.ffn_gg = T_("ffn_gg", D0 + 24576, [128, 1024], BF16)
        ctx.ffn_ub = T_("ffn_ub", D0 + 26624, [128, 1024], BF16)
        ctx.gbuf = T_("gbuf", D0 + 16384, [128, 1024], BF16)
        ctx.sgtmp = T_("sgtmp", D0 + 24576, [128, 1024], F32)
        ctx.obuf = [T_("obuf0", ctx.wqkf.manual_sbuf_range[0], [128, 1024], F32),
                    T_("obuf1", ctx.wqkf.manual_sbuf_range[0] + 4096,
                       [128, 1024], F32)]
        ctx.xfbuf = [T_("xfbuf0", ctx.wqb.manual_sbuf_range[0], [128, 1024], F32),
                     T_("xfbuf1", ctx.wkb.manual_sbuf_range[0], [128, 1024], F32)]

    ctx.ps = nc.alloc_psum_tensor("ps", [128, 4096], F32)

    reset_sched_state(ctx)

    def xn_sl(kc, t0, w=512):
        if t0 < 1024:
            return ctx.xn_own[:, 1024 * kc + t0:1024 * kc + t0 + w]
        return ctx.xn_rest[:, 1024 * kc + (t0 - 1024):1024 * kc + (t0 - 1024) + w]
    ctx.xn_sl = xn_sl
    return ctx


def reset_sched_state(ctx):
    ctx.xs_pe_done = {}
    ctx.xs_act_done = {}
    ctx.x2_pe_done = 0
    ctx.xa_dve_done = {}
    ctx.bcast_cons = {}
    ctx.wv_cast_done = {}
    ctx.vp_evac_done = {}
    ctx.wqk_cast_done = 0
    ctx.kq_evac_done = 0
    ctx.cb_red_done = {}
    ctx.mask_cons = {}
    ctx.exp_done = {}
    ctx.wo_cast_done = {}
    ctx.op_evac_done = {}
    ctx.op_mm_done = {}
    ctx.av_done = 0
    ctx.dq_vec_done = 0
    ctx.dq_act_done = 0
    ctx.kq_mm_done = 0
    ctx.dq_state = {"k": 0, "oat": []}
    ctx.score_done = {}
    ctx.mask_done = {}


def sem_names_for(mode):
    names = ["consts"] + ["E_" + x for x in ENGS]
    if mode in ("dequant", "fused"):
        names += [f"dq_ld{i}" for i in range(3)] + [f"dq_out{i}" for i in range(3)]
    if mode in ("attn", "fused", "ffn", "layer2"):
        names += ["xs0", "xs1", "xa0", "xa1", "wv0", "wv1", "wqk",
                  "mk0", "mk1", "wo0", "wo1", "sout"]
    if mode in ("ffn", "fused", "layer2"):
        names += ["wg00", "wg01", "wg10", "wg11", "wd0", "wd1",
                  "xf0", "xf1", "xo0", "xo1", "sld"]
    return names


def emit_preamble(p, ctx):
    p.dma("sync", ctx.cft[:], ctx.cfd, "consts")
    p.dma("sync", ctx.cit[:], ctx.cid, "consts")
    p.dma("sync", ctx.ones_f[:], ctx.ones_d, "consts")
    p.dma("sync", ctx.ones_b[:], ctx.onesb_d, "consts")
    p.dma("sync", ctx.ones_r[:], ctx.onesr_d, "consts")
    if hasattr(ctx, "dqsel_d"):
        p.dma("sync", ctx.dqsel[:], ctx.dqsel_d, "consts")
    if hasattr(ctx, "lam_d"):
        p.dma("sync", ctx.lrows[:], ctx.lam_d, "consts")
        p.dma("sync", ctx.osel[:], ctx.osel_d, "consts")
    if hasattr(ctx, "dead_d"):
        p.dma("sync", ctx.deadr[:], ctx.dead_d, "consts")
        for j in range(4):
            p.dma("sync", ctx.maskT[:, 512 * j:512 * j + 512],
                  ctx.masks_d[j], "consts")
    for e in ENGS[1:]:
        p.wait_dma(e, "consts")
    p.wait_dma("sync", "consts")


def emit_lambda(p, ctx):
    for i in (0, 1):
        q = ctx.lrows[:, 64 * (2 * i):64 * (2 * i) + 64]
        kk = ctx.lrows[:, 64 * (2 * i + 1):64 * (2 * i + 1) + 64]
        p.op("vector", lambda e, q=q, kk=kk: e.tensor_mul(ctx.lscr[:, 0:64], q, kk))
        p.op("vector", lambda e, i=i: e.reduce_sum(
            ctx.lscr[:, 64 + i:65 + i], ctx.lscr[:, 0:64],
            axis=mybir.AxisListType.X))
    p.wait_eng("scalar", "vector")
    p.op("scalar", lambda e: e.activation(
        ctx.lscr[:, 66:68], ctx.lscr[:, 64:66], AF.Exp))
    p.wait_eng("vector", "scalar")
    p.op("vector", lambda e: e.tensor_sub(
        ctx.lam[:, 0:1], ctx.lscr[:, 66:67], ctx.lscr[:, 67:68]))
    p.op("vector", lambda e: e.tensor_scalar(
        ctx.lam[:, 0:1], ctx.lam[:, 0:1], CF1(ctx, LAMBDA_INIT), None, ALU.add))
    p.op("vector", lambda e: e.tensor_scalar(
        ctx.mlam[:, 0:1], ctx.lam[:, 0:1], CF1(ctx, -1.0), None, ALU.mult))


def emit_ln(p, ctx):
    PS = ctx.ps
    for kc in range(KC):
        b = kc % 2
        if kc >= 2:
            p.wait_eng("sync", "tensor", at=ctx.xs_pe_done[kc - 2])
            p.wait_eng("sync", "scalar", at=ctx.xs_act_done[kc - 2])
        p.dma("sync", ctx.xbuf[b][:], ctx.xfm[128 * kc:128 * kc + 128, :], f"xs{b}")
        p.wait_dma("scalar", f"xs{b}")
        p.wait_eng("scalar", "tensor", at=ctx.x2_pe_done)
        p.op("scalar", lambda e, b=b: e.activation(ctx.x2[:, :], ctx.xbuf[b][:],
                                                   AF.Square))
        ctx.xs_act_done[kc] = p.ecnt["scalar"]
        p.wait_dma("tensor", f"xs{b}")
        for tc in range(4):
            sl = slice(512 * tc, 512 * tc + 512)
            p.op("tensor", lambda e, b=b, sl=sl, tc=tc, kc=kc: e.matmul(
                PS[0:1, 512 * tc:512 * tc + 512], ctx.ones_f[:, 0:1],
                ctx.xbuf[b][:, sl], start=(kc == 0), stop=(kc == KC - 1)),
                inc=(tc == 3))
        p.wait_eng("tensor", "scalar")
        for tc in range(4):
            sl = slice(512 * tc, 512 * tc + 512)
            p.op("tensor", lambda e, sl=sl, tc=tc, kc=kc: e.matmul(
                PS[0:1, 2048 + 512 * tc:2048 + 512 * tc + 512], ctx.ones_b[:, 0:1],
                ctx.x2[:, sl], start=(kc == 0), stop=(kc == KC - 1)),
                inc=(tc == 3))
        ctx.xs_pe_done[kc] = p.ecnt["tensor"]
        ctx.x2_pe_done = p.ecnt["tensor"]
    p.wait_eng("vector", "tensor")
    p.op("vector", lambda e: e.tensor_scalar(
        ctx.murow[:, :], PS[0:1, 0:2048], CF1(ctx, 1.0 / D), None, ALU.mult))
    p.op("vector", lambda e: e.tensor_scalar(
        ctx.isrow[:, :], PS[0:1, 2048:4096], CF1(ctx, 1.0 / D), None, ALU.mult))
    p.op("vector", lambda e: e.tensor_mul(ctx.rtmp[:, :], ctx.murow[:, :],
                                          ctx.murow[:, :]))
    p.op("vector", lambda e: e.tensor_sub(ctx.isrow[:, :], ctx.isrow[:, :],
                                          ctx.rtmp[:, :]))
    p.op("vector", lambda e: e.tensor_scalar(
        ctx.isrow[:, :], ctx.isrow[:, :], CF1(ctx, LN_EPS), None, ALU.add))
    p.op("vector", lambda e: e.reciprocal(ctx.isrow[:, :], ctx.isrow[:, :]))
    p.wait_eng("scalar", "vector")
    p.op("scalar", lambda e: e.activation(ctx.isrow[:, :], ctx.isrow[:, :], AF.Sqrt))
    stats_t = p.ecnt["tensor"]
    stats_s = p.ecnt["scalar"]
    for kc in range(KC):
        b = kc % 2
        if kc >= 2:
            p.wait_eng("sync", "vector", at=ctx.xa_dve_done[kc - 2])
        else:
            p.wait_eng("sync", "tensor", at=stats_t)
            p.wait_eng("sync", "scalar", at=stats_s)
        p.dma("sync", ctx.xbuf[b][:], ctx.xfm[128 * kc:128 * kc + 128, :], f"xa{b}")
        p.wait_dma("tensor", f"xa{b}")
        p.wait_eng("tensor", "scalar")
        for tc in range(4):
            bb = 2 * (tc % 2)
            p.wait_eng("tensor", "vector", at=ctx.bcast_cons.get(bb, 0))
            p.op("tensor", lambda e, tc=tc, bb=bb: e.matmul(
                PS[:, 512 * bb:512 * bb + 512], ctx.ones_r[0:1, :],
                ctx.murow[0:1, 512 * tc:512 * tc + 512], start=True, stop=True))
            p.op("tensor", lambda e, tc=tc, bb=bb: e.matmul(
                PS[:, 512 * (bb + 1):512 * (bb + 1) + 512], ctx.ones_r[0:1, :],
                ctx.isrow[0:1, 512 * tc:512 * tc + 512], start=True, stop=True))
            p.wait_eng("vector", "tensor")
            sl = slice(512 * tc, 512 * tc + 512)
            xnsl = ctx.xn_sl(kc, 512 * tc)
            p.op("vector", lambda e, b=b, sl=sl, bb=bb: e.tensor_sub(
                ctx.ttmp[:, 0:512], ctx.xbuf[b][:, sl], PS[:, 512 * bb:512 * bb + 512]))
            p.op("vector", lambda e, xnsl=xnsl, bb=bb: e.tensor_mul(
                xnsl, ctx.ttmp[:, 0:512], PS[:, 512 * (bb + 1):512 * (bb + 1) + 512]))
            ctx.bcast_cons[bb] = p.ecnt["vector"]
        ctx.xa_dve_done[kc] = p.ecnt["vector"]


def emit_vproj(p, ctx, hook=None):
    PS = ctx.ps
    vq_pe_done = 0
    casts = []
    for quarter in range(2):
        vc0 = 512 * quarter
        for kc in range(KC):
            b = kc % 2
            gi = 16 * quarter + kc
            if gi >= 2:
                p.wait_eng("sync", "vector", at=casts[gi - 2])
            src = ctx.wv.rearrange("(kc pp) a -> pp kc a", pp=128)[:, kc, vc0:vc0 + 512]
            p.dma("sync", ctx.wvf[b][:, 0:512], src, f"wv{b}")
            p.wait_dma("vector", f"wv{b}")
            if quarter > 0:
                p.wait_eng("vector", "tensor", at=vq_pe_done)
            p.op("vector", lambda e, b=b, kc=kc: e.tensor_copy(
                ctx.wvb[:, 512 * kc:512 * kc + 512], ctx.wvf[b][:, 0:512]))
            casts.append(p.ecnt["vector"])
        p.wait_eng("tensor", "vector")
        for tt in range(16):
            bank = 4 + (tt % 2)
            p.wait_eng("tensor", "scalar", at=ctx.vp_evac_done.get(bank, 0))
            for kc in range(KC):
                stat = ctx.xn_sl(kc, 128 * tt, w=128)
                p.op("tensor", lambda e, stat=stat, kc=kc, bank=bank: e.matmul(
                    PS[:, 512 * bank:512 * bank + 512], stat,
                    ctx.wvb[:, 512 * kc:512 * kc + 512],
                    start=(kc == 0), stop=(kc == KC - 1)), inc=(kc == KC - 1))
            p.wait_eng("scalar", "tensor")
            p.op("scalar", lambda e, tt=tt, vc0=vc0, bank=bank: e.activation(
                ctx.V[:, 1024 * tt + vc0:1024 * tt + vc0 + 512],
                PS[:, 512 * bank:512 * bank + 512], AF.Copy))
            ctx.vp_evac_done[bank] = p.ecnt["scalar"]
        vq_pe_done = p.ecnt["tensor"]
        if hook is not None:
            hook(quarter)


def emit_kqproj(p, ctx, hp):
    PS = ctx.ps
    for wi, (wdram, wbt) in enumerate([(ctx.wq, ctx.wqb), (ctx.wk, ctx.wkb)]):
        p.wait_eng("sync", "vector", at=ctx.wqk_cast_done)
        src = wdram.rearrange("(kc pp) a -> pp kc a", pp=128)[:, :, 128 * hp:128 * hp + 128]
        p.dma("sync", ctx.wqkf[:].rearrange("pp (kc c) -> pp kc c", c=128), src, "wqk")
        p.wait_dma("vector", "wqk")
        p.wait_eng("vector", "tensor", at=ctx.kq_mm_done)
        p.op("vector", lambda e, wbt=wbt: e.tensor_copy(wbt[:], ctx.wqkf[:]))
        ctx.wqk_cast_done = p.ecnt["vector"]
    p.wait_eng("tensor", "vector", at=ctx.wqk_cast_done)
    p.wait_eng("tensor", "scalar")
    for dst, wbt, nch, scale, bank in [(ctx.KT, ctx.wkb, 4, None, 0),
                                       (ctx.QT, ctx.wqb, 4, 0.125, 1)]:
        for tch in range(nch):
            p.wait_eng("tensor", "scalar", at=ctx.kq_evac_done)
            for kc in range(KC):
                stat = wbt[:, 128 * kc:128 * kc + 128]
                mov = ctx.xn_sl(kc, 512 * tch, w=512)
                p.op("tensor", lambda e, stat=stat, mov=mov, kc=kc, bank=bank: e.matmul(
                    PS[:, 512 * bank:512 * bank + 512], stat, mov,
                    start=(kc == 0), stop=(kc == KC - 1)), inc=(kc == KC - 1))
            p.wait_eng("scalar", "tensor")
            if scale is None:
                p.op("scalar", lambda e, dst=dst, tch=tch, bank=bank: e.activation(
                    dst[:, 512 * tch:512 * tch + 512],
                    PS[:, 512 * bank:512 * bank + 512], AF.Copy))
            else:
                p.op("scalar", lambda e, dst=dst, tch=tch, bank=bank, scale=scale:
                     e.activation(dst[:, 512 * tch:512 * tch + 512],
                                  PS[:, 512 * bank:512 * bank + 512], AF.Copy,
                                  scale=scale))
            ctx.kq_evac_done = p.ecnt["scalar"]
    ctx.kq_mm_done = p.ecnt["tensor"]


def emit_cbound(p, ctx, hp):
    PS = ctx.ps
    nm = 0
    sqd = {}
    for src in (ctx.QT, ctx.KT):
        for i in range(4):
            sb_ = nm % 2
            bank = 4 + sb_
            p.wait_eng("scalar", "tensor", at=sqd.get(sb_, 0))
            p.op("scalar", lambda e, src=src, i=i, sb_=sb_: e.activation(
                ctx.sq2[sb_][:], src[:, 512 * i:512 * i + 512], AF.Square))
            p.wait_eng("tensor", "scalar")
            p.wait_eng("tensor", "vector", at=ctx.cb_red_done.get(bank, 0))
            p.op("tensor", lambda e, sb_=sb_, bank=bank: e.matmul(
                PS[0:1, 512 * bank:512 * bank + 512], ctx.ones_b[:, 0:1],
                ctx.sq2[sb_][:], start=True, stop=True))
            sqd[sb_] = p.ecnt["tensor"]
            p.wait_eng("vector", "tensor")
            p.op("vector", lambda e, nm=nm, bank=bank: e.reduce_max(
                ctx.crow[:, nm:nm + 1], PS[0:1, 512 * bank:512 * bank + 512],
                axis=mybir.AxisListType.X))
            ctx.cb_red_done[bank] = p.ecnt["vector"]
            nm += 1
    p.op("vector", lambda e: e.reduce_max(ctx.crow[:, 8:9], ctx.crow[:, 0:4],
                                          axis=mybir.AxisListType.X))
    p.op("vector", lambda e: e.reduce_max(ctx.crow[:, 9:10], ctx.crow[:, 4:8],
                                          axis=mybir.AxisListType.X))
    p.op("vector", lambda e: e.tensor_mul(ctx.crow[:, 8:9], ctx.crow[:, 8:9],
                                          ctx.crow[:, 9:10]))
    p.wait_eng("scalar", "vector")
    p.op("scalar", lambda e: e.activation(ctx.crow[:, 10:11], ctx.crow[:, 8:9],
                                          AF.Sqrt))
    p.wait_eng("vector", "scalar")
    p.op("vector", lambda e: e.tensor_scalar(
        ctx.crow[:, 10:11], ctx.crow[:, 10:11], CF1(ctx, -1.0), None, ALU.mult))
    p.op("vector", lambda e: e.tensor_scalar(
        ctx.mrow[:], ctx.deadr[:], CF1(ctx, -1000.0), ctx.crow[0:1, 10:11],
        ALU.mult, ALU.add))
    p.wait_eng("tensor", "vector")
    p.op("tensor", lambda e: e.matmul(
        PS[:, 512 * 4:512 * 4 + 64], ctx.ones_r[0:1, :], ctx.mrow[0:1, :],
        start=True, stop=True))
    p.wait_eng("scalar", "tensor")
    p.op("scalar", lambda e: e.activation(ctx.mcall[:],
                                          PS[:, 512 * 4:512 * 4 + 64], AF.Copy))


def emit_attn_block(p, ctx, hp, blk):
    PS = ctx.ps
    q0 = 512 * blk
    live = LIVE_TK[blk]
    last = len(live) - 1

    def emit_scores(idx):
        tk = live[idx]
        it = 16 * blk + tk
        sb = 2 * (idx % 2)
        p.wait_eng("tensor", "scalar", at=ctx.exp_done.get(idx % 2, 0))
        p.op("tensor", lambda e, tk=tk, q0=q0, sb=sb: e.matmul(
            PS[:, 512 * sb:512 * sb + 512],
            ctx.KT[0:64, 128 * tk:128 * tk + 128],
            ctx.QT[0:64, q0:q0 + 512], start=True, stop=True))
        p.op("tensor", lambda e, tk=tk, q0=q0, sb=sb: e.matmul(
            PS[:, 512 * (sb + 1):512 * (sb + 1) + 512],
            ctx.KT[64:128, 128 * tk:128 * tk + 128],
            ctx.QT[64:128, q0:q0 + 512], start=True, stop=True,
            tile_position=(64, 0)))
        ctx.score_done[idx] = p.ecnt["tensor"]
        pb = ctx.P01[idx % 2]
        p.wait_eng("scalar", "tensor", at=ctx.score_done[idx])
        p.op("scalar", lambda e, pb=pb, sb=sb, it=it: e.activation(
            pb[:, 0:512], PS[:, 512 * sb:512 * sb + 512], AF.Exp,
            bias=ctx.mcall[:, it:it + 1]))
        p.op("scalar", lambda e, pb=pb, sb=sb, it=it: e.activation(
            pb[:, 512:1024], PS[:, 512 * (sb + 1):512 * (sb + 1) + 512], AF.Exp,
            bias=ctx.mcall[:, it:it + 1]))
        ctx.exp_done[idx % 2] = p.ecnt["scalar"]
        j = tk - 4 * blk
        if 0 <= j < 4:
            msk = ctx.maskT[:, 512 * j:512 * j + 512]
            p.wait_eng("vector", "scalar")
            p.op("vector", lambda e, pb=pb, msk=msk: e.tensor_mul(
                pb[:, 0:512], pb[:, 0:512], msk))
            p.op("vector", lambda e, pb=pb, msk=msk: e.tensor_mul(
                pb[:, 512:1024], pb[:, 512:1024], msk))
            ctx.mask_done[idx] = ("vector", p.ecnt["vector"])
        else:
            ctx.mask_done[idx] = ("scalar", ctx.exp_done[idx % 2])

    def emit_zav(idx):
        tk = live[idx]
        pb = ctx.P01[idx % 2]
        meng, mcnt = ctx.mask_done[idx]
        p.wait_eng("tensor", meng, at=mcnt)
        p.op("tensor", lambda e, pb=pb, idx=idx: e.matmul(
            PS[0:1, 512 * 6:512 * 6 + 512], ctx.ones_b[:, 0:1], pb[:, 0:512],
            start=(idx == 0), stop=(idx == last)))
        p.op("tensor", lambda e, pb=pb, idx=idx: e.matmul(
            PS[0:1, 512 * 7:512 * 7 + 512], ctx.ones_b[:, 0:1], pb[:, 512:1024],
            start=(idx == 0), stop=(idx == last)))
        vstat = ctx.V[:, 1024 * tk + 128 * hp:1024 * tk + 128 * hp + 128]
        p.op("tensor", lambda e, pb=pb, vstat=vstat, idx=idx: e.matmul(
            PS[:, 512 * 4:512 * 4 + 512], vstat, pb[:, 0:512],
            start=(idx == 0), stop=(idx == last)))
        p.op("tensor", lambda e, pb=pb, vstat=vstat, idx=idx: e.matmul(
            PS[:, 512 * 5:512 * 5 + 512], vstat, pb[:, 512:1024],
            start=(idx == 0), stop=(idx == last)))
        ctx.av_done = p.ecnt["tensor"]

    for idx in range(len(live)):
        emit_scores(idx)
        if idx > 0:
            emit_zav(idx - 1)
    emit_zav(last)

    p.wait_eng("vector", "tensor", at=ctx.av_done)
    p.op("vector", lambda e: e.reciprocal(ctx.erow[:, 0:512],
                                          PS[0:1, 512 * 6:512 * 6 + 512]))
    p.op("vector", lambda e: e.reciprocal(ctx.erow[:, 512:1024],
                                          PS[0:1, 512 * 7:512 * 7 + 512]))
    p.op("vector", lambda e: e.tensor_scalar(
        ctx.erow[:, 512:1024], ctx.erow[:, 512:1024], ctx.lam[0:1, 0:1], None,
        ALU.mult))
    p.wait_eng("tensor", "vector")
    p.op("tensor", lambda e: e.matmul(
        PS[:, 512 * 6:512 * 6 + 512], ctx.ones_r[0:1, :], ctx.erow[0:1, 0:512],
        start=True, stop=True))
    p.op("tensor", lambda e: e.matmul(
        PS[:, 512 * 7:512 * 7 + 512], ctx.ones_r[0:1, :], ctx.erow[0:1, 512:1024],
        start=True, stop=True))
    p.wait_eng("scalar", "tensor")
    p.op("scalar", lambda e: e.activation(ctx.ou0[:], PS[:, 512 * 4:512 * 4 + 512],
                                          AF.Copy))
    p.op("scalar", lambda e: e.activation(ctx.ou1[:], PS[:, 512 * 5:512 * 5 + 512],
                                          AF.Copy))
    p.wait_eng("vector", "scalar")
    p.op("vector", lambda e: e.tensor_mul(
        ctx.t1f[:], ctx.ou1[:], PS[:, 512 * 7:512 * 7 + 512]))
    p.op("vector", lambda e: e.tensor_mul(
        ctx.opf[:], ctx.ou0[:], PS[:, 512 * 6:512 * 6 + 512]))
    p.op("vector", lambda e: e.tensor_sub(
        ctx.opf[:], ctx.opf[:], ctx.t1f[:]))
    p.wait_eng("scalar", "vector")
    p.op("scalar", lambda e: e.activation(ctx.sqb[:], ctx.opf[:], AF.Square))
    p.wait_eng("tensor", "scalar")
    p.op("tensor", lambda e: e.matmul(
        PS[0:1, 512 * 6:512 * 6 + 512], ctx.ones_f[:, 0:1], ctx.opf[:],
        start=True, stop=True))
    p.op("tensor", lambda e: e.matmul(
        PS[0:1, 512 * 7:512 * 7 + 512], ctx.ones_b[:, 0:1], ctx.sqb[:],
        start=True, stop=True))
    p.wait_eng("vector", "tensor")
    p.op("vector", lambda e: e.tensor_scalar(
        ctx.erow[:, 0:512], PS[0:1, 512 * 6:512 * 6 + 512],
        CF1(ctx, 1.0 / 128.0), None, ALU.mult))
    p.op("vector", lambda e: e.tensor_mul(ctx.erow[:, 512:1024],
                                          ctx.erow[:, 0:512], ctx.erow[:, 0:512]))
    p.op("vector", lambda e: e.scalar_tensor_tensor(
        ctx.erow[:, 512:1024], PS[0:1, 512 * 7:512 * 7 + 512],
        CF1(ctx, 1.0 / 128.0), ctx.erow[:, 512:1024], ALU.mult, ALU.subtract))
    p.op("vector", lambda e: e.tensor_scalar(
        ctx.erow[:, 512:1024], ctx.erow[:, 512:1024], CF1(ctx, GN_EPS), None,
        ALU.add))
    p.op("vector", lambda e: e.reciprocal(ctx.erow[:, 512:1024],
                                          ctx.erow[:, 512:1024]))
    p.op("vector", lambda e: e.tensor_scalar(
        ctx.erow[:, 512:1024], ctx.erow[:, 512:1024],
        CF1(ctx, (1.0 - LAMBDA_INIT) ** 2), None, ALU.mult))
    p.wait_eng("scalar", "vector")
    p.op("scalar", lambda e: e.activation(ctx.erow[:, 512:1024],
                                          ctx.erow[:, 512:1024], AF.Sqrt))
    p.wait_eng("vector", "scalar")
    p.op("vector", lambda e: e.tensor_mul(ctx.erow[:, 0:512], ctx.erow[:, 0:512],
                                          ctx.erow[:, 512:1024]))
    p.wait_eng("tensor", "vector")
    p.op("tensor", lambda e: e.matmul(
        PS[:, 512 * 6:512 * 6 + 512], ctx.ones_r[0:1, :], ctx.erow[0:1, 512:1024],
        start=True, stop=True))
    p.op("tensor", lambda e: e.matmul(
        PS[:, 512 * 7:512 * 7 + 512], ctx.ones_r[0:1, :], ctx.erow[0:1, 0:512],
        start=True, stop=True))
    p.wait_eng("vector", "tensor")
    gsl = ctx.gn_all[:, 2048 * hp + 512 * blk:2048 * hp + 512 * blk + 512]
    p.op("vector", lambda e: e.tensor_mul(
        ctx.t1f[:], ctx.opf[:], PS[:, 512 * 6:512 * 6 + 512]))
    p.op("vector", lambda e, gsl=gsl: e.tensor_sub(
        gsl, ctx.t1f[:], PS[:, 512 * 7:512 * 7 + 512]))


def emit_oproj(p, ctx):
    PS = ctx.ps
    p.wait_eng("sync", "tensor")
    p.wait_eng("sync", "vector")
    p.wait_eng("sync", "scalar")
    p.wait_eng("vector", "tensor")
    for i in range(3):
        if f"dq_out{i}" in p.dcnt:
            p.wait_dma("vector", f"dq_out{i}")
    for hi in range(8):
        base = 2048 * hi
        for half in range(2):
            a = ctx.gn_all[:, base + 1024 * half:base + 1024 * half + 1024]
            bb_ = ctx.gn_all[:, base + 1024 * (1 - half):base + 1024 * (1 - half) + 1024]
            dst = ctx.gn_eff[:, base + 1024 * half:base + 1024 * half + 1024]
            p.op("vector", lambda e, bb_=bb_: e.tensor_single_scalar(
                ctx.P01[0][:], bb_, ctx.osel[:, 1:2], ALU.mult))
            p.op("vector", lambda e, a=a, dst=dst: e.scalar_tensor_tensor(
                dst, a, ctx.osel[:, 0:1], ctx.P01[0][:], ALU.mult, ALU.add))
    for dt in range(16):
        b = dt % 2
        p.wait_eng("sync", "vector", at=ctx.wo_cast_done.get(b, 0))
        src = ctx.wo.rearrange("(ac pp) d -> pp ac d", pp=128)[:, :, 128 * dt:128 * dt + 128]
        p.dma("sync", ctx.wof[b][:].rearrange("pp (ac c) -> pp ac c", c=128), src,
              f"wo{b}")
        p.wait_dma("vector", f"wo{b}")
        p.wait_eng("vector", "tensor", at=ctx.op_mm_done.get(b, 0))
        p.op("vector", lambda e, b=b: e.tensor_copy(ctx.wob[b][:], ctx.wof[b][:]))
        ctx.wo_cast_done[b] = p.ecnt["vector"]
        p.wait_eng("tensor", "vector")
        p.wait_eng("tensor", "scalar", at=ctx.op_evac_done.get(0, 0))
        for kc in range(KC):
            stat = ctx.wob[b][:, 128 * kc:128 * kc + 128]
            for hi in range(8):
                mov = ctx.gn_eff[:, 2048 * hi:2048 * hi + 2048].rearrange(
                    "p (r j) -> p j r", j=16)[:, kc, :]
                p.op("tensor", lambda e, stat=stat, mov=mov, hi=hi, kc=kc: e.matmul(
                    PS[:, 512 * hi:512 * hi + 128], stat, mov,
                    start=(kc == 0), stop=(kc == KC - 1)),
                    inc=(kc == KC - 1 and hi == 7))
        p.wait_eng("scalar", "tensor")
        for hi in range(8):
            p.op("scalar", lambda e, dt=dt, hi=hi: e.activation(
                ctx.s_bf[:, 1024 * dt + 128 * hi:1024 * dt + 128 * hi + 128],
                PS[:, 512 * hi:512 * hi + 128], AF.Copy))
        ctx.op_evac_done[0] = p.ecnt["scalar"]
        ctx.op_mm_done[b] = p.ecnt["tensor"]


def emit_ffn(p, ctx):
    PS = ctx.ps
    p.wait_eng("vector", "tensor")
    p.wait_eng("sync", "tensor")
    p.wait_eng("sync", "scalar")
    p.wait_eng("sync", "vector")
    for i in range(3):
        if f"dq_out{i}" in p.dcnt:
            p.wait_dma("sync", f"dq_out{i}")
    if getattr(ctx, "ag1_done", None) is not None:
        p.wait_eng("sync", "gpsimd", at=ctx.ag1_done)
    p.op("vector", lambda e: e.memset(ctx.dacc_a[:], 0.0))
    p.op("vector", lambda e: e.memset(ctx.dacc_b[:], 0.0))
    gu_mm_done = {}
    dn_mm_done = {}
    ld_done = {}
    for b5 in range(8):
        wsrc = ctx.ccw[b5]
        for l in range(16):
            wb = l % 2
            for mi, wt in [(0, ctx.wgt[wb]), (1, ctx.wut[wb])]:
                p.wait_eng("sync", "tensor", at=gu_mm_done.get((wb, mi), 0))
                src = wsrc[mi].rearrange("(kc pp) n -> pp kc n", pp=128)[:, :, 128 * l:128 * l + 128]
                p.dma("sync", wt[:].rearrange("pp (kc c) -> pp kc c", c=128), src,
                      f"wg{wb}{mi}")
            p.wait_dma("tensor", f"wg{wb}0")
            p.wait_dma("tensor", f"wg{wb}1")
            p.wait_eng("tensor", "scalar", at=ld_done.get("gelu", 0))
            for kc in range(KC):
                stat = ctx.wgt[wb][:, 128 * kc:128 * kc + 128]
                for tqc in range(2):
                    p.op("tensor", lambda e, stat=stat, kc=kc, tqc=tqc: e.matmul(
                        PS[:, 512 * tqc:512 * tqc + 512], stat,
                        ctx.xn_own[:, 1024 * kc + 512 * tqc:1024 * kc + 512 * tqc + 512],
                        start=(kc == 0), stop=(kc == KC - 1)),
                        inc=(kc == KC - 1 and tqc == 1))
            p.wait_eng("tensor", "scalar", at=ld_done.get("up", 0))
            for kc in range(KC):
                stat = ctx.wut[wb][:, 128 * kc:128 * kc + 128]
                for tqc in range(2):
                    p.op("tensor", lambda e, stat=stat, kc=kc, tqc=tqc: e.matmul(
                        PS[:, 1024 + 512 * tqc:1024 + 512 * tqc + 512], stat,
                        ctx.xn_own[:, 1024 * kc + 512 * tqc:1024 * kc + 512 * tqc + 512],
                        start=(kc == 0), stop=(kc == KC - 1)),
                        inc=(kc == KC - 1 and tqc == 1))
            gu_mm_done[(wb, 0)] = gu_mm_done[(wb, 1)] = p.ecnt["tensor"]
            p.wait_eng("scalar", "tensor")
            p.wait_eng("scalar", "vector", at=ld_done.get("inter", 0))
            p.op("scalar", lambda e: e.activation(ctx.ffn_gg[:], PS[:, 0:1024],
                                                  AF.Gelu_apprx_tanh))
            ld_done["gelu"] = p.ecnt["scalar"]
            p.op("scalar", lambda e: e.activation(ctx.ffn_ub[:], PS[:, 1024:2048],
                                                  AF.Copy))
            ld_done["up"] = p.ecnt["scalar"]
            p.wait_eng("vector", "scalar")
            p.op("vector", lambda e, l=l: e.tensor_mul(
                ctx.inter[:, 1024 * l:1024 * l + 1024], ctx.ffn_gg[:], ctx.ffn_ub[:]))
            ld_done["inter"] = p.ecnt["vector"]
        inter_done = p.ecnt["vector"]
        for ht in range(16):
            wb = ht % 2
            if b5 == 0 and ht == 0 and getattr(ctx, "ag2_done", None) is not None:
                p.wait_eng("sync", "gpsimd", at=ctx.ag2_done)
            p.wait_eng("sync", "tensor", at=dn_mm_done.get(wb, 0))
            src = wsrc[2].rearrange("(nc pp) hh -> pp nc hh", pp=128)[:, :, 128 * ht:128 * ht + 128]
            p.dma("sync", ctx.wdt[wb][:].rearrange("pp (nc c) -> pp nc c", c=128),
                  src, f"wd{wb}")
            p.wait_dma("tensor", f"wd{wb}")
            if ht == 0:
                p.wait_eng("tensor", "vector", at=inter_done)
            p.wait_eng("tensor", "vector", at=ld_done.get("dacc", 0))
            for nc_ in range(KC):
                stat = ctx.wdt[wb][:, 128 * nc_:128 * nc_ + 128]
                for tqc in range(2):
                    p.op("tensor", lambda e, stat=stat, nc_=nc_, tqc=tqc: e.matmul(
                        PS[:, 2048 + 512 * tqc:2048 + 512 * tqc + 512], stat,
                        ctx.inter[:, 1024 * nc_ + 512 * tqc:1024 * nc_ + 512 * tqc + 512],
                        start=(nc_ == 0), stop=(nc_ == KC - 1)),
                        inc=(nc_ == KC - 1 and tqc == 1))
            dn_mm_done[wb] = p.ecnt["tensor"]
            p.wait_eng("vector", "tensor")
            dsl = ctx.dacc_a[:, 1024 * ht:1024 * ht + 1024] if ht < 8 else \
                ctx.dacc_b[:, 1024 * (ht - 8):1024 * (ht - 8) + 1024]
            p.op("vector", lambda e, dsl=dsl: e.tensor_add(dsl, dsl,
                                                           PS[:, 2048:3072]))
            ld_done["dacc"] = p.ecnt["vector"]
    oadds = []
    for ht in range(16):
        b = ht % 2
        dsl = ctx.dacc_a[:, 1024 * ht:1024 * ht + 1024] if ht < 8 else \
            ctx.dacc_b[:, 1024 * (ht - 8):1024 * (ht - 8) + 1024]
        p.wait_eng("scalar", "vector")
        p.op("scalar", lambda e, dsl=dsl: e.activation(ctx.gbuf[:], dsl, AF.Silu))
        if ht >= 2:
            p.wait_eng("sync", "vector", at=oadds[ht - 2])
        else:
            p.wait_eng("sync", "tensor")
        p.dma("sync", ctx.xfbuf[b][:], ctx.xfm[128 * ht:128 * ht + 128, 0:1024],
              f"xf{b}")
        p.wait_eng("vector", "scalar")
        p.op("vector", lambda e, ht=ht: e.tensor_mul(
            ctx.sgtmp[:], ctx.s_bf[:, 1024 * ht:1024 * ht + 1024], ctx.gbuf[:]))
        p.wait_dma("vector", f"xf{b}")
        if ht >= 2:
            p.wait_dma("vector", f"xo{b}")
        p.op("vector", lambda e, b=b: e.tensor_add(
            ctx.obuf[b][:], ctx.xfbuf[b][:], ctx.sgtmp[:]))
        oadds.append(p.ecnt["vector"])
        p.wait_eng("scalar", "vector")
        p.dma("scalar", ctx.out_fm[128 * ht:128 * ht + 128, :], ctx.obuf[b][:],
              f"xo{b}")
    p.wait_dma("gpsimd", "xo0")
    p.wait_dma("gpsimd", "xo1")


def emit_attention(p, ctx):
    emit_lambda(p, ctx)
    emit_ln(p, ctx)
    emit_vproj(p, ctx)
    for hp in range(8):
        emit_kqproj(p, ctx, hp)
        emit_cbound(p, ctx, hp)
        for blk in range(4):
            emit_attn_block(p, ctx, hp, blk)
    emit_oproj(p, ctx)


def build_program(mode):
    nc = bass.Bass("TRN2", target_bir_lowering=False,
                   dynamic_dma_scratch_size=4096)
    ctx = make_ctx(nc, mode)
    if mode == "dequant":
        ctx.cc_in = nc.dram_tensor("cc", [3 * T, 2048], BF16, kind="ExternalOutput").ap()
    if mode in ("ffn", "fused", "layer2"):
        ctx.out_fm = nc.dram_tensor("out_fm", [2048, 1024], F32,
                                    kind="ExternalOutput").ap()
    if mode == "fused":
        ctx.dbg_s = nc.dram_tensor("dbg_s", [128, 16 * 1024], BF16,
                                   kind="ExternalOutput").ap()
        ctx.dbg_xn = nc.dram_tensor("dbg_xn", [128, 16 * 1024], BF16,
                                    kind="ExternalOutput").ap()
        ctx.dbg_gn = nc.dram_tensor("dbg_gn", [128, 16 * 1024], BF16,
                                    kind="ExternalOutput").ap()
        ctx.dbg_cc = nc.dram_tensor("dbg_cc", [24, 128, 2048], BF16,
                                    kind="ExternalOutput").ap()
        ctx.dbg_da = nc.dram_tensor("dbg_da", [128, 8 * 1024], F32,
                                    kind="ExternalOutput").ap()
    if mode == "attn":
        ctx.s_out = nc.dram_tensor("s_out", [2048, 1024], BF16, kind="ExternalOutput").ap()
        ctx.xn_out = nc.dram_tensor("xn_out", [2048, 2048], BF16, kind="ExternalOutput").ap()
        ctx.dbg_gn = nc.dram_tensor("dbg_gn", [2048, 1024], BF16, kind="ExternalOutput").ap()
        ctx.dbg_kt = nc.dram_tensor("dbg_kt", [128, 2048], BF16, kind="ExternalOutput").ap()
        ctx.dbg_v = nc.dram_tensor("dbg_v", [128, 2048], BF16, kind="ExternalOutput").ap()
        ctx.dbg_rows = nc.dram_tensor("dbg_rows", [128, 16], F32, kind="ExternalOutput").ap()
    sems = {n: nc.alloc_semaphore(n) for n in sem_names_for(mode)}

    with nc.Block() as block:
        def emit_all(p):
            reset_sched_state(ctx)
            emit_preamble(p, ctx)
            if mode == "dequant":
                emit_dequant(p, ctx)
                emit_dequant_wait_done(p, ctx)
            elif mode == "ffn":
                emit_ln(p, ctx)
                p.wait_eng("sync", "tensor")
                p.wait_eng("sync", "vector")
                for dt in range(16):
                    p.dma("sync", ctx.s_bf[:, 1024 * dt:1024 * dt + 1024],
                          ctx.s_in[128 * dt:128 * dt + 128, :], "sld")
                p.wait_dma("vector", "sld")
                emit_ffn(p, ctx)
            elif mode == "layer2":
                emit_attention(p, ctx)
                emit_ffn(p, ctx)
            elif mode == "fused":
                def maybe_ag(which):
                    k = ctx.dq_state["k"]
                    if which == 1 and ctx.ag1_done is None and k >= 64:
                        for i in range(3):
                            p.wait_dma("gpsimd", f"dq_out{i}")
                        p.op("gpsimd", lambda e: e.collective_compute(
                            "AllGather", mybir.AluOpType.bypass,
                            replica_groups=[[0, 1, 2, 3, 4, 5, 6, 7]],
                            ins=[ctx.cc_in[0:2 * T, :]], outs=[ctx.cc_gu[:]]))
                        ctx.ag1_done = p.ecnt["gpsimd"]
                    if which == 2 and ctx.ag2_done is None and k >= len(DQ_TILES):
                        for i in range(3):
                            p.wait_dma("gpsimd", f"dq_out{i}")
                        p.op("gpsimd", lambda e: e.collective_compute(
                            "AllGather", mybir.AluOpType.bypass,
                            replica_groups=[[0, 1, 2, 3, 4, 5, 6, 7]],
                            ins=[ctx.cc_in[2 * T:3 * T, :]], outs=[ctx.cc_d[:]]))
                        ctx.ag2_done = p.ecnt["gpsimd"]
                ctx.ag1_done = None
                ctx.ag2_done = None
                emit_lambda(p, ctx)
                emit_ln(p, ctx)
                emit_vproj(p, ctx, hook=lambda q: emit_dq_tiles(p, ctx, 10))
                for hp in range(8):
                    emit_kqproj(p, ctx, hp)
                    emit_cbound(p, ctx, hp)
                    for blk in range(4):
                        emit_attn_block(p, ctx, hp, blk)
                        emit_dq_tiles(p, ctx, 6)
                        maybe_ag(1)
                        maybe_ag(2)
                emit_dq_tiles(p, ctx, len(DQ_TILES))
                maybe_ag(1)
                maybe_ag(2)
                emit_oproj(p, ctx)
                emit_ffn(p, ctx)
            elif mode == "attn":
                emit_attention(p, ctx)
                p.wait_eng("sync", "scalar")
                for dt in range(16):
                    p.dma("sync", ctx.dbg_gn[128 * dt:128 * dt + 128, :],
                          ctx.gn_all[:, 1024 * dt:1024 * dt + 1024], "sout")
                for dt in range(16):
                    p.dma("sync", ctx.s_out[128 * dt:128 * dt + 128, :],
                          ctx.s_bf[:, 1024 * dt:1024 * dt + 1024], "sout")
                    p.dma("sync", ctx.xn_out[128 * dt:128 * dt + 128, 0:1024],
                          ctx.xn_own[:, 1024 * dt:1024 * dt + 1024], "sout")
                    p.dma("sync", ctx.xn_out[128 * dt:128 * dt + 128, 1024:2048],
                          ctx.xn_rest[:, 1024 * dt:1024 * dt + 1024], "sout")
                p.dma("sync", ctx.dbg_rows[0:1, 0:11], ctx.crow[0:1, 0:11], "sout")
                p.dma("sync", ctx.dbg_rows[1:2, 0:1], ctx.lam[:], "sout")
                p.dma("sync", ctx.dbg_rows[2:3, 0:1], ctx.mccol[0:1, :], "sout")
                p.wait_dma("sync", "sout")

        for name, reg in [("sync", block.sync), ("vector", block.vector),
                          ("scalar", block.scalar), ("tensor", block.tensor),
                          ("gpsimd", block.gpsimd)]:
            def body(eng, name=name):
                emit_all(Prog(nc, sems, name, eng))
            reg(body)
    return nc


def build_dequant_test():
    return build_program("dequant")


def pi2_rows(core):
    return np.arange(2048) * 2 + 4096 * (core % 4) + (1 if core >= 4 else 0)


def host_dequant_slices(core, gW, gS, uW, uS, dW, dS):
    rows = pi2_rows(core)
    gbT = np.ascontiguousarray(gW[rows, :].T)
    ubT = np.ascontiguousarray(uW[rows, :].T)
    gsT = np.ascontiguousarray(np.repeat(gS[rows, :].T, 16, axis=0))
    usT = np.ascontiguousarray(np.repeat(uS[rows, :].T, 16, axis=0))
    nb0 = 2048 * (core % 4)
    dbT = np.ascontiguousarray(dW[:, nb0:nb0 + 2048].T)
    dbT = np.ascontiguousarray(dbT[:, PI])
    j0 = 128 * (core % 4)
    dsT = np.ascontiguousarray(np.repeat(dS[:, j0:j0 + 128].T, 16, axis=0))
    dsT = np.ascontiguousarray(dsT[:, PI])
    w = 1.0 if core < 4 else 0.0
    dqsel = np.tile(np.array([[w, 1.0 - w]], np.float32), (128, 1))
    return gbT, ubT, gsT, usT, dbT, dsT, dqsel


def tau_perm(core):
    if core % 2 == 0:
        return np.arange(2048)
    return np.concatenate([np.arange(1024, 2048), np.arange(0, 1024)])


def head_order(core):
    own = np.arange(8) + 8 * (core % 2)
    rest = np.arange(8) + 8 * (1 - core % 2)
    return np.concatenate([own, rest])


def host_attn_inputs(core, x, Wq, Wk, Wv, Wo, lq1, lk1, lq2, lk2):
    b = core // 2
    tau = tau_perm(core)
    ho = head_order(core)
    qk_cols = np.concatenate([np.arange(128 * g, 128 * g + 128) for g in ho])
    v_cols = np.concatenate([np.arange(128 * g, 128 * g + 128) for g in ho])
    xfm = np.ascontiguousarray(x[b].T[PI][:, tau], np.float32)
    wq = np.ascontiguousarray(Wq[PI, :][:, qk_cols], np.float32)
    wk = np.ascontiguousarray(Wk[PI, :][:, qk_cols], np.float32)
    wv = np.ascontiguousarray(Wv[PI, :][:, v_cols], np.float32)
    wo = np.ascontiguousarray(Wo[:, PI], np.float32)
    lams = np.concatenate([lq1, lk1, lq2, lk2]).reshape(1, 256).astype(np.float32)
    masks = np.zeros((64, 128, 512), ml_dtypes.bfloat16)
    for i in range(64):
        blk, tk = i // 16, i % 16
        kpos = tau[128 * tk:128 * tk + 128][:, None]
        qpos = tau[512 * blk:512 * blk + 512][None, :]
        masks[i] = (kpos <= qpos).astype(ml_dtypes.bfloat16)
    w = 1.0 if core % 2 == 0 else 0.0
    osel = np.tile(np.array([[w, 1.0 - w]], np.float32), (128, 1))
    dead = (masks.reshape(64, -1).astype(np.float32).max(axis=1) == 0)
    deadrow = dead.astype(np.float32).reshape(1, 64)
    return {"xfm": xfm, "wq": wq, "wk": wk, "wv": wv, "wo": wo,
            "lams": lams, "masks": masks, "osel": osel, "deadrow": deadrow}


_PROG_CACHE = {}


def _prog(mode):
    if mode not in _PROG_CACHE:
        _PROG_CACHE[mode] = build_program(mode)
    return _PROG_CACHE[mode]


def kernel(L0_attn_x, L0_attn_Wq, L0_attn_Wk, L0_attn_Wv, L0_attn_Wo,
           L0_attn_lambda_q1, L0_attn_lambda_k1, L0_attn_lambda_q2,
           L0_attn_lambda_k2, L0_ffn_gate_W_packed, L0_ffn_gate_scales,
           L0_ffn_up_W_packed, L0_ffn_up_scales, L0_ffn_down_W_packed,
           L0_ffn_down_scales):
    x = np.asarray(L0_attn_x, np.float32)
    gW = np.asarray(L0_ffn_gate_W_packed, np.int32)
    uW = np.asarray(L0_ffn_up_W_packed, np.int32)
    dW = np.asarray(L0_ffn_down_W_packed, np.int32)
    gS = np.asarray(L0_ffn_gate_scales, np.float32)
    uS = np.asarray(L0_ffn_up_scales, np.float32)
    dS = np.asarray(L0_ffn_down_scales, np.float32)

    in_maps = []
    for c in range(NCORE):
        gbT, ubT, gsT, usT, dbT, dsT, dqsel = host_dequant_slices(
            c, gW, gS, uW, uS, dW, dS)
        d = host_attn_inputs(c, x, np.asarray(L0_attn_Wq, np.float32),
                             np.asarray(L0_attn_Wk, np.float32),
                             np.asarray(L0_attn_Wv, np.float32),
                             np.asarray(L0_attn_Wo, np.float32),
                             np.asarray(L0_attn_lambda_q1, np.float32),
                             np.asarray(L0_attn_lambda_k1, np.float32),
                             np.asarray(L0_attn_lambda_q2, np.float32),
                             np.asarray(L0_attn_lambda_k2, np.float32))
        d.update({"gbT": gbT, "ubT": ubT, "gsT": gsT, "usT": usT,
                  "dbT": dbT, "dsT": dsT, "dqsel": dqsel})
        in_maps.append(d)
    res = run_bass_kernel_spmd(_prog("fused"), in_maps, core_ids=list(range(NCORE)))

    out = np.zeros((BATCH, T, D), np.float32)
    for c in range(NCORE):
        o = np.asarray(res.results[c]["out_fm"], np.float32)
        tau = tau_perm(c)
        out[c // 2, tau[:1024], :] = o[PI_INV, :].T
    return out
